# version 13
# baseline (speedup 1.0000x reference)
import os
import numpy as np
from contextlib import ExitStack
L0A_PH = int(os.environ.get("L0A_PH", "9"))
import concourse.bass as bass
import concourse.mybir as mybir
from concourse.bass_utils import run_bass_kernel_spmd

F32 = mybir.dt.float32
BF16 = mybir.dt.bfloat16
I32 = mybir.dt.int32
AF = mybir.ActivationFunctionType
ALU = mybir.AluOpType

P = 128
T = 2048
D = 2048
DF = D // P

ALL_BLOCKS = ("l0a", "l0f", "l1a", "l1f")


class SC:
    __slots__ = ("h", "n")

    def __init__(self, ctx, nc, name):
        self.h = ctx.enter_context(nc.semaphore(name))
        self.n = 0

    def reserve(self, k):
        base = self.n
        self.n += k
        return base


def build(blocks=ALL_BLOCKS):
    nc = bass.Bass("TRN2", target_bir_lowering=False)

    x_h = nc.dram_tensor("x", [T, D], F32, kind="ExternalInput").ap()
    y_h = nc.dram_tensor("y", [T, D], F32, kind="ExternalOutput").ap()

    w = {}
    if "l0a" in blocks:
        w["ip"] = nc.dram_tensor("L0_attn_in_proj_w", [8512, D], F32, kind="ExternalInput").ap()
        w["cw"] = nc.dram_tensor("L0_attn_conv_w", [4352, 4], F32, kind="ExternalInput").ap()
        w["cb"] = nc.dram_tensor("L0_attn_conv_b", [4352], F32, kind="ExternalInput").ap()
        w["dtb"] = nc.dram_tensor("L0_attn_dt_bias", [64], F32, kind="ExternalInput").ap()
        w["alog"] = nc.dram_tensor("L0_attn_A_log", [64], F32, kind="ExternalInput").ap()
        w["dd"] = nc.dram_tensor("L0_attn_D", [64], F32, kind="ExternalInput").ap()
        w["nw"] = nc.dram_tensor("L0_attn_norm_weight", [4096], F32, kind="ExternalInput").ap()
        w["op"] = nc.dram_tensor("L0_attn_out_proj_w", [D, 4096], F32, kind="ExternalInput").ap()
        pc_ip = nc.dram_tensor("pc_ip", [8512, D], BF16, kind="Internal").ap()
        blk_ip = nc.dram_tensor("blk_ip", [66, P, D], BF16, kind="Internal").ap()
        pc_op = nc.dram_tensor("pc_op", [D, 4096], BF16, kind="Internal").ap()
        blk_op2 = nc.dram_tensor("blk_op2", [16, P, 4096], BF16, kind="Internal").ap()
        xtm_d = nc.dram_tensor("xtm_d", [T, 4096], BF16, kind="Internal").ap()
        dt_d = nc.dram_tensor("dt_d", [64, T], F32, kind="Internal").ap()
        cs_d = nc.dram_tensor("cs_d", [64, 256], F32, kind="Internal").ap()
        ztm_d = nc.dram_tensor("ztm_d", [T, 4096], BF16, kind="Internal").ap()
    if "l1a" in blocks:
        w["wq"] = nc.dram_tensor("L1_attn_wq", [D, D], F32, kind="ExternalInput").ap()
        w["wk"] = nc.dram_tensor("L1_attn_wk", [D, 512], F32, kind="ExternalInput").ap()
        w["wv"] = nc.dram_tensor("L1_attn_wv", [D, 512], F32, kind="ExternalInput").ap()
        w["wo"] = nc.dram_tensor("L1_attn_wo", [D, D], F32, kind="ExternalInput").ap()
        w["sink"] = nc.dram_tensor("L1_attn_sink", [1, 16, 1, 1], F32, kind="ExternalInput").ap()
        blk_q = nc.dram_tensor("blk_q", [16, P, D], BF16, kind="Internal").ap()
        blk_k = nc.dram_tensor("blk_k", [4, P, D], BF16, kind="Internal").ap()
        blk_v = nc.dram_tensor("blk_v", [4, P, D], BF16, kind="Internal").ap()
        blk_o = nc.dram_tensor("blk_o", [16, P, D], BF16, kind="Internal").ap()
    if "l0f" in blocks:
        w["fw"] = nc.dram_tensor("L0_ffn_weight", [D, D], F32, kind="ExternalInput").ap()
        w["fw_s"] = nc.dram_tensor("L0_ffn_scale", [D], F32, kind="ExternalInput").ap()
    if "l1f" in blocks:
        w["up"] = nc.dram_tensor("L1_ffn_up_weight", [8192, D], F32, kind="ExternalInput").ap()
        w["dn"] = nc.dram_tensor("L1_ffn_down_weight", [D, 8192], F32, kind="ExternalInput").ap()
        w["us"] = nc.dram_tensor("L1_ffn_up_scale", [8192], F32, kind="ExternalInput").ap()
        w["ds"] = nc.dram_tensor("L1_ffn_down_scale", [D], F32, kind="ExternalInput").ap()
        w["os"] = nc.dram_tensor("L1_ffn_output_scale", [1], F32, kind="ExternalInput").ap()
        pc_up = nc.dram_tensor("pc_up", [8192, D], BF16, kind="Internal").ap()
        pc_dn = nc.dram_tensor("pc_dn", [D, 8192], BF16, kind="Internal").ap()
        blk_up = nc.dram_tensor("blk_up", [64, P, D], BF16, kind="Internal").ap()
        blk_dn = nc.dram_tensor("blk_dn", [16, P, 8192], BF16, kind="Internal").ap()

    ctx = ExitStack()
    with ctx:
        sb = lambda name, shape, dt: ctx.enter_context(nc.sbuf_tensor(name, shape, dt))

        res = [sb(f"res{f}", [P, T], BF16) for f in range(DF)]
        ident = sb("ident", [P, P], BF16)
        identf = sb("identf", [64, 64], F32)
        ones_col = sb("ones_col", [P, 1], BF16)
        ones_f128 = sb("ones_f128", [P, P], F32)
        ones_rowf = ones_f128[0:1, :]
        ones_rowb = sb("ones_rowb", [1, P], BF16)


        sqs = [sb(f"sqs{i}", [P, 512], BF16) for i in range(2)]
        m2rows = sb("m2rows", [64, 1024], F32)
        lnrow = m2rows[0:1, :]
        lnact = m2rows[32:33, :]
        bcfm = sb("bcfm", [P, 4096], BF16)
        lnab = bcfm[0:1, 0:1024]
        lnA = sb("lnA", [P, 512], BF16)
        lnB = sb("lnB", [P, 512], BF16)
        tpool = [sb(f"tpool{i}", [P, 512], BF16) for i in range(2)]
        ceps = sb("ceps", [1, 1], F32)
        cepsP = sb("cepsP", [P, 1], F32)

        xqt = sb("xqt", [P, DF * 512], BF16)
        xq = [xqt[:, f * 512:(f + 1) * 512] for f in range(DF)]
        Hbuf = sb("Hbuf", [P, 64 * 512], BF16)
        wslab = [sb(f"wslab{i}", [P, D], BF16) for i in range(2)]
        dnslab = [sb(f"dnslab{i}", [P, 8192], BF16) for i in range(2)]
        usf = sb("usf", [P, 64], F32)
        dsosf = sb("dsosf", [P, 16], F32)
        usr = sb("usr", [64, P], F32)
        dsr = sb("dsr", [16, P], F32)
        osr = sb("osr", [1, 1], F32)
        osb = sb("osb", [P, 1], F32)
        fstat = sb("fstat", [P, 24], F32)
        xtm = wslab
        ytm = [dnslab[0][:, 0:D]] * 2

        class _SameBlk:
            def __init__(self, b):
                self.b = b
            def __enter__(self):
                return self.b
            def __exit__(self, *a):
                return False
        CURBLK = [None]

        def BLKC():
            return nc.Block() if CURBLK[0] is None else _SameBlk(CURBLK[0])
        if "l0a" in blocks:
            Hbf = Hbuf[:, 24576:28672]
            m2t = wslab[0][:, 0:384].bitcast(F32)
            aux = wslab[0][:, 384:640].bitcast(F32)
            gt_sb = wslab[0][:, 640:1152]
            bds_sb = wslab[0][:, 1152:1280]
            yc_sb = wslab[0][:, 1280:1536]
            dtc = wslab[0][0:64, 1536:2048].bitcast(F32)
            cwf = sb("cwf", [P, 136], F32)
            cbf = sb("cbf", [P, 34], F32)
            cwr = dnslab[1][0:34, 0:1024].bitcast(F32)
            cwr2 = dnslab[1][0:34, 2048:3072].bitcast(F32)
            cbr = dnslab[1][0:34, 1024:1280].bitcast(F32)
            nwr = dnslab[1][0:32, 1280:1536].bitcast(F32)
            nwf = sb("nwf", [P, 32], F32)
            dtval = xqt[0:64, 4096:8192].bitcast(F32)
            dr64 = dnslab[1][0:1, 1536:1664].bitcast(F32)
            blk_dt_sb = sb("blk_dt_sb", [P, 1024], BF16)
            cepsF = sb("cepsF", [P, 1], F32)
            c1f = sb("c1f", [P, 1], F32)
        if "l1a" in blocks:
            cosf = xqt[0:64, 0:4096].bitcast(F32)
            sinf = xqt[0:64, 4096:8192].bitcast(F32)
            cosf2 = xqt[64:128, 0:4096].bitcast(F32)
            sinf2 = xqt[64:128, 4096:8192].bitcast(F32)
            mask3 = bcfm[:, 1024:1792]
            sk_row = sb("sk_row", [1, 16], F32)
            sinkb = sb("sinkb", [P, 16], F32)

        pf = ctx.enter_context(nc.psum_tensor("pf", [P, 6 * 512], F32))
        pbank = lambda i: pf[:, i * 512:(i + 1) * 512]
        ptb = [ctx.enter_context(nc.psum_tensor(f"ptb{i}", [P, 1024], BF16))
               for i in range(2)]

        sd = SC(ctx, nc, "sd")
        sg = SC(ctx, nc, "sg")
        sga = [SC(ctx, nc, f"sga{i}") for i in range(2)]
        sta = [SC(ctx, nc, f"sta{i}") for i in range(2)]
        sha = [SC(ctx, nc, f"sha{i}") for i in range(2)]
        sgb = [SC(ctx, nc, f"sgb{i}") for i in range(2)]
        sp = SC(ctx, nc, "sp")
        sv = SC(ctx, nc, "sv")
        ss = SC(ctx, nc, "ss")
        sq = SC(ctx, nc, "sq")
        st = SC(ctx, nc, "st")
        scs = [SC(ctx, nc, f"scs{i}") for i in range(2)]

        q_c = sq.reserve(4)
        with nc.Block() as blk:
            @blk.gpsimd
            def _(g):
                g.memset(ident[:, :], 1.0).then_inc(sq.h, 1)
                g.memset(identf[:, :], 1.0).then_inc(sq.h, 1)
                g.wait_ge(sq.h, q_c + 2)
                g.affine_select(ident[:, :], ident[:, :], pattern=[[1, P]],
                                compare_op=ALU.is_equal, fill=0.0, base=0,
                                channel_multiplier=-1).then_inc(sq.h, 1)
                g.affine_select(identf[:, :], identf[:, :], pattern=[[1, 64]],
                                compare_op=ALU.is_equal, fill=0.0, base=0,
                                channel_multiplier=-1).then_inc(sq.h, 1)
                g.memset(ones_col[:, :], 1.0)
                g.memset(ones_f128[:, :], 1.0)
                g.memset(ones_rowb[:, :], 1.0)
                g.memset(cepsP[:, :], 1e-38)
                (g.memset(c1f[:, :], 1.0),
                 g.memset(cepsF[:, :], 1e-5)) if "l0a" in blocks else None
                g.memset(ceps[:, :], 1e-5).then_inc(sq.h, 1)
        CONST_Q = sq.reserve(1) + 1

        g0 = [sga[0].reserve(8 * 16), sga[1].reserve(8 * 16)]
        p0 = sp.reserve(16 * 4)
        v0 = sv.reserve(16 * 4)
        with nc.Block() as blk:
            @blk.gpsimd
            def _(g):
                for tc in range(16):
                    if tc >= 2:
                        g.wait_ge(sp.h, p0 + (tc - 1) * 4)
                    g.dma_start(out=xtm[tc % 2][:, :],
                                in_=x_h[tc * P:(tc + 1) * P, :]).then_inc(sga[tc % 2].h, 16)

            @blk.tensor
            def _(t):
                t.wait_ge(sq.h, CONST_Q)
                for tc in range(16):
                    t.wait_ge(sga[tc % 2].h, g0[tc % 2] + (tc // 2 + 1) * 16)
                    for b in range(4):
                        if tc * 4 + b >= 2:
                            t.wait_ge(sv.h, v0 + tc * 4 + b - 1)
                        half = (tc * 4 + b) % 2
                        last = None
                        for j in range(4):
                            f = b * 4 + j
                            last = t.transpose(
                                ptb[half][:, j * 128:(j + 1) * 128],
                                xtm[tc % 2][:, f * P:(f + 1) * P], ident[:, :])
                        last.then_inc(sp.h, 1)

            @blk.vector
            def _(v):
                for tc in range(16):
                    for b in range(4):
                        v.wait_ge(sp.h, p0 + tc * 4 + b + 1)
                        half = (tc * 4 + b) % 2
                        for j in range(4):
                            f = b * 4 + j
                            ins = v.tensor_copy(
                                out=res[f][:, tc * P:(tc + 1) * P],
                                in_=ptb[half][:, j * 128:(j + 1) * 128])
                        ins.then_inc(sv.h, 1)


        def emit_ln(tok0, W, xdst):
            invD = 1.0 / D
            p_guard = sp.n
            v_guard = sv.n
            s_guard = ss.n
            q_guard = sq.n
            g_guard = sg.n
            p_b = sp.reserve(18)
            s_b = ss.reserve(18)
            v_b = sv.reserve(25)
            q_b = sq.reserve(16)
            mrow = lnrow[0:1, 0:W]
            nrstd = lnrow[0:1, 512:512 + W]
            msqrow = lnact[0:1, 512:512 + W]
            trow = lnact[0:1, 0:W]
            varrow = lnact[0:1, 0:W]
            lnv = lnact[0:1, 0:W]
            rstd = lnact[0:1, 0:W]
            arow = lnab[0:1, 0:W]
            brow = lnab[0:1, 512:512 + W]
            with BLKC() as blk:
                @blk.scalar
                def _(sc):
                    for f in range(DF):
                        if f >= 2:
                            sc.wait_ge(sp.h, p_b + f)
                        elif f == 0:
                            sc.wait_ge(sp.h, p_guard)
                        sc.activation(out=sqs[f % 2][:, 0:W],
                                      in_=res[f][:, tok0:tok0 + W],
                                      func=AF.Square).then_inc(ss.h, 1)
                    sc.wait_ge(sv.h, v_b + 4)
                    sc.activation(out=lnv, in_=varrow, func=AF.Ln,
                                  bias=ceps[0:1, 0:1]).then_inc(ss.h, 1)
                    sc.wait_ge(ss.h, s_b + 17)
                    sc.activation(out=rstd, in_=lnv, func=AF.Exp,
                                  scale=-0.5).then_inc(ss.h, 1)

                @blk.tensor
                def _(t):
                    t.wait_ge(sv.h, v_guard)
                    t.wait_ge(sq.h, CONST_Q)
                    for f in range(DF):
                        ins = t.matmul(pbank(4)[0:1, 0:W], ones_col[:, :],
                                       res[f][:, tok0:tok0 + W],
                                       start=(f == 0), stop=(f == DF - 1))
                    ins.then_inc(sp.h, 1)
                    for f in range(DF):
                        t.wait_ge(ss.h, s_b + f + 1)
                        t.matmul(pbank(5)[0:1, 0:W], ones_col[:, :],
                                 sqs[f % 2][:, 0:W],
                                 start=(f == 0), stop=(f == DF - 1)).then_inc(sp.h, 1)
                    t.wait_ge(sv.h, v_b + 7)
                    t.matmul(pbank(4)[:, 0:W], ones_rowb[0:1, :], arow,
                             start=True, stop=True)
                    t.matmul(pbank(5)[:, 0:W], ones_rowb[0:1, :], brow,
                             start=True, stop=True).then_inc(sp.h, 1)

                @blk.vector
                def _(v):
                    v.wait_ge(sp.h, p_b + 1)
                    v.tensor_scalar(out=mrow, in0=pbank(4)[0:1, 0:W], scalar1=invD,
                                    scalar2=None, op0=ALU.mult).then_inc(sv.h, 1)
                    v.wait_ge(sp.h, p_b + 17)
                    v.tensor_scalar(out=msqrow, in0=pbank(5)[0:1, 0:W], scalar1=invD,
                                    scalar2=None, op0=ALU.mult).then_inc(sv.h, 1)
                    v.wait_ge(sv.h, v_b + 2)
                    v.tensor_tensor(out=trow, in0=mrow, in1=mrow,
                                    op=ALU.mult).then_inc(sv.h, 1)
                    v.wait_ge(sv.h, v_b + 3)
                    v.tensor_tensor(out=varrow, in0=msqrow, in1=trow,
                                    op=ALU.subtract).then_inc(sv.h, 1)
                    v.wait_ge(ss.h, s_b + 18)
                    v.tensor_copy(out=arow, in_=rstd).then_inc(sv.h, 1)
                    v.tensor_scalar(out=nrstd, in0=rstd, scalar1=-1.0, scalar2=None,
                                    op0=ALU.mult).then_inc(sv.h, 1)
                    v.wait_ge(sv.h, v_b + 6)
                    v.tensor_tensor(out=brow, in0=mrow, in1=nrstd,
                                    op=ALU.mult).then_inc(sv.h, 1)
                    v.wait_ge(sp.h, p_b + 18)
                    v.tensor_copy(out=lnA[:, 0:W], in_=pbank(4)[:, 0:W]).then_inc(sv.h, 1)
                    v.tensor_copy(out=lnB[:, 0:W], in_=pbank(5)[:, 0:W]).then_inc(sv.h, 1)
                    v.wait_ge(sp.h, p_guard)
                    v.wait_ge(sg.h, g_guard)
                    for f in range(DF):
                        v.wait_ge(sq.h, q_b + f + 1)
                        v.tensor_tensor(out=xdst[f], in0=tpool[f % 2][:, 0:W],
                                        in1=lnB[:, 0:W], op=ALU.add).then_inc(sv.h, 1)

                @blk.gpsimd
                def _(g):
                    g.wait_ge(sv.h, v_b + 9)
                    for f in range(DF):
                        if f >= 2:
                            g.wait_ge(sv.h, v_b + 8 + f)
                        g.tensor_tensor(out=tpool[f % 2][:, 0:W],
                                        in0=res[f][:, tok0:tok0 + W],
                                        in1=lnA[:, 0:W], op=ALU.mult).then_inc(sq.h, 1)
            return sv.n

        def emit_pc(w_ap, pc_ap, nslabs, width, sign=True, total_rows=None):
            total_rows = total_rows if total_rows is not None else nslabs * P
            p_gd = sp.n
            q_gd = sq.n
            ga = [sga[0].reserve(((nslabs + 1) // 2) * 16),
                  sga[1].reserve((nslabs // 2) * 16)]
            ta_ = [sta[0].reserve(((nslabs + 1) // 2) * 16),
                   sta[1].reserve((nslabs // 2) * 16)]
            s_b = ss.reserve(nslabs if sign else 0)
            with BLKC() as blk:
                @blk.gpsimd
                def _(g):
                    g.wait_ge(sp.h, p_gd)
                    g.wait_ge(sq.h, q_gd)
                    for i in range(nslabs):
                        rows = min(P, total_rows - i * P)
                        if i >= 2:
                            g.wait_ge(sta[i % 2].h, ta_[i % 2] + (i // 2) * 16)
                        else:
                            g.wait_ge(sta[i % 2].h, ta_[i % 2])
                        g.dma_start(out=dnslab[i % 2][0:rows, 0:width],
                                    in_=w_ap[i * P:i * P + rows, :]).then_inc(sga[i % 2].h, 16)

                if sign:
                    @blk.scalar
                    def _(sc):
                        for i in range(nslabs):
                            rows = min(P, total_rows - i * P)
                            sc.wait_ge(sga[i % 2].h, ga[i % 2] + (i // 2 + 1) * 16)
                            sc.sign(out=dnslab[i % 2][0:rows, 0:width],
                                    in_=dnslab[i % 2][0:rows, 0:width]).then_inc(ss.h, 1)

                @blk.sync
                def _(s):
                    for i in range(nslabs):
                        rows = min(P, total_rows - i * P)
                        if sign:
                            s.wait_ge(ss.h, s_b + i + 1)
                        else:
                            s.wait_ge(sga[i % 2].h, ga[i % 2] + (i // 2 + 1) * 16)
                        s.dma_start(out=pc_ap[i * P:i * P + rows, :],
                                    in_=dnslab[i % 2][0:rows, 0:width]).then_inc(sta[i % 2].h, 16)
            return [sta[0].n, sta[1].n]


        if "l0a" in blocks:
            CH = 256
            NCH = T // CH
            xafull = [Hbuf[:, f * D:(f + 1) * D] for f in range(DF)]
            zst = [xqt[:, 0:2048], xqt[:, 2048:4096]]
            sil_out = dnslab[0][:, 4096:6144]
            acc_f = dnslab[0][:, 0:4096].bitcast(F32)
            xraw = dnslab[1][:, 0:4104].bitcast(F32)
            sp_tmp = xqt[0:64, 0:4096].bitcast(F32)

            pcip_done = emit_pc(w["ip"], pc_ip, 67, D, sign=False, total_rows=8512)
            p_gd = sp.n
            g_gd = sg.n
            trl = sd.reserve(16 * 16)
            sct2 = sg.reserve(16 * 16)
            v_bi = sv.reserve(16)
            with nc.Block() as blk:
                @blk.sync
                def _(s):
                    s.wait_ge(sta[0].h, sta[0].n)
                    s.wait_ge(sta[1].h, sta[1].n)
                    s.wait_ge(sg.h, g_gd)
                    s.wait_ge(sp.h, p_gd)
                    for k in range(16):
                        if k >= 1:
                            s.wait_ge(sg.h, sct2 + k * 16)
                            s.wait_ge(sv.h, v_bi + k)
                        s.dma_start_transpose(Hbuf[:, 0:8512],
                                              pc_ip[:, k * P:(k + 1) * P]).then_inc(sd.h, 16)

                @blk.gpsimd
                def _(g):
                    for k in range(16):
                        g.wait_ge(sd.h, trl + (k + 1) * 16)
                        g.dma_start(
                            out=blk_ip[:, :, k * P:(k + 1) * P].rearrange("o p m -> p o m"),
                            in_=Hbuf[:, 0:8448].rearrange("p (o m) -> p o m", o=66),
                        ).then_inc(sg.h, 16)

                @blk.vector
                def _(v):
                    for k in range(16):
                        v.wait_ge(sd.h, trl + (k + 1) * 16)
                        v.tensor_copy(out=blk_dt_sb[:, k * 64:(k + 1) * 64],
                                      in_=Hbuf[:, 8448:8512]).then_inc(sv.h, 1)

            pcop_done = emit_pc(w["op"], pc_op, 16, 4096, sign=False)
            if "l1f" in blocks:
                cu_b = scs[0].reserve(16 * 16)
                cd_b = scs[1].reserve(16 * 16)
                with nc.Block() as blk:
                    @blk.gpsimd
                    def _(g):
                        for j in range(16):
                            g.dma_start(out=pc_up[j * 512:(j + 1) * 512, :],
                                        in_=w["up"][j * 512:(j + 1) * 512, :]
                                        ).then_inc(scs[0].h, 16)
                        for j in range(16):
                            g.dma_start(out=pc_dn[j * 128:(j + 1) * 128, :],
                                        in_=w["dn"][j * 128:(j + 1) * 128, :]
                                        ).then_inc(scs[1].h, 16)
                CAST_UP = (scs[0], cu_b + 256)
                CAST_DN = (scs[1], cd_b + 256)
            d_nw = sd.reserve(16)
            p_nw = sp.reserve(1)
            v_nw = sv.reserve(1)
            NW_TA = [sta[0].n, sta[1].n]
            with nc.Block() as blk:
                @blk.sync
                def _(s):
                    s.wait_ge(sta[0].h, NW_TA[0])
                    s.wait_ge(sta[1].h, NW_TA[1])
                    s.dma_start(out=nwr[:, :],
                                in_=w["nw"].rearrange("(r c) -> r c", r=32)).then_inc(sd.h, 16)

                @blk.tensor
                def _(t):
                    t.wait_ge(sd.h, d_nw + 16)
                    t.wait_ge(sq.h, CONST_Q)
                    t.transpose(pbank(5)[:, 0:32], nwr[:, :],
                                identf[0:32, 0:32]).then_inc(sp.h, 1)

                @blk.vector
                def _(v):
                    v.wait_ge(sp.h, p_nw + 1)
                    v.tensor_copy(out=nwf[:, :], in_=pbank(5)[:, 0:32]).then_inc(sv.h, 1)
            g_gd2 = sg.n
            trl2 = sd.reserve(32 * 16)
            sct3 = sg.reserve(32 * 16)
            v_bo = sv.reserve(32)
            with nc.Block() as blk:
                @blk.sync
                def _(s):
                    s.wait_ge(sta[0].h, sta[0].n)
                    s.wait_ge(sta[1].h, sta[1].n)
                    s.wait_ge(sg.h, g_gd2)
                    for di in range(32):
                        if di >= 1:
                            s.wait_ge(sg.h, sct3 + di * 16)
                        s.dma_start_transpose(Hbuf[:, 0:2048],
                                              pc_op[:, di * P:(di + 1) * P]).then_inc(sd.h, 16)

                @blk.vector
                def _(v):
                    v.wait_ge(sv.h, v_nw + 1)
                    for di in range(32):
                        v.wait_ge(sd.h, trl2 + (di + 1) * 16)
                        v.tensor_scalar(out=Hbuf[:, 2048:4096], in0=Hbuf[:, 0:2048],
                                        scalar1=nwf[:, di:di + 1], scalar2=None,
                                        op0=ALU.mult).then_inc(sv.h, 1)

                @blk.gpsimd
                def _(g):
                    for di in range(32):
                        g.wait_ge(sv.h, v_bo + di + 1)
                        g.dma_start(
                            out=blk_op2[:, :, di * P:(di + 1) * P].rearrange("o p m -> p o m"),
                            in_=Hbuf[:, 2048:4096].rearrange("p (o m) -> p o m", o=16),
                        ).then_inc(sg.h, 16)

            d_cv = sd.reserve(4 * 16)
            p_cv = sp.reserve(1)
            v_cv = sv.reserve(8)
            s_cv = ss.reserve(1)
            CV_TA = [sta[0].n, sta[1].n]
            with nc.Block() as blk:
                @blk.sync
                def _(s):
                    s.wait_ge(sta[0].h, CV_TA[0])
                    s.wait_ge(sta[1].h, CV_TA[1])
                    s.dma_start(out=cwr[:, :],
                                in_=w["cw"].rearrange("(f p) j -> f (p j)", f=34)).then_inc(sd.h, 16)
                    s.dma_start(out=cbr[:, :],
                                in_=w["cb"].rearrange("(f p) -> f p", f=34)).then_inc(sd.h, 16)
                    s.dma_start(out=fstat[0:64, 0:1],
                                in_=w["dtb"].rearrange("(a b) -> a b", a=64)).then_inc(sd.h, 16)
                    s.dma_start(out=fstat[0:64, 1:2],
                                in_=w["alog"].rearrange("(a b) -> a b", a=64)).then_inc(sd.h, 16)


                @blk.tensor
                def _(t):
                    t.wait_ge(sv.h, v_cv + 1)
                    t.wait_ge(sd.h, d_cv + 64)
                    t.wait_ge(sq.h, CONST_Q)
                    for j in range(4):
                        t.transpose(pbank(5)[:, j * 34:(j + 1) * 34],
                                    cwr2[0:34, j * 128:(j + 1) * 128],
                                    identf[0:34, 0:34])
                    t.transpose(pbank(5)[:, 136:170], cbr[:, :],
                                identf[0:34, 0:34]).then_inc(sp.h, 1)

                @blk.vector
                def _(v):
                    v.wait_ge(sd.h, d_cv + 64)
                    for j in range(4):
                        ins = v.tensor_copy(out=cwr2[0:34, j * 128:(j + 1) * 128],
                                            in_=cwr[:, j:512:4])
                    ins.then_inc(sv.h, 1)
                    v.wait_ge(sp.h, p_cv + 1)
                    for j in range(4):
                        v.tensor_copy(out=cwf[:, j * 34:(j + 1) * 34],
                                      in_=pbank(5)[:, j * 34:(j + 1) * 34]).then_inc(sv.h, 1)
                    v.tensor_copy(out=cbf[:, :], in_=pbank(5)[:, 136:170]).then_inc(sv.h, 1)
                    v.wait_ge(ss.h, s_cv + 1)
                    v.tensor_scalar(out=fstat[0:64, 2:3], in0=fstat[0:64, 3:4],
                                    scalar1=-1.0, scalar2=None,
                                    op0=ALU.mult).then_inc(sv.h, 1)
                    v.tensor_scalar(out=fstat[0:64, 4:5], in0=fstat[0:64, 3:4],
                                    scalar1=1.0, scalar2=None,
                                    op0=ALU.mult).then_inc(sv.h, 1)

                @blk.scalar
                def _(sc):
                    sc.wait_ge(sd.h, d_cv + 64)
                    sc.activation(out=fstat[0:64, 3:4], in_=fstat[0:64, 1:2],
                                  func=AF.Exp).then_inc(ss.h, 1)

            for sl in range(4):
                ln_done = emit_ln(sl * 512, 512,
                                  [xafull[f][:, sl * 512:(sl + 1) * 512] for f in range(DF)])
            LN1_V = ln_done

            ZNF = 32
            SKIP_M1 = L0A_PH < 2
            ip_guard_v = sv.n
            ip_guard_q = sq.n
            SCT3_END = sg.n
            hi_ = [sha[0].reserve(33 * 16), sha[1].reserve(33 * 16)]
            p_m1 = sp.reserve(64 * 3 + 2)

            def pfn(fs):
                return p_m1 + (3 * fs if fs <= 64 else 192 + (fs - 64))
            s_m1 = ss.reserve(66 * 5)
            v_m1 = sv.reserve(64 * 2)
            q_m1 = sq.reserve(34 * 5)
            g_m1 = [sgb[0].reserve(32 * 16), sgb[1].reserve(32 * 16)]
            with nc.Block() as blk:
                @blk.sync
                def _(s):
                    for fs in range(66):
                        if fs >= 2:
                            s.wait_ge(sp.h, pfn(fs - 1))
                        else:
                            s.wait_ge(sg.h, SCT3_END)
                        s.dma_start(out=wslab[fs % 2][:, :],
                                    in_=blk_ip[fs, :, :]).then_inc(sha[fs % 2].h, 16)

                @blk.tensor
                def _(t):
                    t.wait_ge(sv.h, LN1_V)
                    for fs in range(66):
                        is_bc = fs >= 64
                        t.wait_ge(sha[fs % 2].h, hi_[fs % 2] + (fs // 2 + 1) * 16)
                        if fs >= 1:
                            t.wait_ge(ss.h, s_m1 + fs * 5)
                        for b in range(4):
                            for k in range(DF):
                                ins = t.matmul(pbank(b)[:, 0:512],
                                               wslab[fs % 2][:, k * P:(k + 1) * P],
                                               xafull[k][:, b * 512:(b + 1) * 512],
                                               start=(k == 0), stop=(k == DF - 1))
                        ins.then_inc(sp.h, 1)
                        if not is_bc:
                            for bt in range(2):
                                t.wait_ge(ss.h, s_m1 + fs * 5 + 5)
                                t.wait_ge(sv.h, v_m1 + fs * 2)
                                for j in range(8):
                                    ins = t.transpose(
                                        ptb[bt][:, j * 128:(j + 1) * 128],
                                        sil_out[:, (bt * 8 + j) * 128:(bt * 8 + j + 1) * 128],
                                        ident[:, :])
                                ins.then_inc(sp.h, 1)

                @blk.scalar
                def _(sc):
                    for fs in range(66):
                        is_z = fs < ZNF
                        is_bc = fs >= 64
                        ci = fs - ZNF
                        sc.wait_ge(sp.h, pfn(fs) + 1)
                        if fs >= 1 and not is_bc:
                            sc.wait_ge(sp.h, pfn(fs - 1) + (3 if fs - 1 < 64 else 1))
                        if is_z:
                            for b in range(4):
                                ins = sc.activation(
                                    out=sil_out[:, b * 512:(b + 1) * 512],
                                    in_=pbank(b)[:, 0:512], func=AF.Silu)
                                ins.then_inc(ss.h, 2 if b == 3 else 1)
                        else:
                            if ci >= 1:
                                sc.wait_ge(sq.h, q_m1 + ci * 5)
                            for b in range(4):
                                sc.copy(out=xraw[:, 3 + b * 512:3 + (b + 1) * 512],
                                        in_=pbank(b)[:, 0:512]).then_inc(ss.h, 1)
                            sc.wait_ge(sq.h, q_m1 + ci * 5 + 5)
                            dst = (bcfm[:, (fs - 64) * 2048:(fs - 63) * 2048]
                                   if is_bc else sil_out)
                            sc.activation(out=dst, in_=acc_f[:, 0:2048], func=AF.Silu,
                                          bias=cbf[:, ci:ci + 1]).then_inc(ss.h, 1)

                @blk.vector
                def _(v):
                    v.wait_ge(sv.h, ip_guard_v)
                    for fs in range(66):
                        if fs >= ZNF:
                            ci = fs - ZNF
                            v.wait_ge(ss.h, s_m1 + fs * 5 + 4)
                            v.memset(xraw[:, 0:3], 0.0).then_inc(sq.h, 1)
                            v.wait_ge(sq.h, q_m1 + ci * 5 + 1)
                            v.tensor_scalar(out=acc_f[:, 0:2048], in0=xraw[:, 0:2048],
                                            scalar1=cwf[:, ci:ci + 1],
                                            scalar2=None, op0=ALU.mult).then_inc(sq.h, 1)
                            for j in range(1, 4):
                                v.wait_ge(sq.h, q_m1 + ci * 5 + 1 + j)
                                v.scalar_tensor_tensor(
                                    out=acc_f[:, 0:2048], in0=xraw[:, j:j + 2048],
                                    scalar=cwf[:, j * 34 + ci:j * 34 + ci + 1],
                                    in1=acc_f[:, 0:2048], op0=ALU.mult,
                                    op1=ALU.add).then_inc(sq.h, 1)
                        if fs >= 64:
                            continue
                        for bt in range(2):
                            v.wait_ge(sp.h, pfn(fs) + 2 + bt)
                            if bt == 0 and fs >= 2:
                                pv = fs % 2
                                v.wait_ge(sgb[pv].h, g_m1[pv] + (fs // 2) * 16)
                            v.tensor_copy(
                                out=zst[fs % 2][:, bt * 1024:(bt + 1) * 1024]
                                    .rearrange("p (a m) -> p a m", a=8),
                                in_=ptb[bt].rearrange("p (a m) -> p a m", a=8),
                            ).then_inc(sv.h, 1)

                @blk.gpsimd
                def _(g):
                    g.wait_ge(sq.h, ip_guard_q)
                    for fs in range(66):
                        if fs < 64:
                            g.wait_ge(sv.h, v_m1 + fs * 2 + 2)
                            dram = ztm_d if fs < ZNF else xtm_d
                            col = fs * 128 if fs < ZNF else (fs - ZNF) * 128
                            g.dma_start(
                                out=dram[:, col:col + 128].rearrange("(tc p) m -> p tc m", p=P),
                                in_=zst[fs % 2].rearrange("p (tc m) -> p tc m", m=P),
                            ).then_inc(sgb[fs % 2].h, 16)

            p_dt = sp.reserve(2)
            s_dt = ss.reserve(3)
            v_dt = sv.reserve(7)
            d_dt = sd.reserve(32)
            M1_SGB = [sgb[0].n, sgb[1].n]
            with nc.Block() as blk:
                @blk.tensor
                def _(t):
                    t.wait_ge(ss.h, s_m1 + 66 * 5)
                    for b in range(4):
                        for k in range(DF):
                            ins = t.matmul(pbank(b)[0:64, 0:512],
                                           blk_dt_sb[:, k * 64:(k + 1) * 64],
                                           xafull[k][:, b * 512:(b + 1) * 512],
                                           start=(k == 0), stop=(k == DF - 1))
                    ins.then_inc(sp.h, 1)
                    t.wait_ge(sd.h, d_dt + 16)
                    t.matmul(pbank(5)[:, 0:64], ones_rowf[0:1, :],
                             m2rows[0:1, 960:1024], start=True,
                             stop=True).then_inc(sp.h, 1)

                @blk.vector
                def _(v):
                    v.wait_ge(sgb[0].h, M1_SGB[0])
                    v.wait_ge(sgb[1].h, M1_SGB[1])
                    v.wait_ge(sv.h, v_m1 + 64 * 2)
                    v.wait_ge(sp.h, p_dt + 1)
                    for b in range(4):
                        v.tensor_scalar(out=dtval[:, b * 512:(b + 1) * 512],
                                        in0=pbank(b)[0:64, 0:512],
                                        scalar1=fstat[0:64, 0:1], scalar2=None,
                                        op0=ALU.add).then_inc(sv.h, 1)
                    v.wait_ge(ss.h, s_dt + 1)
                    v.tensor_scalar(out=dtval[:, :], in0=dtval[:, :], scalar1=0.0,
                                    scalar2=None, op0=ALU.max).then_inc(sv.h, 1)
                    v.wait_ge(ss.h, s_dt + 3)
                    v.wait_ge(sv.h, v_dt + 5)
                    v.tensor_tensor(out=dtval[:, :], in0=dtval[:, :], in1=sp_tmp[:, 0:T],
                                    op=ALU.add).then_inc(sv.h, 1)
                    v.wait_ge(sp.h, p_dt + 2)
                    v.tensor_copy(out=aux[:, 0:64],
                                  in_=pbank(5)[:, 0:64]).then_inc(sv.h, 1)

                @blk.scalar
                def _(sc):
                    sc.wait_ge(sv.h, v_dt + 4)
                    sc.activation(out=sp_tmp[:, 0:T], in_=dtval[:, :],
                                  func=AF.Abs).then_inc(ss.h, 1)
                    sc.wait_ge(ss.h, s_dt + 1)
                    sc.activation(out=sp_tmp[:, 0:T], in_=sp_tmp[:, 0:T],
                                  func=AF.Exp, scale=-1.0).then_inc(ss.h, 1)
                    sc.wait_ge(ss.h, s_dt + 2)
                    sc.activation(out=sp_tmp[:, 0:T], in_=sp_tmp[:, 0:T],
                                  func=AF.Ln, bias=c1f[0:64, 0:1]).then_inc(ss.h, 1)

                @blk.sync
                def _(s):
                    s.dma_start(out=m2rows[0:1, 960:1024],
                                in_=w["dd"].rearrange("(a b) -> a b", a=1)).then_inc(sd.h, 16)
                    s.wait_ge(sd.h, d_dt + 16)
                    s.wait_ge(sv.h, v_dt + 6)
                    s.dma_start(out=dt_d[:, :], in_=dtval[:, :]).then_inc(sd.h, 16)

            x2 = [Hbuf[:, 0:4096], Hbuf[:, 4096:8192]]
            z2 = [Hbuf[:, 8192:12288], Hbuf[:, 12288:16384]]
            Xdt = [Hbuf[:, 16384:20480], Hbuf[:, 20480:24576]]
            y2 = Xdt
            ynfm = [xqt[:, di * 256:(di + 1) * 256] for di in range(32)]
            stage0 = dnslab[1].bitcast(F32)
            stage1 = dnslab[0].bitcast(F32)
            opsl = [dnslab[0][:, 0:4096], dnslab[0][:, 4096:8192]]
            dA_r = m2rows[:, 768:1024]
            cs_r = m2rows[:, 0:256]
            ncs_r = m2rows[:, 256:512]
            ds_r = m2rows[:, 512:768]
            lnBf = lnB.bitcast(F32)
            B_fm = bcfm[:, 0:2048]
            C_fm = bcfm[:, 2048:4096]

            HB_P = sp.n
            q_hz = sq.reserve(1)
            with nc.Block() as blk:
                @blk.gpsimd
                def _(g):
                    g.wait_ge(sp.h, HB_P)
                    g.memset(Hbf[:, :], 0.0).then_inc(sq.h, 1)
            HZERO_Q = q_hz + 1

            def st_row(h):
                g = h // 16
                if g < 3:
                    return stage0[32 * g:32 * g + 1, :], ones_f128[32 * g:32 * g + 1, :]
                return stage1[0:1, :], ones_f128[0:1, :]

            MBLK = nc.Block()
            MBLK.__enter__()
            PREV_GV = None
            PREV_GP = None
            for cc in range(NCH):
                t0c = cc * CH
                PRV = sv.n
                PRS = ss.n
                PRP = sp.n
                PRQ = sq.n
                hx = [sha[0].reserve(16), sha[1].reserve(16)]
                d_c2 = sd.reserve(16)
                p_c2 = sp.reserve(4)
                s_c2 = ss.reserve(3)
                v_c2 = sv.reserve(16)
                q_c2 = sq.reserve(2)
                g_st = sg.reserve(5 * 16)
                with _SameBlk(MBLK) as blk:
                    @blk.sync
                    def _(s):
                        s.wait_ge(sv.h, PRV if PREV_GV is None else PREV_GV)
                        s.wait_ge(sp.h, PRP if PREV_GP is None else PREV_GP)
                        s.wait_ge(ss.h, PRS)
                        s.wait_ge(sq.h, PRQ)
                        s.wait_ge(sha[0].h, hx[0])
                        s.wait_ge(sha[1].h, hx[1])
                        s.dma_start(out=Hbuf[:, 0:8192]
                                    .rearrange("p (ct m) -> p ct m", m=4096),
                                    in_=xtm_d[t0c:t0c + CH, :]
                                    .rearrange("(ct p) m -> p ct m", p=P)
                                    ).then_inc(sha[0].h, 16)
                        s.dma_start(out=Hbuf[:, 8192:16384]
                                    .rearrange("p (ct m) -> p ct m", m=4096),
                                    in_=ztm_d[t0c:t0c + CH, :]
                                    .rearrange("(ct p) m -> p ct m", p=P)
                                    ).then_inc(sha[1].h, 16)
                        s.wait_ge(sd.h, d_c2)
                        s.dma_start(out=dtc[:, :],
                                    in_=dt_d[:, t0c:t0c + CH]).then_inc(sd.h, 16)

                    @blk.vector
                    def _(v):
                        v.wait_ge(sv.h, PRV)
                        v.wait_ge(sp.h, PRP)
                        v.wait_ge(sd.h, d_c2 + 16)
                        v.tensor_scalar(out=dA_r, in0=dtc[:, :],
                                        scalar1=fstat[0:64, 2:3], scalar2=None,
                                        op0=ALU.mult).then_inc(sv.h, 1)
                        v.wait_ge(sv.h, v_c2 + 1)
                        v.tensor_tensor_scan(out=cs_r, data0=dA_r, data1=dA_r,
                                             initial=0.0, op0=ALU.add,
                                             op1=ALU.bypass).then_inc(sv.h, 1)
                        v.wait_ge(sv.h, v_c2 + 2)
                        v.tensor_scalar(out=ncs_r, in0=cs_r, scalar1=-1.0,
                                        scalar2=None, op0=ALU.mult).then_inc(sv.h, 1)
                        v.tensor_scalar(out=ds_r, in0=cs_r,
                                        scalar1=cs_r[:, 255:256], scalar2=None,
                                        op0=ALU.subtract).then_inc(sv.h, 1)
                        v.wait_ge(sp.h, p_c2 + 1)
                        v.wait_ge(ss.h, PRS)
                        v.tensor_copy(out=m2t[:, 0:128],
                                      in_=pbank(5)[:, 0:128]).then_inc(sv.h, 1)
                        v.tensor_copy(out=lnBf[:, 0:128],
                                      in_=pbank(5)[:, 128:256]).then_inc(sv.h, 1)
                        v.tensor_copy(out=lnBf[:, 128:256],
                                      in_=pbank(5)[:, 256:384]).then_inc(sv.h, 1)
                        v.tensor_copy(out=lnA[:, 256:384],
                                      in_=pbank(5)[:, 384:512]).then_inc(sv.h, 1)
                        v.wait_ge(sp.h, p_c2 + 2)
                        v.tensor_copy(out=lnA[:, 0:256],
                                      in_=ptb[0][:, 0:256]).then_inc(sv.h, 1)
                        v.wait_ge(sha[0].h, hx[0] + 16)
                        v.wait_ge(sv.h, v_c2 + 8)
                        for ct in range(2):
                            v.tensor_tensor(
                                out=Xdt[ct].rearrange("p (h e) -> p h e", e=64),
                                in0=x2[ct].rearrange("p (h e) -> p h e", e=64),
                                in1=lnA[:, 256 + ct * 64:256 + (ct + 1) * 64]
                                    .rearrange("p (h o) -> p h o", o=1)
                                    .broadcast_to([P, 64, 64]),
                                op=ALU.mult).then_inc(sv.h, 1)
                        v.wait_ge(sp.h, p_c2 + 3)
                        v.tensor_copy(out=gt_sb[:, 0:256],
                                      in_=pbank(4)[:, 0:256]).then_inc(sv.h, 1)
                        v.tensor_copy(out=gt_sb[:, 256:512],
                                      in_=pbank(4)[:, 256:512]).then_inc(sv.h, 1)
                        v.wait_ge(sg.h, g_st + 80)
                        for g4 in range(4):
                            row, _ = st_row(g4 * 16)
                            ins = v.tensor_copy(out=m2rows[0:1, 960 + g4 * 16:976 + g4 * 16],
                                                in_=row[:, 255:4096:256])
                        ins.then_inc(sv.h, 1)
                        v.wait_ge(sv.h, v_c2 + 11)
                        for ct in range(2):
                            v.tensor_tensor(
                                out=x2[ct].rearrange("p (h e) -> p h e", e=64),
                                in0=x2[ct].rearrange("p (h e) -> p h e", e=64),
                                in1=aux[:, 0:64]
                                    .rearrange("p (h o) -> p h o", o=1)
                                    .broadcast_to([P, 64, 64]),
                                op=ALU.mult).then_inc(sv.h, 1)

                    @blk.scalar
                    def _(sc):
                        sc.wait_ge(sv.h, v_c2 + 4)
                        sc.activation(out=ds_r, in_=ds_r, func=AF.Exp,
                                      scale=-1.0).then_inc(ss.h, 1)
                        sc.wait_ge(ss.h, s_c2 + 1)
                        sc.activation(out=dA_r, in_=cs_r, func=AF.Exp).then_inc(ss.h, 1)
                        sc.wait_ge(sp.h, p_c2 + 4)
                        sc.activation(out=aux[:, 64:128], in_=pbank(0)[:, 0:64],
                                      func=AF.Exp).then_inc(ss.h, 1)

                    @blk.tensor
                    def _(t):
                        if cc == 0:
                            t.wait_ge(sq.h, HZERO_Q)
                        t.wait_ge(ss.h, s_c2 + 2)
                        t.wait_ge(sv.h, v_c2 + 4)
                        for ii, src in enumerate([ncs_r, ds_r, dA_r]):
                            for half in range(2):
                                ins = t.transpose(
                                    pbank(5)[:, ii * 128 + half * 64:
                                             ii * 128 + (half + 1) * 64],
                                    src[:, half * 128:(half + 1) * 128],
                                    identf[0:64, 0:64])
                        for half in range(2):
                            ins = t.transpose(
                                pbank(5)[:, 384 + half * 64:384 + (half + 1) * 64],
                                dtc[:, half * 128:(half + 1) * 128],
                                identf[0:64, 0:64])
                        ins.then_inc(sp.h, 1)
                        for half in range(2):
                            ins = t.transpose(ptb[0][:, half * 128:(half + 1) * 128],
                                              B_fm[:, t0c + half * 128:
                                                   t0c + (half + 1) * 128],
                                              ident[:, :])
                        ins.then_inc(sp.h, 1)
                        for sh in range(2):
                            ins = t.matmul(pbank(4)[:, sh * 256:(sh + 1) * 256],
                                           B_fm[:, t0c + sh * 128:t0c + (sh + 1) * 128],
                                           C_fm[:, t0c:t0c + CH],
                                           start=True, stop=True)
                        ins.then_inc(sp.h, 1)
                        t.wait_ge(sv.h, v_c2 + 14)
                        for g4 in range(4):
                            ins = t.matmul(pbank(0)[:, g4 * 16:(g4 + 1) * 16],
                                           ones_rowf,
                                           m2rows[0:1, 960 + g4 * 16:976 + g4 * 16],
                                           start=True, stop=True)
                        ins.then_inc(sp.h, 1)

                    @blk.gpsimd
                    def _(g):
                        g.wait_ge(sv.h, v_c2 + 2)
                        g.wait_ge(sp.h, PRP)
                        g.wait_ge(ss.h, PRS)
                        g.dma_start(out=cs_d[:, :], in_=cs_r).then_inc(sg.h, 16)
                        g.wait_ge(sg.h, g_st + 16)
                        for g4 in range(4):
                            dst = (stage0[32 * g4:32 * g4 + 1, :] if g4 < 3
                                   else stage1[0:1, :])
                            g.dma_start(out=dst,
                                        in_=cs_d[g4 * 16:(g4 + 1) * 16, :]
                                        .rearrange("h c -> (h c)")
                                        .rearrange("(a n) -> a n", a=1)
                                        ).then_inc(sg.h, 16)
                        g.wait_ge(sv.h, v_c2 + 13)
                        for sh in range(2):
                            g.affine_select(gt_sb[:, sh * 256:(sh + 1) * 256],
                                            gt_sb[:, sh * 256:(sh + 1) * 256],
                                            pattern=[[1, 256]], compare_op=ALU.is_ge,
                                            fill=0.0, base=-128 * sh,
                                            channel_multiplier=-1).then_inc(sq.h, 1)

                d8p = [[xqt[:, 0:1024], xqt[:, 1024:2048]],
                       [xqt[:, 4096:5120], xqt[:, 5120:6144]]]
                M4 = [xqt[:, 2048:3072], xqt[:, 3072:4096]]
                ptb0f = ptb[0][:, :].bitcast(F32)
                def pbc(g_, pr):
                    if g_ % 2 == 0:
                        return pbank(pr)[:, 0:512]
                    return pbank(5)[:, 0:512] if pr == 0 else ptb0f
                bds4 = [sqs[0][:, 0:512], sqs[1][:, 0:512]]
                t1s = tpool[0][:, 0:256]
                t2a = tpool[0][:, 256:512]
                t2s = tpool[1][:, 0:256]
                htmp = tpool[1][:, 256:512]
                NG = 16
                BC = {}
                YD = {}
                SD = {}

                def emit_bcast(t, j):
                    for pr in range(2):
                        h = j * 4 + pr * 2
                        row, ones1 = st_row(h)
                        ins = t.matmul(pbc(j, pr), ones1,
                                       row[:, (h % 16) * 256:(h % 16) * 256 + 512],
                                       start=True, stop=True)
                    ins.then_inc(sp.h, 1)

                for g in range(NG):
                    H0 = g * 4
                    d8 = d8p[g % 2]
                    npg = 4 + (2 if g == 0 else 0) - (1 if g >= NG - 2 else 0)
                    p_g = sp.reserve(npg)
                    s_g = ss.reserve(10)
                    v_g = sv.reserve(12)
                    pv = p_g
                    if g == 0:
                        BC[0] = pv + 1
                        BC[1] = pv + 2
                        BC[2] = pv + 3
                        pv += 3
                    elif g + 2 < NG:
                        BC[g + 2] = pv + 1
                        pv += 1
                    YD[g] = pv + 1
                    SD[g] = pv + 3
                    with _SameBlk(MBLK) as blk:
                        @blk.tensor
                        def _(t):
                            if g == 0:
                                t.wait_ge(ss.h, s_g)
                                t.wait_ge(sv.h, v_g)
                                emit_bcast(t, 0)
                                emit_bcast(t, 1)
                                t.wait_ge(ss.h, s_g + 9)
                                emit_bcast(t, 2)
                            elif g + 2 < NG:
                                t.wait_ge(ss.h, s_g + 9)
                                emit_bcast(t, g + 2)
                            t.wait_ge(sv.h, v_g + 2)
                            for hh in range(4):
                                for ch in range(2):
                                    for sh in range(2):
                                        ins = t.matmul(
                                            pbank(2)[:, (ch * 4 + hh) * 64:(ch * 4 + hh + 1) * 64],
                                            M4[sh][:, hh * 256 + ch * 128:hh * 256 + (ch + 1) * 128],
                                            Xdt[sh][:, (H0 + hh) * 64:(H0 + hh + 1) * 64],
                                            start=(sh == 0), stop=(sh == 1))
                            for ch in range(2):
                                ins = t.matmul(
                                    pbank(3)[:, ch * 256:(ch + 1) * 256],
                                    C_fm[:, t0c + ch * 128:t0c + (ch + 1) * 128],
                                    Hbf[:, H0 * 64:(H0 + 4) * 64],
                                    start=True, stop=True)
                            ins.then_inc(sp.h, 1)
                            t.wait_ge(sv.h, v_g + 4)
                            for hh in range(4):
                                for sh in range(2):
                                    ins = t.matmul(
                                        pbank(4)[:, hh * 64:(hh + 1) * 64],
                                        bds4[sh][:, hh * 128:(hh + 1) * 128],
                                        Xdt[sh][:, (H0 + hh) * 64:(H0 + hh + 1) * 64],
                                        start=(sh == 0), stop=(sh == 1))
                                if hh == 1:
                                    ins.then_inc(sp.h, 1)
                            ins.then_inc(sp.h, 1)

                        @blk.scalar
                        def _(sc):
                            sc.wait_ge(sp.h, BC[g])
                            if g >= 2:
                                sc.wait_ge(sv.h, v_g - 22)
                            for sh in range(2):
                                for hh in range(4):
                                    c0 = sh * 64 + H0 + hh
                                    sc.activation(
                                        out=d8[sh][:, hh * 256:(hh + 1) * 256],
                                        in_=pbc(g, hh // 2)[:, (hh % 2) * 256:
                                                            (hh % 2) * 256 + 256],
                                        func=AF.Abs,
                                        bias=m2t[:, c0:c0 + 1]).then_inc(ss.h, 1)
                                sc.wait_ge(ss.h, s_g + 5 * sh + 4)
                                sc.activation(out=d8[sh][:, :], in_=d8[sh][:, :],
                                              func=AF.Exp,
                                              scale=-1.0).then_inc(ss.h, 1)

                        @blk.vector
                        def _(v):
                            v.wait_ge(sq.h, q_c2 + 2)
                            if g >= 1:
                                v.wait_ge(sp.h, YD[g - 1])
                            for sh in range(2):
                                v.wait_ge(ss.h, s_g + 5 * sh + 5)
                                v.tensor_tensor(
                                    out=M4[sh].rearrange("p (h m) -> p h m", m=256),
                                    in0=d8[sh].rearrange("p (h m) -> p h m", m=256),
                                    in1=gt_sb[:, sh * 256:(sh + 1) * 256]
                                        .rearrange("p (o m) -> p o m", o=1)
                                        .broadcast_to([P, 4, 256]),
                                    op=ALU.mult).then_inc(sv.h, 1)
                            if g >= 1:
                                v.wait_ge(sp.h, SD[g - 1])
                            for sh in range(2):
                                v.tensor_tensor(
                                    out=bds4[sh].rearrange("p (h m) -> p h m", m=128),
                                    in0=lnA[:, sh * 128:(sh + 1) * 128]
                                        .rearrange("p (o m) -> p o m", o=1)
                                        .broadcast_to([P, 4, 128]),
                                    in1=lnBf[:, sh * 64 + H0:sh * 64 + H0 + 4]
                                        .rearrange("p (h o) -> p h o", o=1)
                                        .broadcast_to([P, 4, 128]),
                                    op=ALU.mult).then_inc(sv.h, 1)
                            v.wait_ge(sp.h, YD[g])
                            if g >= 1:
                                v.wait_ge(sv.h, v_g - 2)
                            for ch in range(2):
                                if ch == 1:
                                    v.wait_ge(sv.h, v_g + 7)
                                v.tensor_tensor(
                                    out=t1s.rearrange("p (h m) -> p h m", m=64),
                                    in0=pbank(3)[:, ch * 256:(ch + 1) * 256]
                                        .rearrange("p (h m) -> p h m", m=64),
                                    in1=lnBf[:, 128 + 64 * ch + H0:128 + 64 * ch + H0 + 4]
                                        .rearrange("p (h o) -> p h o", o=1)
                                        .broadcast_to([P, 4, 64]),
                                    op=ALU.mult).then_inc(sv.h, 1)
                                v.tensor_tensor(out=t2s,
                                                in0=pbank(2)[:, ch * 256:(ch + 1) * 256],
                                                in1=x2[ch][:, H0 * 64:(H0 + 4) * 64],
                                                op=ALU.add).then_inc(sv.h, 1)
                                v.wait_ge(sv.h, v_g + 6 + 3 * ch)
                                if ch == 0:
                                    v.wait_ge(sp.h, SD[g])
                                v.tensor_tensor(out=y2[ch][:, H0 * 64:(H0 + 4) * 64],
                                                in0=t1s, in1=t2s,
                                                op=ALU.add).then_inc(sv.h, 1)
                            v.wait_ge(ss.h, s_c2 + 3)
                            v.tensor_tensor(
                                out=htmp.rearrange("p (h m) -> p h m", m=64),
                                in0=Hbf[:, H0 * 64:(H0 + 4) * 64]
                                    .rearrange("p (h m) -> p h m", m=64),
                                in1=aux[:, 64 + H0:64 + H0 + 4]
                                    .rearrange("p (h o) -> p h o", o=1)
                                    .broadcast_to([P, 4, 64]),
                                op=ALU.mult).then_inc(sv.h, 1)
                            v.wait_ge(sv.h, v_g + 11)
                            v.tensor_tensor(out=Hbf[:, H0 * 64:(H0 + 4) * 64],
                                            in0=htmp, in1=pbank(4)[:, 0:256],
                                            op=ALU.add).then_inc(sv.h, 1)

                P6_P = sp.n
                P6_S = ss.n
                P6_V = sv.n
                P6_Q = sq.n
                PREV_GP = P6_P
                PREV_GV = P6_V
                p_6 = sp.reserve(8)
                s_6 = ss.reserve(6)
                v_6 = sv.reserve(10)
                q_6 = sq.reserve(2)
                hm3 = [sha[0].reserve(8 * 16), sha[1].reserve(8 * 16)]
                p_m3 = sp.reserve(16)
                v_m3 = sv.reserve(16)
                with _SameBlk(MBLK) as blk:
                    @blk.gpsimd
                    def _(g):
                        g.wait_ge(sha[1].h, hx[1] + 16)
                        g.wait_ge(sv.h, P6_V)
                        g.wait_ge(sp.h, P6_P)
                        for ct in range(2):
                            g.tensor_tensor(out=y2[ct], in0=y2[ct], in1=z2[ct],
                                            op=ALU.mult).then_inc(sq.h, 1)

                    @blk.scalar
                    def _(sc):
                        sc.wait_ge(sp.h, P6_P)
                        sc.wait_ge(sv.h, P6_V)
                        for ct in range(2):
                            sc.wait_ge(sq.h, q_6 + ct + 1)
                            sc.activation(out=x2[ct], in_=y2[ct], func=AF.Square,
                                          accum_out=fstat[:, 4 + ct:5 + ct]).then_inc(ss.h, 1)
                        for ct in range(2):
                            sc.wait_ge(ss.h, s_6 + 2 + ct)
                            sc.activation(out=fstat[:, 6 + ct:7 + ct],
                                          in_=fstat[:, 4 + ct:5 + ct], func=AF.Ln,
                                          scale=1.0 / 4096.0,
                                          bias=cepsF[:, 0:1]).then_inc(ss.h, 1)
                        for ct in range(2):
                            sc.wait_ge(ss.h, s_6 + 4 + ct)
                            sc.activation(out=fstat[:, 8 + ct:9 + ct],
                                          in_=fstat[:, 6 + ct:7 + ct], func=AF.Exp,
                                          scale=-0.5).then_inc(ss.h, 1)

                    @blk.vector
                    def _(v):
                        for ct in range(2):
                            v.wait_ge(ss.h, s_6 + 5 + ct)
                            v.tensor_scalar(out=y2[ct], in0=y2[ct],
                                            scalar1=fstat[:, 8 + ct:9 + ct],
                                            scalar2=None,
                                            op0=ALU.mult).then_inc(sv.h, 1)
                        for bt in range(8):
                            v.wait_ge(sp.h, p_6 + bt + 1)
                            ct = bt // 4
                            fb0 = (bt % 4) * 8
                            v.tensor_copy(
                                out=xqt.rearrange("p (di m) -> p di m", m=256)
                                    [:, fb0:fb0 + 8, ct * 128:(ct + 1) * 128],
                                in_=ptb[bt % 2].rearrange("p (j m) -> p j m", m=128),
                            ).then_inc(sv.h, 1)

                    @blk.tensor
                    def _(t):
                        for bt in range(8):
                            ct = bt // 4
                            fb0 = (bt % 4) * 8
                            t.wait_ge(sv.h, v_6 + ct + 1)
                            if bt >= 2:
                                t.wait_ge(sv.h, v_6 + 2 + bt - 1)
                            elif bt == 0:
                                t.wait_ge(sv.h, P6_V)
                                t.wait_ge(ss.h, P6_S)
                            for j in range(8):
                                ins = t.transpose(
                                    ptb[bt % 2][:, j * 128:(j + 1) * 128],
                                    y2[ct][:, (fb0 + j) * 128:(fb0 + j + 1) * 128],
                                    ident[:, :])
                            ins.then_inc(sp.h, 1)
                        t.wait_ge(sv.h, v_6 + 10)
                        for dmc in range(16):
                            t.wait_ge(sha[dmc % 2].h, hm3[dmc % 2] + (dmc // 2 + 1) * 16)
                            if dmc >= 2:
                                t.wait_ge(sv.h, v_m3 + dmc - 1)
                            for di in range(32):
                                ins = t.matmul(pbank(2 + dmc % 2)[:, 0:256],
                                               opsl[dmc % 2][:, di * 128:(di + 1) * 128],
                                               ynfm[di][:, 0:256],
                                               start=(di == 0), stop=(di == 31))
                            ins.then_inc(sp.h, 1)

                    @blk.sync
                    def _(s):
                        for dmc in range(16):
                            if dmc >= 2:
                                s.wait_ge(sp.h, p_m3 + dmc - 1)
                            else:
                                s.wait_ge(sp.h, p_6 + 8)
                            s.dma_start(out=opsl[dmc % 2][:, :],
                                        in_=blk_op2[dmc, :, :]).then_inc(sha[dmc % 2].h, 16)

                    @blk.vector
                    def _(v):
                        for dmc in range(16):
                            v.wait_ge(sp.h, p_m3 + dmc + 1)
                            v.tensor_tensor(out=res[dmc][:, t0c:t0c + CH],
                                            in0=pbank(2 + dmc % 2)[:, 0:256],
                                            in1=res[dmc][:, t0c:t0c + CH],
                                            op=ALU.add).then_inc(sv.h, 1)
            MBLK.__exit__(None, None, None)
        MB2 = nc.Block()
        MB2.__enter__()
        CURBLK[0] = MB2
        if "l0f" in blocks:
            wsl = [Hbuf[:, nk * D:(nk + 1) * D] for nk in range(DF)]
            Sx = dnslab[0][:, 2048:4096]
            outtm = dnslab[0][:, 0:2048]
            srow = dnslab[1][0:1, 0:4096].bitcast(F32)

            g_guard = sg.n
            v_guard0 = sv.n
            p_guard0 = sp.n
            d_p = sd.reserve(16)
            ga_p = [sga[0].reserve(8 * 16), sga[1].reserve(8 * 16)]
            p_p = sp.reserve(1)
            v_p = sv.reserve(4)
            q_p = sq.reserve(16)
            with _SameBlk(MB2) as blk:
                @blk.sync
                def _(s):
                    s.dma_start(out=srow,
                                in_=w["fw_s"].rearrange("(a n) -> a n", a=1)).then_inc(sd.h, 16)

                @blk.tensor
                def _(t):
                    t.wait_ge(sd.h, d_p + 16)
                    t.wait_ge(sv.h, v_guard0)
                    for b in range(4):
                        ins = t.matmul(pbank(b)[:, 0:512], ones_rowf[0:1, :],
                                       srow[0:1, b * 512:(b + 1) * 512],
                                       start=True, stop=True)
                    ins.then_inc(sp.h, 1)

                @blk.vector
                def _(v):
                    v.wait_ge(sp.h, p_p + 1)
                    for b in range(4):
                        v.tensor_copy(out=Sx[:, b * 512:(b + 1) * 512],
                                      in_=pbank(b)[:, 0:512]).then_inc(sv.h, 1)

                @blk.gpsimd
                def _(g):
                    g.wait_ge(sg.h, g_guard)
                    g.wait_ge(sp.h, p_guard0)
                    for nk in range(DF):
                        if nk >= 2:
                            g.wait_ge(sga[nk % 2].h, ga_p[nk % 2] + (nk // 2) * 16)
                        g.dma_start(out=wsl[nk],
                                    in_=w["fw"][nk * P:(nk + 1) * P, :]).then_inc(sga[nk % 2].h, 16)
                    g.wait_ge(sv.h, v_p + 4)
                    for nk in range(DF):
                        g.wait_ge(sga[nk % 2].h, ga_p[nk % 2] + (nk // 2 + 1) * 16)
                        g.tensor_tensor(out=wsl[nk], in0=wsl[nk], in1=Sx,
                                        op=ALU.mult).then_inc(sq.h, 1)

            sqdump = [sqs[0], sqs[1], tpool[0], tpool[1]]
            prev_v_end = 0
            for ts4 in range(4):
                tok0 = ts4 * 512
                ln_done = emit_ln(tok0, 512, [xq[f][:, 0:512] for f in range(DF)])
                for tcc in range(4):
                    c0 = tcc * 128
                    r0 = tok0 + c0
                    p_c = sp.reserve(5)
                    s_c = ss.reserve(7)
                    v_c = sv.reserve(20)
                    with _SameBlk(MB2) as blk:
                        @blk.tensor
                        def _(t):
                            t.wait_ge(sv.h, ln_done)
                            t.wait_ge(sq.h, q_p + 16)
                            if prev_v_end:
                                t.wait_ge(sv.h, prev_v_end - 4)
                            for b in range(4):
                                for nk in range(DF):
                                    ins = t.matmul(pbank(b)[:, 0:512],
                                                   xq[nk][:, c0:c0 + 128],
                                                   wsl[nk][:, b * 512:(b + 1) * 512],
                                                   start=(nk == 0), stop=(nk == DF - 1))
                            ins.then_inc(sp.h, 1)
                            for bt in range(4):
                                t.wait_ge(sv.h, v_c + 13 + bt)
                                if bt >= 2:
                                    t.wait_ge(sv.h, v_c + 16 + bt - 1)
                                last = None
                                for j in range(4):
                                    last = t.transpose(
                                        ptb[bt % 2][:, j * 128:(j + 1) * 128],
                                        outtm[:, (bt * 4 + j) * 128:(bt * 4 + j + 1) * 128],
                                        ident[:, :])
                                last.then_inc(sp.h, 1)

                        @blk.scalar
                        def _(sc):
                            sc.wait_ge(sp.h, p_c + 1)
                            for b in range(4):
                                sc.activation(out=sqdump[b][:, 0:512],
                                              in_=pbank(b)[:, 0:512], func=AF.Square,
                                              accum_out=fstat[:, b:b + 1]).then_inc(ss.h, 1)
                            sc.wait_ge(sv.h, v_c + 3)
                            sc.activation(out=fstat[:, 7:8], in_=fstat[:, 6:7],
                                          func=AF.Ln, bias=cepsP[:, 0:1]).then_inc(ss.h, 1)
                            sc.wait_ge(ss.h, s_c + 5)
                            sc.activation(out=fstat[:, 8:9], in_=fstat[:, 7:8],
                                          func=AF.Exp, scale=0.5).then_inc(ss.h, 1)
                            sc.wait_ge(sv.h, v_c + 4)
                            sc.activation(out=fstat[:, 11:12], in_=fstat[:, 9:10],
                                          func=AF.Tanh).then_inc(ss.h, 1)

                        @blk.vector
                        def _(v):
                            v.wait_ge(ss.h, s_c + 4)
                            v.tensor_tensor(out=fstat[:, 4:5], in0=fstat[:, 0:1],
                                            in1=fstat[:, 1:2], op=ALU.add).then_inc(sv.h, 1)
                            v.tensor_tensor(out=fstat[:, 5:6], in0=fstat[:, 2:3],
                                            in1=fstat[:, 3:4], op=ALU.add).then_inc(sv.h, 1)
                            v.wait_ge(sv.h, v_c + 2)
                            v.tensor_tensor(out=fstat[:, 6:7], in0=fstat[:, 4:5],
                                            in1=fstat[:, 5:6], op=ALU.add).then_inc(sv.h, 1)
                            v.wait_ge(ss.h, s_c + 6)
                            v.tensor_scalar(out=fstat[:, 9:10], in0=fstat[:, 8:9],
                                            scalar1=1e-7, scalar2=None,
                                            op0=ALU.max).then_inc(sv.h, 1)
                            v.wait_ge(sv.h, v_c + 4)
                            v.tensor_scalar(out=fstat[:, 10:11], in0=fstat[:, 9:10],
                                            scalar1=1e-7, scalar2=None,
                                            op0=ALU.add).then_inc(sv.h, 1)
                            v.wait_ge(sv.h, v_c + 5)
                            v.reciprocal(out=fstat[:, 12:13],
                                         in_=fstat[:, 10:11]).then_inc(sv.h, 1)
                            v.wait_ge(ss.h, s_c + 7)
                            v.wait_ge(sv.h, v_c + 6)
                            v.tensor_tensor(out=fstat[:, 13:14], in0=fstat[:, 11:12],
                                            in1=fstat[:, 12:13], op=ALU.mult).then_inc(sv.h, 1)
                            v.wait_ge(sv.h, v_c + 7)
                            v.tensor_tensor(out=fstat[:, 14:15], in0=fstat[:, 13:14],
                                            in1=fstat[:, 8:9], op=ALU.mult).then_inc(sv.h, 1)
                            v.wait_ge(sv.h, v_c + 8)
                            v.tensor_scalar(out=fstat[:, 15:16], in0=fstat[:, 14:15],
                                            scalar1=1e-7, scalar2=None,
                                            op0=ALU.max).then_inc(sv.h, 1)
                            v.wait_ge(sv.h, v_c + 9)
                            v.reciprocal(out=fstat[:, 16:17],
                                         in_=fstat[:, 15:16]).then_inc(sv.h, 1)
                            v.wait_ge(sv.h, v_c + 10)
                            v.tensor_scalar(out=fstat[:, 17:18], in0=fstat[:, 16:17],
                                            scalar1=0.99, scalar2=1.0, op0=ALU.mult,
                                            op1=ALU.min).then_inc(sv.h, 1)
                            v.wait_ge(sv.h, v_c + 11)
                            v.tensor_tensor(out=fstat[:, 18:19], in0=fstat[:, 13:14],
                                            in1=fstat[:, 17:18], op=ALU.mult).then_inc(sv.h, 1)
                            v.wait_ge(sv.h, v_c + 12)
                            for b in range(4):
                                v.tensor_scalar(out=outtm[:, b * 512:(b + 1) * 512],
                                                in0=pbank(b)[:, 0:512],
                                                scalar1=fstat[:, 18:19], scalar2=None,
                                                op0=ALU.mult).then_inc(sv.h, 1)
                            for bt in range(4):
                                v.wait_ge(sp.h, p_c + 2 + bt)
                                for j in range(4):
                                    f = bt * 4 + j
                                    ins = v.tensor_tensor(
                                        out=res[f][:, r0:r0 + 128],
                                        in0=ptb[bt % 2][:, j * 128:(j + 1) * 128],
                                        in1=res[f][:, r0:r0 + 128], op=ALU.add)
                                ins.then_inc(sv.h, 1)
                    prev_v_end = sv.n

        if "l1f" in blocks:
            P0b = sp.n
            S0b = ss.n
            V0b = sv.n
            Q0b = sq.n

            def emit_blkS(pc_ap, blk_ap, nk, rows, stages, srows, cast_done):
                nh = rows // srows
                oc = srows // P
                trl = sd.reserve(nk * nh * 16)
                g_g = sg.n
                sct = sg.reserve(nk * nh * 16)
                with _SameBlk(MB2) as blk:
                    @blk.sync
                    def _(s):
                        s.wait_ge(cast_done[0].h, cast_done[1])
                        s.wait_ge(sd.h, trl)
                        s.wait_ge(sg.h, g_g)
                        s.wait_ge(sp.h, P0b)
                        s.wait_ge(ss.h, S0b)
                        s.wait_ge(sv.h, V0b)
                        s.wait_ge(sq.h, Q0b)
                        for i in range(nk * nh):
                            k, h = divmod(i, nh)
                            if i >= 1:
                                s.wait_ge(sg.h, sct + i * 16)
                            s.dma_start_transpose(stages[i % 2][:, 0:srows],
                                                  pc_ap[h * srows:(h + 1) * srows,
                                                        k * P:(k + 1) * P]
                                                  ).then_inc(sd.h, 16)

                    @blk.gpsimd
                    def _(g):
                        for i in range(nk * nh):
                            k, h = divmod(i, nh)
                            g.wait_ge(sd.h, trl + (i + 1) * 16)
                            g.dma_start(
                                out=blk_ap[h * oc:(h + 1) * oc, :,
                                           k * P:(k + 1) * P].rearrange("o p m -> p o m"),
                                in_=stages[i % 2][:, 0:srows]
                                    .rearrange("p (o m) -> p o m", o=oc),
                            ).then_inc(sg.h, 16)

            st_up2 = [dnslab[0][:, 4096:8192], dnslab[1][:, 4096:8192]]
            emit_blkS(pc_up, blk_up, 16, 8192, st_up2, 4096, CAST_UP)
            emit_blkS(pc_dn, blk_dn, 64, D, wslab, D, CAST_DN)
            PREP_G = sg.n
            PREP_V = sv.n

        if "l1a" in blocks:
            NH, NKV, HD = 16, 4, 128
            xa = [Hbuf[:, f * 1152:(f + 1) * 1152] for f in range(DF)]
            kbufs = [Hbuf[:, 18432 + kv * 1152:18432 + (kv + 1) * 1152] for kv in range(NKV)]
            vbufs = [Hbuf[:, 23040 + kv * 1152:23040 + (kv + 1) * 1152] for kv in range(NKV)]
            vt_tile = lambda kv, j: Hbuf[:, 27648 + (kv * 10 + j) * 128:
                                         27648 + (kv * 10 + j + 1) * 128]
            ao = [Hbuf[:, f * 1024:(f + 1) * 1024] for f in range(DF)]
            qs = [(dnslab[0] if f < 8 else dnslab[1])[:, (f % 8) * 1024:(f % 8 + 1) * 1024]
                  for f in range(NH)]
            rt_a = lnA[0:64, 0:512]
            rt_b = tpool[0][0:64, 0:512]
            rt_c = lnB[0:64, 0:512]
            kint = dnslab[1][0:64, 0:4096].bitcast(I32)
            kflt = dnslab[1][0:64, 0:4096].bitcast(F32)

            v_ta = sv.n
            p_ta0 = sp.n
            d_ta = sd.reserve(16)
            q_ta = sq.reserve(10)
            s_ta = ss.reserve(3)
            v_tb = sv.reserve(12)
            p_ta = sp.reserve(1)
            TPI = float(2 * np.pi)
            with _SameBlk(MB2) as blk:
                @blk.gpsimd
                def _(g):
                    g.wait_ge(sq.h, CONST_Q)
                    g.wait_ge(sv.h, v_ta)
                    g.iota(fstat[0:64, 20:21], pattern=[[1, 1]], base=0,
                           channel_multiplier=1,
                           allow_small_or_imprecise_dtypes=True).then_inc(sq.h, 1)
                    g.iota(sinf[:, :], pattern=[[1, T]], base=0, channel_multiplier=0,
                           allow_small_or_imprecise_dtypes=True).then_inc(sq.h, 1)
                    g.memset(mask3[:, :], 0.0).then_inc(sq.h, 1)
                    g.wait_ge(sq.h, q_ta + 3)
                    g.affine_select(mask3[:, 0:256], mask3[:, 0:256], pattern=[[1, 256]],
                                    compare_op=ALU.is_ge, fill=-1e30, base=64,
                                    channel_multiplier=-1).then_inc(sq.h, 1)
                    g.wait_ge(sq.h, q_ta + 4)
                    g.affine_select(mask3[:, 0:256], mask3[:, 0:256], pattern=[[-1, 256]],
                                    compare_op=ALU.is_ge, fill=-1e30, base=64,
                                    channel_multiplier=1).then_inc(sq.h, 1)
                    g.wait_ge(sq.h, q_ta + 5)
                    g.affine_select(mask3[:, 256:512], mask3[:, 256:512], pattern=[[1, 256]],
                                    compare_op=ALU.is_ge, fill=-1e30, base=0,
                                    channel_multiplier=-1).then_inc(sq.h, 1)
                    g.wait_ge(sq.h, q_ta + 6)
                    g.affine_select(mask3[:, 256:512], mask3[:, 256:512], pattern=[[-1, 256]],
                                    compare_op=ALU.is_ge, fill=-1e30, base=128,
                                    channel_multiplier=1).then_inc(sq.h, 1)
                    g.affine_select(mask3[:, 512:768], mask3[:, 512:768], pattern=[[1, 256]],
                                    compare_op=ALU.is_ge, fill=-1e30, base=0,
                                    channel_multiplier=-1).then_inc(sq.h, 1)
                    g.wait_ge(sq.h, q_ta + 8)
                    g.affine_select(mask3[:, 512:768], mask3[:, 512:768], pattern=[[-1, 256]],
                                    compare_op=ALU.is_ge, fill=-1e30, base=128,
                                    channel_multiplier=1).then_inc(sq.h, 1)
                    g.wait_ge(sq.h, q_ta + 9)
                    g.affine_select(mask3[:, 512:768], mask3[:, 512:768], pattern=[[-1, 256]],
                                    compare_op=ALU.is_ge, fill=-1e30, base=191,
                                    channel_multiplier=0).then_inc(sq.h, 1)

                @blk.scalar
                def _(sc):
                    sc.wait_ge(sq.h, q_ta + 1)
                    sc.activation(out=fstat[0:64, 21:22], in_=fstat[0:64, 20:21],
                                  func=AF.Exp,
                                  scale=float(-np.log(10000.0) / 64.0)).then_inc(ss.h, 1)
                    sc.wait_ge(sv.h, v_tb + 5)
                    sc.activation(out=sinf[:, :], in_=sinf[:, :],
                                  func=AF.Sin).then_inc(ss.h, 1)
                    sc.wait_ge(sv.h, v_tb + 10)
                    sc.activation(out=cosf[:, :], in_=cosf[:, :],
                                  func=AF.Sin).then_inc(ss.h, 1)

                @blk.vector
                def _(v):
                    v.wait_ge(ss.h, s_ta + 1)
                    v.wait_ge(sq.h, q_ta + 2)
                    v.wait_ge(sp.h, p_ta0)
                    v.tensor_scalar(out=cosf[:, :], in0=sinf[:, :],
                                    scalar1=fstat[0:64, 21:22], scalar2=None,
                                    op0=ALU.mult).then_inc(sv.h, 1)
                    v.wait_ge(sv.h, v_tb + 1)
                    v.tensor_scalar(out=kint[:, :], in0=cosf[:, :], scalar1=1.0 / TPI,
                                    scalar2=None, op0=ALU.mult).then_inc(sv.h, 1)
                    v.wait_ge(sv.h, v_tb + 2)
                    v.scalar_tensor_tensor(out=sinf[:, :], in0=kint[:, :], scalar=-TPI,
                                           in1=cosf[:, :], op0=ALU.mult,
                                           op1=ALU.add).then_inc(sv.h, 1)
                    v.wait_ge(sv.h, v_tb + 3)
                    v.tensor_scalar(out=kflt[:, :], in0=sinf[:, :], scalar1=float(np.pi),
                                    scalar2=None, op0=ALU.is_ge).then_inc(sv.h, 1)
                    v.wait_ge(sv.h, v_tb + 4)
                    v.scalar_tensor_tensor(out=sinf[:, :], in0=kflt[:, :], scalar=-TPI,
                                           in1=sinf[:, :], op0=ALU.mult,
                                           op1=ALU.add).then_inc(sv.h, 1)
                    v.wait_ge(ss.h, s_ta + 2)
                    v.tensor_scalar(out=cosf[:, :], in0=cosf[:, :], scalar1=float(np.pi / 2),
                                    scalar2=None, op0=ALU.add).then_inc(sv.h, 1)
                    v.wait_ge(sv.h, v_tb + 6)
                    v.tensor_scalar(out=kint[:, :], in0=cosf[:, :], scalar1=1.0 / TPI,
                                    scalar2=None, op0=ALU.mult).then_inc(sv.h, 1)
                    v.wait_ge(sv.h, v_tb + 7)
                    v.scalar_tensor_tensor(out=cosf[:, :], in0=kint[:, :], scalar=-TPI,
                                           in1=cosf[:, :], op0=ALU.mult,
                                           op1=ALU.add).then_inc(sv.h, 1)
                    v.wait_ge(sv.h, v_tb + 8)
                    v.tensor_scalar(out=kflt[:, :], in0=cosf[:, :], scalar1=float(np.pi),
                                    scalar2=None, op0=ALU.is_ge).then_inc(sv.h, 1)
                    v.wait_ge(sv.h, v_tb + 9)
                    v.scalar_tensor_tensor(out=cosf[:, :], in0=kflt[:, :], scalar=-TPI,
                                           in1=cosf[:, :], op0=ALU.mult,
                                           op1=ALU.add).then_inc(sv.h, 1)
                    v.wait_ge(ss.h, s_ta + 3)
                    v.tensor_copy(out=cosf2[:, :], in_=cosf[:, :]).then_inc(sv.h, 1)
                    v.tensor_copy(out=sinf2[:, :], in_=sinf[:, :]).then_inc(sv.h, 1)

                @blk.sync
                def _(s):
                    s.wait_ge(sd.h, d_ta)
                    s.dma_start(out=sk_row[:, :],
                                in_=w["sink"].rearrange("a h b c -> a (h b c)")).then_inc(sd.h, 16)

                @blk.tensor
                def _(t):
                    t.wait_ge(sd.h, d_ta + 16)
                    t.wait_ge(sq.h, CONST_Q)
                    t.wait_ge(sv.h, v_ta)
                    t.matmul(pbank(5)[:, 0:16], ones_rowf[0:1, :], sk_row[0:1, :],
                             start=True, stop=True).then_inc(sp.h, 1)
            v_tc = sv.reserve(1)
            with _SameBlk(MB2) as blk:
                @blk.vector
                def _(v):
                    v.wait_ge(sp.h, p_ta + 1)
                    v.tensor_copy(out=sinkb[:, :], in_=pbank(5)[:, 0:16]).then_inc(sv.h, 1)
            ATT_TABLES_V = sv.n
            ATT_TABLES_S = ss.n

            def emit_blk_direct(w_ap, blk_ap, ncols, noc, permute):
                p_gd = sp.n
                v_gd = sv.n
                q_gd = sq.n
                g_gd = sg.n
                ga = [sga[0].reserve(8 * 16), sga[1].reserve(8 * 16)]
                v_bd = sv.reserve(16)
                sct = sg.reserve(16 * 16)
                pslab = dnslab[0][:, 0:D]
                with _SameBlk(MB2) as blk:
                    @blk.gpsimd
                    def _(g):
                        g.wait_ge(sp.h, p_gd)
                        g.wait_ge(sq.h, q_gd)
                        g.wait_ge(sv.h, v_gd)
                        g.wait_ge(sg.h, g_gd)
                        for kc in range(16):
                            if kc >= 2:
                                if permute:
                                    g.wait_ge(sv.h, v_bd + kc - 1)
                                else:
                                    g.wait_ge(sg.h, sct + (kc - 1) * 16)
                            g.dma_start(out=wslab[kc % 2][:, 0:ncols],
                                        in_=w_ap[kc * P:(kc + 1) * P, :]).then_inc(sga[kc % 2].h, 16)
                            g.wait_ge(sv.h, v_bd + kc + 1)
                            if kc >= 1:
                                g.wait_ge(sg.h, sct + kc * 16)
                            src = pslab[:, 0:ncols] if permute else wslab[kc % 2][:, 0:ncols]
                            g.dma_start(
                                out=blk_ap[:, :, kc * P:(kc + 1) * P].rearrange("o p m -> p o m"),
                                in_=src.rearrange("p (o m) -> p o m", o=noc),
                            ).then_inc(sg.h, 16)

                    @blk.vector
                    def _(v):
                        for kc in range(16):
                            v.wait_ge(sga[kc % 2].h, ga[kc % 2] + (kc // 2 + 1) * 16)
                            if permute:
                                if kc >= 1:
                                    v.wait_ge(sg.h, sct + kc * 16)
                                for oc in range(noc):
                                    v.tensor_copy(
                                        out=pslab[:, oc * 128:oc * 128 + 64],
                                        in_=wslab[kc % 2][:, oc * 128:(oc + 1) * 128:2])
                                    ins = v.tensor_copy(
                                        out=pslab[:, oc * 128 + 64:(oc + 1) * 128],
                                        in_=wslab[kc % 2][:, oc * 128 + 1:(oc + 1) * 128:2])
                                ins.then_inc(sv.h, 1)
                            else:
                                v.nop().then_inc(sv.h, 1)

            emit_blk_direct(w["wq"], blk_q, D, 16, True)
            emit_blk_direct(w["wk"], blk_k, 512, 4, True)
            emit_blk_direct(w["wv"], blk_v, 512, 4, False)
            emit_blk_direct(w["wo"], blk_o, D, 16, False)

            for hh2 in range(2):
                t0 = hh2 * 1024
                lo = max(0, t0 - 64)
                hi = min(T, t0 + 1024 + 64)
                kw = hi - lo
                ln_parts = []
                off = 0
                while off < kw:
                    Wp = min(512, kw - off)
                    ln_done = emit_ln(lo + off, Wp,
                                      [xa[f][:, off:off + Wp] for f in range(DF)])
                    off += Wp
                XA_V = ln_done

                p_gd = sp.n
                s_gd = ss.n
                g_gd2 = sg.n
                hq = [sha[0].reserve(8 * 16), sha[1].reserve(8 * 16)]
                p_q = sp.reserve(32)
                v_q = sv.reserve(32)
                hkv = [sha[0].reserve(2 * 2 * 16), sha[1].reserve(2 * 2 * 16)]
                p_kv = sp.reserve(NKV * 2 * 3)
                v_kv = sv.reserve(NKV * 2 * 3)
                with _SameBlk(MB2) as blk:
                    @blk.sync
                    def _(s):
                        s.wait_ge(sha[0].h, hq[0])
                        s.wait_ge(sha[1].h, hq[1])
                        for qf in range(NH):
                            if qf >= 2:
                                s.wait_ge(sp.h, p_q + (qf - 1) * 2)
                            else:
                                s.wait_ge(sp.h, p_gd)
                                s.wait_ge(sg.h, g_gd2)
                            s.dma_start(out=wslab[qf % 2][:, :],
                                        in_=blk_q[qf, :, :]).then_inc(sha[qf % 2].h, 16)
                        for kv in range(NKV):
                            for which in range(2):
                                i = kv * 2 + which
                                if i >= 2:
                                    s.wait_ge(sp.h, p_kv + (i - 1) * 3)
                                else:
                                    s.wait_ge(sp.h, p_q + 32)
                                src = blk_k if which == 0 else blk_v
                                s.dma_start(out=wslab[i % 2][:, :],
                                            in_=src[kv, :, :]).then_inc(sha[i % 2].h, 16)

                    @blk.tensor
                    def _(t):
                        t.wait_ge(sv.h, XA_V)
                        for qf in range(NH):
                            t.wait_ge(sha[qf % 2].h, hq[qf % 2] + (qf // 2 + 1) * 16)
                            for tb in range(2):
                                i2 = qf * 2 + tb
                                if i2 >= 2:
                                    t.wait_ge(sv.h, v_q + i2 - 1)
                                for k in range(DF):
                                    ins = t.matmul(
                                        pbank(i2 % 2)[:, 0:512],
                                        wslab[qf % 2][:, k * P:(k + 1) * P],
                                        xa[k][:, (t0 - lo) + tb * 512:(t0 - lo) + (tb + 1) * 512],
                                        start=(k == 0), stop=(k == DF - 1))
                                ins.then_inc(sp.h, 1)
                        tbs = [(0, 512), (512, 512), (1024, kw - 1024)]
                        for kv in range(NKV):
                            for which in range(2):
                                i = kv * 2 + which
                                t.wait_ge(sha[i % 2].h, hkv[i % 2] + (i // 2 + 1) * 16)
                                for ti, (toff, tw) in enumerate(tbs):
                                    i3 = i * 3 + ti
                                    if i3 >= 2:
                                        t.wait_ge(sv.h, v_kv + i3 - 1)
                                    else:
                                        t.wait_ge(sv.h, v_q + 31 + i3)
                                    for k in range(DF):
                                        ins = t.matmul(
                                            pbank(i3 % 2)[:, 0:tw],
                                            wslab[i % 2][:, k * P:(k + 1) * P],
                                            xa[k][:, toff:toff + tw],
                                            start=(k == 0), stop=(k == DF - 1))
                                    ins.then_inc(sp.h, 1)

                    @blk.vector
                    def _(v):
                        for qf in range(NH):
                            for tb in range(2):
                                i2 = qf * 2 + tb
                                v.wait_ge(sp.h, p_q + i2 + 1)
                                v.tensor_copy(out=qs[qf][:, tb * 512:(tb + 1) * 512],
                                              in_=pbank(i2 % 2)[:, 0:512]).then_inc(sv.h, 1)
                        tbs = [(0, 512), (512, 512), (1024, kw - 1024)]
                        for kv in range(NKV):
                            for which in range(2):
                                i = kv * 2 + which
                                dst = kbufs[kv] if which == 0 else vbufs[kv]
                                for ti, (toff, tw) in enumerate(tbs):
                                    i3 = i * 3 + ti
                                    v.wait_ge(sp.h, p_kv + i3 + 1)
                                    v.tensor_copy(out=dst[:, toff:toff + tw],
                                                  in_=pbank(i3 % 2)[:, 0:tw]).then_inc(sv.h, 1)

                QKV_V = sv.n
                n_rp = NH * 2 + NKV * 3
                v_r = sv.reserve(n_rp * 6)
                with _SameBlk(MB2) as blk:
                    @blk.vector
                    def _(v):
                        v.wait_ge(sv.h, QKV_V)
                        v.wait_ge(ss.h, ATT_TABLES_S)
                        idx = 0
                        pieces = []
                        for f in range(NH):
                            for pc in range(2):
                                pieces.append((qs[f], pc * 512, 512, t0 + pc * 512))
                        for kv in range(NKV):
                            for pc, (toff, tw) in enumerate([(0, 512), (512, 512), (1024, kw - 1024)]):
                                pieces.append((kbufs[kv], toff, tw, lo + toff))
                        for (buf, coff, cw, gt) in pieces:
                            b0 = v_r + idx * 6
                            v.wait_ge(sv.h, b0)
                            qr = buf[0:64, coff:coff + cw]
                            qi = buf[64:128, coff:coff + cw]
                            cs = cosf[0:64, gt:gt + cw]
                            sn = sinf[0:64, gt:gt + cw]
                            cs2 = cosf2[:, gt:gt + cw]
                            sn2 = sinf2[:, gt:gt + cw]
                            ta = rt_a[:, 0:cw]
                            tb_ = rt_b[:, 0:cw]
                            tc_ = rt_c[:, 0:cw]
                            v.tensor_tensor(out=ta, in0=qr, in1=cs, op=ALU.mult).then_inc(sv.h, 1)
                            v.tensor_tensor(out=tb_, in0=qr, in1=sn, op=ALU.mult).then_inc(sv.h, 1)
                            v.wait_ge(sv.h, b0 + 2)
                            v.tensor_tensor(out=tc_, in0=qi, in1=sn2, op=ALU.mult).then_inc(sv.h, 1)
                            v.wait_ge(sv.h, b0 + 3)
                            v.tensor_tensor(out=qr, in0=ta, in1=tc_, op=ALU.subtract).then_inc(sv.h, 1)
                            v.wait_ge(sv.h, b0 + 4)
                            v.tensor_tensor(out=ta, in0=qi, in1=cs2, op=ALU.mult).then_inc(sv.h, 1)
                            v.wait_ge(sv.h, b0 + 5)
                            v.tensor_tensor(out=qi, in0=tb_, in1=ta, op=ALU.add).then_inc(sv.h, 1)
                            idx += 1
                ROPE_V = sv.n

                if hh2 == 0:
                    grids = [(0, [0, 128]), (1, [64 + 128 * j for j in range(8)])]
                else:
                    grids = [(0, [128 * j for j in range(9)])]
                vt_index = {}
                n_vt = sum(len(offs) for _, offs in grids) * NKV
                p_vt = sp.reserve((n_vt + 3) // 4)
                v_vt = sv.reserve((n_vt + 3) // 4)
                q_vt = sq.reserve(1)
                with _SameBlk(MB2) as blk:
                    @blk.gpsimd
                    def _(g):
                        g.wait_ge(sv.h, QKV_V)
                        for kv in range(NKV):
                            g.memset(vbufs[kv][:, kw:1152], 0.0)
                            ins = g.memset(kbufs[kv][:, kw:1152], 0.0)
                        ins.then_inc(sq.h, 1)

                    @blk.tensor
                    def _(t):
                        t.wait_ge(sv.h, ROPE_V)
                        t.wait_ge(sq.h, q_vt + 1)
                        tiles = []
                        for kv in range(NKV):
                            slot = 0
                            for gi, offs in grids:
                                for off in offs:
                                    vt_index[(kv, gi, off)] = slot
                                    tiles.append((kv, slot, off))
                                    slot += 1
                        for bi in range(0, len(tiles), 4):
                            batch = tiles[bi:bi + 4]
                            bb = bi // 4
                            if bb >= 2:
                                t.wait_ge(sv.h, v_vt + bb - 1)
                            for j, (kv, slot, off) in enumerate(batch):
                                ins = t.transpose(ptb[bb % 2][:, j * 128:(j + 1) * 128],
                                                  vbufs[kv][:, off:off + 128], ident[:, :])
                            ins.then_inc(sp.h, 1)

                    @blk.vector
                    def _(v):
                        tiles = []
                        for kv in range(NKV):
                            slot = 0
                            for gi, offs in grids:
                                for off in offs:
                                    tiles.append((kv, slot, off))
                                    slot += 1
                        for bi in range(0, len(tiles), 4):
                            bb = bi // 4
                            v.wait_ge(sp.h, p_vt + bb + 1)
                            for j, (kv, slot, off) in enumerate(batch := tiles[bi:bi + 4]):
                                ins = v.tensor_copy(out=vt_tile(kv, slot),
                                                    in_=ptb[bb % 2][:, j * 128:(j + 1) * 128])
                            ins.then_inc(sv.h, 1)
                VT_V = sv.n

                scale = float(HD ** -0.5)
                for qc in range(8):
                    gqc = hh2 * 8 + qc
                    if hh2 == 0:
                        if qc == 0:
                            koff, mwid, moff, gi = 0, 256, 0, 0
                        else:
                            koff, mwid, moff, gi = qc * 128 - 64, 256, 256, 1
                    else:
                        koff, gi = qc * 128, 0
                        if qc == 7:
                            mwid, moff = 256, 512
                        else:
                            mwid, moff = 256, 256
                    for g4 in range(4):
                        p_s = sp.reserve(1)
                        s_s = ss.reserve(4)
                        v_s = sv.reserve(6)
                        p_av = sp.reserve(NH // 4 * 2)
                        v_av = sv.reserve(NH // 4 * 2)
                        pslc = pf[:, (g4 % 2) * 1024:(g4 % 2) * 1024 + 1024]
                        with _SameBlk(MB2) as blk:
                            @blk.tensor
                            def _(t):
                                t.wait_ge(sv.h, VT_V)
                                t.wait_ge(ss.h, s_s)
                                for hh in range(4):
                                    ins = t.matmul(pslc[:, hh * 256:hh * 256 + 256],
                                                   qs[g4 * 4 + hh][:, qc * 128:(qc + 1) * 128],
                                                   kbufs[g4][:, koff:koff + 256],
                                                   start=True, stop=True)
                                ins.then_inc(sp.h, 1)
                                for hh in range(4):
                                    hgl = g4 * 4 + hh
                                    t.wait_ge(sv.h, v_s + 5 + (hh // 2))
                                    if hh >= 2:
                                        t.wait_ge(sv.h, v_av + 2 * hh - 3)
                                    last = None
                                    for jj in range(2):
                                        last = t.transpose(
                                            ptb[hh % 2][:, jj * 128:(jj + 1) * 128],
                                            tpool[hh // 2][:, (hh % 2) * 256 + jj * 128:
                                                           (hh % 2) * 256 + (jj + 1) * 128],
                                            ident[:, :])
                                    last.then_inc(sp.h, 1)
                                    t.wait_ge(sv.h, v_av + 2 * hh + 1)
                                    for jj in range(2):
                                        off_j = koff + jj * 128
                                        slot = vt_index[(g4, gi, off_j)]
                                        ins = t.matmul(pbank(4)[:, 0:128],
                                                       vt_tile(g4, slot),
                                                       (lnA if hh % 2 == 0 else lnB)[:, jj * 128:(jj + 1) * 128],
                                                       start=(jj == 0), stop=(jj == 1))
                                    ins.then_inc(sp.h, 1)

                            @blk.vector
                            def _(v):
                                v.wait_ge(sp.h, p_s + 1)
                                v.tensor_tensor(
                                    out=pslc.rearrange("p (h n) -> p h n", h=4),
                                    in0=pslc.rearrange("p (h n) -> p h n", h=4),
                                    in1=mask3[:, moff:moff + mwid]
                                        .rearrange("p (a n) -> p a n", a=1)
                                        .broadcast_to([P, 4, 256]),
                                    op=ALU.add).then_inc(sv.h, 1)
                                v.wait_ge(sv.h, v_s + 1)
                                v.tensor_reduce(out=fstat[:, 16:20].rearrange("p (h a) -> p h a", a=1),
                                                in_=pslc.rearrange("p (h n) -> p h n", h=4),
                                                axis=mybir.AxisListType.X, op=ALU.max,
                                                negate=True).then_inc(sv.h, 1)
                                v.wait_ge(sv.h, v_s + 2)
                                v.tensor_scalar(out=fstat[:, 16:20], in0=fstat[:, 16:20],
                                                scalar1=scale, scalar2=None,
                                                op0=ALU.mult).then_inc(sv.h, 1)
                                v.wait_ge(ss.h, s_s + 4)
                                v.reciprocal(out=fstat[:, 20:24],
                                             in_=fstat[:, 12:16]).then_inc(sv.h, 1)
                                v.wait_ge(sv.h, v_s + 4)
                                for hh in range(4):
                                    ins = v.tensor_scalar(
                                        out=tpool[hh // 2][:, (hh % 2) * 256:(hh % 2 + 1) * 256],
                                        in0=sqs[hh // 2][:, (hh % 2) * 256:(hh % 2 + 1) * 256],
                                        scalar1=fstat[:, 20 + hh:21 + hh], scalar2=None,
                                        op0=ALU.mult)
                                    if hh % 2 == 1:
                                        ins.then_inc(sv.h, 1)
                                for hh in range(4):
                                    v.wait_ge(sp.h, p_av + 2 * hh + 1)
                                    v.tensor_copy(out=(lnA if hh % 2 == 0 else lnB)[:, 0:256],
                                                  in_=ptb[hh % 2][:, 0:256]).then_inc(sv.h, 1)
                                    v.wait_ge(sp.h, p_av + 2 * hh + 2)
                                    v.tensor_copy(out=ao[g4 * 4 + hh][:, qc * 128:(qc + 1) * 128],
                                                  in_=pbank(4)[:, 0:128]).then_inc(sv.h, 1)

                            @blk.scalar
                            def _(sc):
                                sc.wait_ge(sv.h, v_s + 3)
                                for hh in range(4):
                                    sc.activation(out=sqs[hh // 2][:, (hh % 2) * 256:(hh % 2 + 1) * 256],
                                                  in_=pslc[:, hh * 256:hh * 256 + 256],
                                                  func=AF.Exp, scale=scale,
                                                  bias=fstat[:, 16 + hh:17 + hh],
                                                  accum_out=fstat[:, 12 + hh:13 + hh]).then_inc(ss.h, 1)

                AO_V = sv.n
                ho = [sha[0].reserve(8 * 16), sha[1].reserve(8 * 16)]
                p_o = sp.reserve(32)
                v_o = sv.reserve(32)
                with _SameBlk(MB2) as blk:
                    @blk.sync
                    def _(s):
                        for dmc in range(16):
                            if dmc >= 2:
                                s.wait_ge(sp.h, p_o + (dmc - 1) * 2)
                            else:
                                s.wait_ge(sp.h, p_o)
                            s.dma_start(out=wslab[dmc % 2][:, :],
                                        in_=blk_o[dmc, :, :]).then_inc(sha[dmc % 2].h, 16)

                    @blk.tensor
                    def _(t):
                        t.wait_ge(sv.h, AO_V)
                        for dmc in range(16):
                            t.wait_ge(sha[dmc % 2].h, ho[dmc % 2] + (dmc // 2 + 1) * 16)
                            for tb in range(2):
                                i2 = dmc * 2 + tb
                                if i2 >= 2:
                                    t.wait_ge(sv.h, v_o + i2 - 1)
                                for k in range(DF):
                                    ins = t.matmul(pbank(4 + i2 % 2)[:, 0:512],
                                                   wslab[dmc % 2][:, k * P:(k + 1) * P],
                                                   ao[k][:, tb * 512:(tb + 1) * 512],
                                                   start=(k == 0), stop=(k == DF - 1))
                                ins.then_inc(sp.h, 1)

                    @blk.vector
                    def _(v):
                        for dmc in range(16):
                            for tb in range(2):
                                i2 = dmc * 2 + tb
                                v.wait_ge(sp.h, p_o + i2 + 1)
                                v.tensor_tensor(
                                    out=res[dmc][:, t0 + tb * 512:t0 + (tb + 1) * 512],
                                    in0=pbank(4 + i2 % 2)[:, 0:512],
                                    in1=res[dmc][:, t0 + tb * 512:t0 + (tb + 1) * 512],
                                    op=ALU.add).then_inc(sv.h, 1)

        if "l1f" in blocks:
            d_w5 = sd.reserve(3 * 16)
            p_w5 = sp.reserve(1)
            v_w5 = sv.reserve(3)
            with _SameBlk(MB2) as blk:
                @blk.sync
                def _(s):
                    s.wait_ge(sd.h, d_w5)
                    s.dma_start(out=usr[:, :],
                                in_=w["us"].rearrange("(r c) -> r c", r=64)).then_inc(sd.h, 16)
                    s.dma_start(out=dsr[:, :],
                                in_=w["ds"].rearrange("(r c) -> r c", r=16)).then_inc(sd.h, 16)
                    s.dma_start(out=osr[:, :],
                                in_=w["os"].rearrange("(a b) -> a b", a=1)).then_inc(sd.h, 16)

                @blk.tensor
                def _(t):
                    t.wait_ge(sd.h, d_w5 + 48)
                    t.wait_ge(sq.h, CONST_Q)
                    t.transpose(pbank(2)[:, 0:64], usr[:, :], identf[0:64, 0:64])
                    t.transpose(pbank(3)[:, 0:16], dsr[:, :], identf[0:16, 0:16])
                    t.matmul(pbank(3)[:, 16:17], ones_rowf[0:1, :], osr[0:1, 0:1],
                             start=True, stop=True).then_inc(sp.h, 1)

                @blk.vector
                def _(v):
                    v.wait_ge(sp.h, p_w5 + 1)
                    v.tensor_copy(out=usf[:, :], in_=pbank(2)[:, 0:64]).then_inc(sv.h, 1)
                    v.tensor_copy(out=osb[:, :], in_=pbank(3)[:, 16:17]).then_inc(sv.h, 1)
                    v.wait_ge(sv.h, v_w5 + 2)
                    v.tensor_scalar(out=dsosf[:, :], in0=pbank(3)[:, 0:16],
                                    scalar1=osb[:, 0:1], scalar2=None,
                                    op0=ALU.mult).then_inc(sv.h, 1)


            for qq in range(4):
                tok0 = qq * 512
                ln_done = emit_ln(tok0, 512, [xq[f][:, 0:512] for f in range(DF)])

                p_guard = sp.n
                s_guard = ss.n
                du = [sha[0].reserve(32 * 16), sha[1].reserve(32 * 16)]
                p_up = sp.reserve(64)
                s_q = ss.reserve(128)

                def up_sign(sc, j):
                    sc.wait_ge(sha[j % 2].h, du[j % 2] + (j // 2 + 1) * 16)
                    if j >= 2:
                        sc.wait_ge(sp.h, p_up + j - 1)
                    sc.sign(out=wslab[j % 2][:, :],
                            in_=wslab[j % 2][:, :]).then_inc(ss.h, 1)

                with _SameBlk(MB2) as blk:
                    @blk.sync
                    def _(s):
                        s.wait_ge(sha[0].h, du[0])
                        s.wait_ge(sha[1].h, du[1])
                        if qq == 0:
                            s.wait_ge(sg.h, PREP_G)
                        for hc in range(64):
                            if hc >= 2:
                                s.wait_ge(sp.h, p_up + hc - 1)
                            else:
                                s.wait_ge(sp.h, p_guard)
                            s.dma_start(out=wslab[hc % 2][:, :],
                                        in_=blk_up[hc, :, :]).then_inc(sha[hc % 2].h, 16)

                    @blk.tensor
                    def _(t):
                        t.wait_ge(sv.h, ln_done)
                        for hc in range(64):
                            t.wait_ge(ss.h, s_q + (2 * hc if hc >= 2 else hc + 1))
                            for k in range(DF):
                                ins = t.matmul(pbank(hc % 2)[:, 0:512],
                                               wslab[hc % 2][:, k * P:(k + 1) * P],
                                               xq[k][:, 0:512],
                                               start=(k == 0), stop=(k == DF - 1))
                            ins.then_inc(sp.h, 1)

                    @blk.scalar
                    def _(sc):
                        if qq == 0:
                            sc.wait_ge(sg.h, PREP_G)
                        up_sign(sc, 0)
                        up_sign(sc, 1)
                        for hc in range(64):
                            if hc + 2 < 64:
                                up_sign(sc, hc + 2)
                            sc.wait_ge(sp.h, p_up + hc + 1)
                            sc.activation(out=Hbuf[:, hc * 512:(hc + 1) * 512],
                                          in_=pbank(hc % 2)[:, 0:512], func=AF.Relu,
                                          scale=usf[:, hc:hc + 1]).then_inc(ss.h, 1)

                dd = [sha[0].reserve(8 * 16), sha[1].reserve(8 * 16)]
                p_dn = sp.reserve(16)
                v_dn = sv.reserve(16)
                s_d = ss.reserve(16)
                with _SameBlk(MB2) as blk:
                    @blk.sync
                    def _(s):
                        s.wait_ge(sha[0].h, dd[0])
                        s.wait_ge(sha[1].h, dd[1])
                        for dmc in range(16):
                            if dmc >= 2:
                                s.wait_ge(sp.h, p_dn + dmc - 1)
                            s.dma_start(out=dnslab[dmc % 2][:, :],
                                        in_=blk_dn[dmc, :, :]).then_inc(sha[dmc % 2].h, 16)

                    @blk.scalar
                    def _(sc):
                        for dmc in range(16):
                            sc.wait_ge(sha[dmc % 2].h, dd[dmc % 2] + (dmc // 2 + 1) * 16)
                            if dmc >= 2:
                                sc.wait_ge(sp.h, p_dn + dmc - 1)
                            sc.sign(out=dnslab[dmc % 2][:, :],
                                    in_=dnslab[dmc % 2][:, :]).then_inc(ss.h, 1)

                    @blk.tensor
                    def _(t):
                        t.wait_ge(ss.h, s_q + 128)
                        for dmc in range(16):
                            t.wait_ge(ss.h, s_d + dmc + 1)
                            if dmc >= 2:
                                t.wait_ge(sv.h, v_dn + dmc - 1)
                            for hc in range(64):
                                ins = t.matmul(pbank(2 + dmc % 2)[:, 0:512],
                                               dnslab[dmc % 2][:, hc * P:(hc + 1) * P],
                                               Hbuf[:, hc * 512:(hc + 1) * 512],
                                               start=(hc == 0), stop=(hc == 63))
                            ins.then_inc(sp.h, 1)

                    @blk.vector
                    def _(v):
                        v.wait_ge(sv.h, v_w5 + 3)
                        for dmc in range(16):
                            v.wait_ge(sp.h, p_dn + dmc + 1)
                            v.scalar_tensor_tensor(
                                out=res[dmc][:, tok0:tok0 + 512],
                                in0=pbank(2 + dmc % 2)[:, 0:512],
                                scalar=dsosf[:, dmc:dmc + 1],
                                in1=res[dmc][:, tok0:tok0 + 512],
                                op0=ALU.mult, op1=ALU.add).then_inc(sv.h, 1)

        MB2.__exit__(None, None, None)
        CURBLK[0] = None
        pe0 = sp.reserve(16 * 4)
        ve0 = sv.reserve(16 * 4)
        st0 = [sgb[0].reserve(8 * 16), sgb[1].reserve(8 * 16)]
        with nc.Block() as blk:
            @blk.tensor
            def _(t):
                for tc in range(16):
                    for b in range(4):
                        i = tc * 4 + b
                        if i >= 2:
                            t.wait_ge(sv.h, ve0 + i - 1)
                        elif i == 0:
                            t.wait_ge(sv.h, ve0)
                        half = i % 2
                        last = None
                        for j in range(4):
                            f = b * 4 + j
                            last = t.transpose(
                                ptb[half][:, j * 128:(j + 1) * 128],
                                res[f][:, tc * P:(tc + 1) * P], ident[:, :])
                        last.then_inc(sp.h, 1)

            @blk.vector
            def _(v):
                for tc in range(16):
                    if tc >= 1:
                        pv = (tc - 1) % 2
                        v.wait_ge(sgb[pv].h, st0[pv] + ((tc - 1) // 2 + 1) * 16)
                    for b in range(4):
                        i = tc * 4 + b
                        v.wait_ge(sp.h, pe0 + i + 1)
                        half = i % 2
                        for j in range(4):
                            f = b * 4 + j
                            ins = v.tensor_copy(
                                out=ytm[tc % 2][:, f * P:(f + 1) * P],
                                in_=ptb[half][:, j * 128:(j + 1) * 128])
                        ins.then_inc(sv.h, 1)

            @blk.gpsimd
            def _(g):
                for tc in range(16):
                    g.wait_ge(sv.h, ve0 + tc * 4 + 4)
                    g.dma_start(out=y_h[tc * P:(tc + 1) * P, :],
                                in_=ytm[tc % 2][:, :]).then_inc(sgb[tc % 2].h, 16)
                g.wait_ge(sgb[0].h, st0[0] + 128)
                g.wait_ge(sgb[1].h, st0[1] + 128)

    return nc




_NC_CACHE = {}


def _get_nc(blocks):
    key = tuple(blocks)
    if key not in _NC_CACHE:
        _NC_CACHE[key] = build(blocks)
    return _NC_CACHE[key]


WKEYS = {
    "l1f": ["L1_ffn_up_weight", "L1_ffn_down_weight", "L1_ffn_up_scale",
            "L1_ffn_down_scale", "L1_ffn_output_scale"],
    "l0f": ["L0_ffn_weight", "L0_ffn_scale"],
    "l1a": ["L1_attn_wq", "L1_attn_wk", "L1_attn_wv", "L1_attn_wo", "L1_attn_sink"],
    "l0a": ["L0_attn_in_proj_w", "L0_attn_conv_w", "L0_attn_conv_b",
            "L0_attn_dt_bias", "L0_attn_A_log", "L0_attn_D",
            "L0_attn_norm_weight", "L0_attn_out_proj_w"],
}


def make_in_map(inputs, core, blocks=ALL_BLOCKS):
    m = {"x": np.ascontiguousarray(np.asarray(inputs["L0_attn_u"][core], dtype=np.float32))}
    for b in blocks:
        for k in WKEYS.get(b, []):
            m[k] = np.ascontiguousarray(np.asarray(inputs[k], dtype=np.float32))
    return m


def kernel(**inputs):
    nc = _get_nc(ALL_BLOCKS)
    in_maps = [make_in_map(inputs, c, ALL_BLOCKS) for c in range(8)]
    res = run_bass_kernel_spmd(nc, in_maps, core_ids=list(range(8)))
    out = np.stack([r["y"] for r in res.results], axis=0)
    return out.astype(np.float32)


# version 14 (final)
# speedup vs baseline: 1.0300x; 1.0300x over previous
import os
import numpy as np
from contextlib import ExitStack
L0A_PH = int(os.environ.get("L0A_PH", "9"))
import concourse.bass as bass
import concourse.mybir as mybir
from concourse.bass_utils import run_bass_kernel_spmd

F32 = mybir.dt.float32
BF16 = mybir.dt.bfloat16
I32 = mybir.dt.int32
AF = mybir.ActivationFunctionType
ALU = mybir.AluOpType

P = 128
T = 2048
D = 2048
DF = D // P

ALL_BLOCKS = ("l0a", "l0f", "l1a", "l1f")


class SC:
    __slots__ = ("h", "n")

    def __init__(self, ctx, nc, name):
        self.h = ctx.enter_context(nc.semaphore(name))
        self.n = 0

    def reserve(self, k):
        base = self.n
        self.n += k
        return base


def build(blocks=ALL_BLOCKS):
    nc = bass.Bass("TRN2", target_bir_lowering=False)

    x_h = nc.dram_tensor("x", [T, D], F32, kind="ExternalInput").ap()
    y_h = nc.dram_tensor("y", [T, D], F32, kind="ExternalOutput").ap()

    w = {}
    if "l0a" in blocks:
        w["ip"] = nc.dram_tensor("L0_attn_in_proj_w", [8512, D], F32, kind="ExternalInput").ap()
        w["cw"] = nc.dram_tensor("L0_attn_conv_w", [4352, 4], F32, kind="ExternalInput").ap()
        w["cb"] = nc.dram_tensor("L0_attn_conv_b", [4352], F32, kind="ExternalInput").ap()
        w["dtb"] = nc.dram_tensor("L0_attn_dt_bias", [64], F32, kind="ExternalInput").ap()
        w["alog"] = nc.dram_tensor("L0_attn_A_log", [64], F32, kind="ExternalInput").ap()
        w["dd"] = nc.dram_tensor("L0_attn_D", [64], F32, kind="ExternalInput").ap()
        w["nw"] = nc.dram_tensor("L0_attn_norm_weight", [4096], F32, kind="ExternalInput").ap()
        w["op"] = nc.dram_tensor("L0_attn_out_proj_w", [D, 4096], F32, kind="ExternalInput").ap()
        pc_ip = nc.dram_tensor("pc_ip", [8512, D], BF16, kind="Internal").ap()
        blk_ip = nc.dram_tensor("blk_ip", [66, P, D], BF16, kind="Internal").ap()
        pc_op = nc.dram_tensor("pc_op", [D, 4096], BF16, kind="Internal").ap()
        blk_op2 = nc.dram_tensor("blk_op2", [16, P, 4096], BF16, kind="Internal").ap()
        xtm_d = nc.dram_tensor("xtm_d", [T, 4096], BF16, kind="Internal").ap()
        dt_d = nc.dram_tensor("dt_d", [64, T], F32, kind="Internal").ap()
        cs_d = nc.dram_tensor("cs_d", [64, 256], F32, kind="Internal").ap()
        ztm_d = nc.dram_tensor("ztm_d", [T, 4096], BF16, kind="Internal").ap()
    if "l1a" in blocks:
        w["wq"] = nc.dram_tensor("L1_attn_wq", [D, D], F32, kind="ExternalInput").ap()
        w["wk"] = nc.dram_tensor("L1_attn_wk", [D, 512], F32, kind="ExternalInput").ap()
        w["wv"] = nc.dram_tensor("L1_attn_wv", [D, 512], F32, kind="ExternalInput").ap()
        w["wo"] = nc.dram_tensor("L1_attn_wo", [D, D], F32, kind="ExternalInput").ap()
        w["sink"] = nc.dram_tensor("L1_attn_sink", [1, 16, 1, 1], F32, kind="ExternalInput").ap()
        blk_q = nc.dram_tensor("blk_q", [16, P, D], BF16, kind="Internal").ap()
        blk_k = nc.dram_tensor("blk_k", [4, P, D], BF16, kind="Internal").ap()
        blk_v = nc.dram_tensor("blk_v", [4, P, D], BF16, kind="Internal").ap()
        blk_o = nc.dram_tensor("blk_o", [16, P, D], BF16, kind="Internal").ap()
    if "l0f" in blocks:
        w["fw"] = nc.dram_tensor("L0_ffn_weight", [D, D], F32, kind="ExternalInput").ap()
        w["fw_s"] = nc.dram_tensor("L0_ffn_scale", [D], F32, kind="ExternalInput").ap()
    if "l1f" in blocks:
        w["up"] = nc.dram_tensor("L1_ffn_up_weight", [8192, D], F32, kind="ExternalInput").ap()
        w["dn"] = nc.dram_tensor("L1_ffn_down_weight", [D, 8192], F32, kind="ExternalInput").ap()
        w["us"] = nc.dram_tensor("L1_ffn_up_scale", [8192], F32, kind="ExternalInput").ap()
        w["ds"] = nc.dram_tensor("L1_ffn_down_scale", [D], F32, kind="ExternalInput").ap()
        w["os"] = nc.dram_tensor("L1_ffn_output_scale", [1], F32, kind="ExternalInput").ap()
        pc_up = nc.dram_tensor("pc_up", [8192, D], BF16, kind="Internal").ap()
        pc_dn = nc.dram_tensor("pc_dn", [D, 8192], BF16, kind="Internal").ap()
        blk_up = nc.dram_tensor("blk_up", [64, P, D], BF16, kind="Internal").ap()
        blk_dn = nc.dram_tensor("blk_dn", [16, P, 8192], BF16, kind="Internal").ap()

    ctx = ExitStack()
    with ctx:
        sb = lambda name, shape, dt: ctx.enter_context(nc.sbuf_tensor(name, shape, dt))

        res = [sb(f"res{f}", [P, T], BF16) for f in range(DF)]
        ident = sb("ident", [P, P], BF16)
        identf = sb("identf", [64, 64], F32)
        ones_col = sb("ones_col", [P, 1], BF16)
        ones_f128 = sb("ones_f128", [P, P], F32)
        ones_rowf = ones_f128[0:1, :]
        ones_rowb = sb("ones_rowb", [1, P], BF16)


        sqs = [sb(f"sqs{i}", [P, 512], BF16) for i in range(2)]
        m2rows = sb("m2rows", [64, 1024], F32)
        lnrow = m2rows[0:1, :]
        lnact = m2rows[32:33, :]
        bcfm = sb("bcfm", [P, 4096], BF16)
        lnab = bcfm[0:1, 0:1024]
        lnA = sb("lnA", [P, 512], BF16)
        lnB = sb("lnB", [P, 512], BF16)
        tpool = [sb(f"tpool{i}", [P, 512], BF16) for i in range(2)]
        ceps = sb("ceps", [1, 1], F32)
        cepsP = sb("cepsP", [P, 1], F32)

        xqt = sb("xqt", [P, DF * 512], BF16)
        xq = [xqt[:, f * 512:(f + 1) * 512] for f in range(DF)]
        Hbuf = sb("Hbuf", [P, 64 * 512], BF16)
        wslab = [sb(f"wslab{i}", [P, D], BF16) for i in range(2)]
        dnslab = [sb(f"dnslab{i}", [P, 8192], BF16) for i in range(2)]
        usf = sb("usf", [P, 64], F32)
        dsosf = sb("dsosf", [P, 16], F32)
        usr = sb("usr", [64, P], F32)
        dsr = sb("dsr", [16, P], F32)
        osr = sb("osr", [1, 1], F32)
        osb = sb("osb", [P, 1], F32)
        fstat = sb("fstat", [P, 24], F32)
        xtm = wslab
        ytm = [dnslab[0][:, 0:D]] * 2

        class _SameBlk:
            def __init__(self, b):
                self.b = b
            def __enter__(self):
                return self.b
            def __exit__(self, *a):
                return False
        CURBLK = [None]

        def BLKC():
            return nc.Block() if CURBLK[0] is None else _SameBlk(CURBLK[0])
        if "l0a" in blocks:
            Hbf = Hbuf[:, 24576:28672]
            m2t = wslab[0][:, 0:384].bitcast(F32)
            aux = wslab[0][:, 384:640].bitcast(F32)
            gt_sb = wslab[0][:, 640:1152]
            bds_sb = wslab[0][:, 1152:1280]
            yc_sb = wslab[0][:, 1280:1536]
            dtc = wslab[0][0:64, 1536:2048].bitcast(F32)
            cwf = sb("cwf", [P, 136], F32)
            cbf = sb("cbf", [P, 34], F32)
            cwr = dnslab[1][0:34, 0:1024].bitcast(F32)
            cwr2 = dnslab[1][0:34, 2048:3072].bitcast(F32)
            cbr = dnslab[1][0:34, 1024:1280].bitcast(F32)
            nwr = dnslab[1][0:32, 1280:1536].bitcast(F32)
            nwf = sb("nwf", [P, 32], F32)
            dtval = xqt[0:64, 4096:8192].bitcast(F32)
            dr64 = dnslab[1][0:1, 1536:1664].bitcast(F32)
            blk_dt_sb = sb("blk_dt_sb", [P, 1024], BF16)
            cepsF = sb("cepsF", [P, 1], F32)
            c1f = sb("c1f", [P, 1], F32)
        if "l1a" in blocks:
            cosf = xqt[0:64, 0:4096].bitcast(F32)
            sinf = xqt[0:64, 4096:8192].bitcast(F32)
            cosf2 = xqt[64:128, 0:4096].bitcast(F32)
            sinf2 = xqt[64:128, 4096:8192].bitcast(F32)
            mask3 = bcfm[:, 1024:1792]
            sk_row = sb("sk_row", [1, 16], F32)
            sinkb = sb("sinkb", [P, 16], F32)

        pf = ctx.enter_context(nc.psum_tensor("pf", [P, 6 * 512], F32))
        pbank = lambda i: pf[:, i * 512:(i + 1) * 512]
        ptb = [ctx.enter_context(nc.psum_tensor(f"ptb{i}", [P, 1024], BF16))
               for i in range(2)]

        sd = SC(ctx, nc, "sd")
        sg = SC(ctx, nc, "sg")
        sga = [SC(ctx, nc, f"sga{i}") for i in range(2)]
        sta = [SC(ctx, nc, f"sta{i}") for i in range(2)]
        sha = [SC(ctx, nc, f"sha{i}") for i in range(2)]
        sgb = [SC(ctx, nc, f"sgb{i}") for i in range(2)]
        sp = SC(ctx, nc, "sp")
        sv = SC(ctx, nc, "sv")
        ss = SC(ctx, nc, "ss")
        sq = SC(ctx, nc, "sq")
        st = SC(ctx, nc, "st")

        q_c = sq.reserve(4)
        with nc.Block() as blk:
            @blk.gpsimd
            def _(g):
                g.memset(ident[:, :], 1.0).then_inc(sq.h, 1)
                g.memset(identf[:, :], 1.0).then_inc(sq.h, 1)
                g.wait_ge(sq.h, q_c + 2)
                g.affine_select(ident[:, :], ident[:, :], pattern=[[1, P]],
                                compare_op=ALU.is_equal, fill=0.0, base=0,
                                channel_multiplier=-1).then_inc(sq.h, 1)
                g.affine_select(identf[:, :], identf[:, :], pattern=[[1, 64]],
                                compare_op=ALU.is_equal, fill=0.0, base=0,
                                channel_multiplier=-1).then_inc(sq.h, 1)
                g.memset(ones_col[:, :], 1.0)
                g.memset(ones_f128[:, :], 1.0)
                g.memset(ones_rowb[:, :], 1.0)
                g.memset(cepsP[:, :], 1e-38)
                (g.memset(c1f[:, :], 1.0),
                 g.memset(cepsF[:, :], 1e-5)) if "l0a" in blocks else None
                g.memset(ceps[:, :], 1e-5).then_inc(sq.h, 1)
        CONST_Q = sq.reserve(1) + 1

        g0 = [sga[0].reserve(8 * 16), sga[1].reserve(8 * 16)]
        p0 = sp.reserve(16 * 4)
        v0 = sv.reserve(16 * 4)
        with nc.Block() as blk:
            @blk.gpsimd
            def _(g):
                for tc in range(16):
                    if tc >= 2:
                        g.wait_ge(sp.h, p0 + (tc - 1) * 4)
                    g.dma_start(out=xtm[tc % 2][:, :],
                                in_=x_h[tc * P:(tc + 1) * P, :]).then_inc(sga[tc % 2].h, 16)

            @blk.tensor
            def _(t):
                t.wait_ge(sq.h, CONST_Q)
                for tc in range(16):
                    t.wait_ge(sga[tc % 2].h, g0[tc % 2] + (tc // 2 + 1) * 16)
                    for b in range(4):
                        if tc * 4 + b >= 2:
                            t.wait_ge(sv.h, v0 + tc * 4 + b - 1)
                        half = (tc * 4 + b) % 2
                        last = None
                        for j in range(4):
                            f = b * 4 + j
                            last = t.transpose(
                                ptb[half][:, j * 128:(j + 1) * 128],
                                xtm[tc % 2][:, f * P:(f + 1) * P], ident[:, :])
                        last.then_inc(sp.h, 1)

            @blk.vector
            def _(v):
                for tc in range(16):
                    for b in range(4):
                        v.wait_ge(sp.h, p0 + tc * 4 + b + 1)
                        half = (tc * 4 + b) % 2
                        for j in range(4):
                            f = b * 4 + j
                            ins = v.tensor_copy(
                                out=res[f][:, tc * P:(tc + 1) * P],
                                in_=ptb[half][:, j * 128:(j + 1) * 128])
                        ins.then_inc(sv.h, 1)


        def emit_ln(tok0, W, xdst):
            invD = 1.0 / D
            p_guard = sp.n
            v_guard = sv.n
            s_guard = ss.n
            q_guard = sq.n
            g_guard = sg.n
            p_b = sp.reserve(18)
            s_b = ss.reserve(18)
            v_b = sv.reserve(25)
            q_b = sq.reserve(16)
            mrow = lnrow[0:1, 0:W]
            nrstd = lnrow[0:1, 512:512 + W]
            msqrow = lnact[0:1, 512:512 + W]
            trow = lnact[0:1, 0:W]
            varrow = lnact[0:1, 0:W]
            lnv = lnact[0:1, 0:W]
            rstd = lnact[0:1, 0:W]
            arow = lnab[0:1, 0:W]
            brow = lnab[0:1, 512:512 + W]
            with BLKC() as blk:
                @blk.scalar
                def _(sc):
                    for f in range(DF):
                        if f >= 2:
                            sc.wait_ge(sp.h, p_b + f)
                        elif f == 0:
                            sc.wait_ge(sp.h, p_guard)
                        sc.activation(out=sqs[f % 2][:, 0:W],
                                      in_=res[f][:, tok0:tok0 + W],
                                      func=AF.Square).then_inc(ss.h, 1)
                    sc.wait_ge(sv.h, v_b + 4)
                    sc.activation(out=lnv, in_=varrow, func=AF.Ln,
                                  bias=ceps[0:1, 0:1]).then_inc(ss.h, 1)
                    sc.wait_ge(ss.h, s_b + 17)
                    sc.activation(out=rstd, in_=lnv, func=AF.Exp,
                                  scale=-0.5).then_inc(ss.h, 1)

                @blk.tensor
                def _(t):
                    t.wait_ge(sv.h, v_guard)
                    t.wait_ge(sq.h, CONST_Q)
                    for f in range(DF):
                        ins = t.matmul(pbank(4)[0:1, 0:W], ones_col[:, :],
                                       res[f][:, tok0:tok0 + W],
                                       start=(f == 0), stop=(f == DF - 1))
                    ins.then_inc(sp.h, 1)
                    for f in range(DF):
                        t.wait_ge(ss.h, s_b + f + 1)
                        t.matmul(pbank(5)[0:1, 0:W], ones_col[:, :],
                                 sqs[f % 2][:, 0:W],
                                 start=(f == 0), stop=(f == DF - 1)).then_inc(sp.h, 1)
                    t.wait_ge(sv.h, v_b + 7)
                    t.matmul(pbank(4)[:, 0:W], ones_rowb[0:1, :], arow,
                             start=True, stop=True)
                    t.matmul(pbank(5)[:, 0:W], ones_rowb[0:1, :], brow,
                             start=True, stop=True).then_inc(sp.h, 1)

                @blk.vector
                def _(v):
                    v.wait_ge(sp.h, p_b + 1)
                    v.tensor_scalar(out=mrow, in0=pbank(4)[0:1, 0:W], scalar1=invD,
                                    scalar2=None, op0=ALU.mult).then_inc(sv.h, 1)
                    v.wait_ge(sp.h, p_b + 17)
                    v.tensor_scalar(out=msqrow, in0=pbank(5)[0:1, 0:W], scalar1=invD,
                                    scalar2=None, op0=ALU.mult).then_inc(sv.h, 1)
                    v.wait_ge(sv.h, v_b + 2)
                    v.tensor_tensor(out=trow, in0=mrow, in1=mrow,
                                    op=ALU.mult).then_inc(sv.h, 1)
                    v.wait_ge(sv.h, v_b + 3)
                    v.tensor_tensor(out=varrow, in0=msqrow, in1=trow,
                                    op=ALU.subtract).then_inc(sv.h, 1)
                    v.wait_ge(ss.h, s_b + 18)
                    v.tensor_copy(out=arow, in_=rstd).then_inc(sv.h, 1)
                    v.tensor_scalar(out=nrstd, in0=rstd, scalar1=-1.0, scalar2=None,
                                    op0=ALU.mult).then_inc(sv.h, 1)
                    v.wait_ge(sv.h, v_b + 6)
                    v.tensor_tensor(out=brow, in0=mrow, in1=nrstd,
                                    op=ALU.mult).then_inc(sv.h, 1)
                    v.wait_ge(sp.h, p_b + 18)
                    v.tensor_copy(out=lnA[:, 0:W], in_=pbank(4)[:, 0:W]).then_inc(sv.h, 1)
                    v.tensor_copy(out=lnB[:, 0:W], in_=pbank(5)[:, 0:W]).then_inc(sv.h, 1)
                    v.wait_ge(sp.h, p_guard)
                    v.wait_ge(sg.h, g_guard)
                    for f in range(DF):
                        v.wait_ge(sq.h, q_b + f + 1)
                        v.tensor_tensor(out=xdst[f], in0=tpool[f % 2][:, 0:W],
                                        in1=lnB[:, 0:W], op=ALU.add).then_inc(sv.h, 1)

                @blk.gpsimd
                def _(g):
                    g.wait_ge(sv.h, v_b + 9)
                    for f in range(DF):
                        if f >= 2:
                            g.wait_ge(sv.h, v_b + 8 + f)
                        g.tensor_tensor(out=tpool[f % 2][:, 0:W],
                                        in0=res[f][:, tok0:tok0 + W],
                                        in1=lnA[:, 0:W], op=ALU.mult).then_inc(sq.h, 1)
            return sv.n

        def emit_pc(w_ap, pc_ap, nslabs, width, sign=True, total_rows=None):
            total_rows = total_rows if total_rows is not None else nslabs * P
            p_gd = sp.n
            q_gd = sq.n
            ga = [sga[0].reserve(((nslabs + 1) // 2) * 16),
                  sga[1].reserve((nslabs // 2) * 16)]
            ta_ = [sta[0].reserve(((nslabs + 1) // 2) * 16),
                   sta[1].reserve((nslabs // 2) * 16)]
            s_b = ss.reserve(nslabs if sign else 0)
            with BLKC() as blk:
                @blk.gpsimd
                def _(g):
                    g.wait_ge(sp.h, p_gd)
                    g.wait_ge(sq.h, q_gd)
                    for i in range(nslabs):
                        rows = min(P, total_rows - i * P)
                        if i >= 2:
                            g.wait_ge(sta[i % 2].h, ta_[i % 2] + (i // 2) * 16)
                        else:
                            g.wait_ge(sta[i % 2].h, ta_[i % 2])
                        g.dma_start(out=dnslab[i % 2][0:rows, 0:width],
                                    in_=w_ap[i * P:i * P + rows, :]).then_inc(sga[i % 2].h, 16)

                if sign:
                    @blk.scalar
                    def _(sc):
                        for i in range(nslabs):
                            rows = min(P, total_rows - i * P)
                            sc.wait_ge(sga[i % 2].h, ga[i % 2] + (i // 2 + 1) * 16)
                            sc.sign(out=dnslab[i % 2][0:rows, 0:width],
                                    in_=dnslab[i % 2][0:rows, 0:width]).then_inc(ss.h, 1)

                @blk.sync
                def _(s):
                    for i in range(nslabs):
                        rows = min(P, total_rows - i * P)
                        if sign:
                            s.wait_ge(ss.h, s_b + i + 1)
                        else:
                            s.wait_ge(sga[i % 2].h, ga[i % 2] + (i // 2 + 1) * 16)
                        s.dma_start(out=pc_ap[i * P:i * P + rows, :],
                                    in_=dnslab[i % 2][0:rows, 0:width]).then_inc(sta[i % 2].h, 16)
            return [sta[0].n, sta[1].n]


        if "l0a" in blocks:
            CH = 256
            NCH = T // CH
            xafull = [Hbuf[:, f * D:(f + 1) * D] for f in range(DF)]
            zst = [xqt[:, 0:2048], xqt[:, 2048:4096]]
            sil_out = dnslab[0][:, 4096:6144]
            acc_f = dnslab[0][:, 0:4096].bitcast(F32)
            xraw = dnslab[1][:, 0:4104].bitcast(F32)
            sp_tmp = xqt[0:64, 0:4096].bitcast(F32)

            pcip_done = emit_pc(w["ip"], pc_ip, 67, D, sign=False, total_rows=8512)
            p_gd = sp.n
            g_gd = sg.n
            trl = sd.reserve(16 * 16)
            sct2 = sg.reserve(16 * 16)
            v_bi = sv.reserve(16)
            with nc.Block() as blk:
                @blk.sync
                def _(s):
                    s.wait_ge(sta[0].h, sta[0].n)
                    s.wait_ge(sta[1].h, sta[1].n)
                    s.wait_ge(sg.h, g_gd)
                    s.wait_ge(sp.h, p_gd)
                    for k in range(16):
                        if k >= 1:
                            s.wait_ge(sg.h, sct2 + k * 16)
                            s.wait_ge(sv.h, v_bi + k)
                        s.dma_start_transpose(Hbuf[:, 0:8512],
                                              pc_ip[:, k * P:(k + 1) * P]).then_inc(sd.h, 16)

                @blk.gpsimd
                def _(g):
                    for k in range(16):
                        g.wait_ge(sd.h, trl + (k + 1) * 16)
                        g.dma_start(
                            out=blk_ip[:, :, k * P:(k + 1) * P].rearrange("o p m -> p o m"),
                            in_=Hbuf[:, 0:8448].rearrange("p (o m) -> p o m", o=66),
                        ).then_inc(sg.h, 16)

                @blk.vector
                def _(v):
                    for k in range(16):
                        v.wait_ge(sd.h, trl + (k + 1) * 16)
                        v.tensor_copy(out=blk_dt_sb[:, k * 64:(k + 1) * 64],
                                      in_=Hbuf[:, 8448:8512]).then_inc(sv.h, 1)

            pcop_done = emit_pc(w["op"], pc_op, 16, 4096, sign=False)
            d_nw = sd.reserve(16)
            p_nw = sp.reserve(1)
            v_nw = sv.reserve(1)
            NW_TA = [sta[0].n, sta[1].n]
            with nc.Block() as blk:
                @blk.sync
                def _(s):
                    s.wait_ge(sta[0].h, NW_TA[0])
                    s.wait_ge(sta[1].h, NW_TA[1])
                    s.dma_start(out=nwr[:, :],
                                in_=w["nw"].rearrange("(r c) -> r c", r=32)).then_inc(sd.h, 16)

                @blk.tensor
                def _(t):
                    t.wait_ge(sd.h, d_nw + 16)
                    t.wait_ge(sq.h, CONST_Q)
                    t.transpose(pbank(5)[:, 0:32], nwr[:, :],
                                identf[0:32, 0:32]).then_inc(sp.h, 1)

                @blk.vector
                def _(v):
                    v.wait_ge(sp.h, p_nw + 1)
                    v.tensor_copy(out=nwf[:, :], in_=pbank(5)[:, 0:32]).then_inc(sv.h, 1)
            g_gd2 = sg.n
            trl2 = sd.reserve(32 * 16)
            sct3 = sg.reserve(32 * 16)
            v_bo = sv.reserve(32)
            with nc.Block() as blk:
                @blk.sync
                def _(s):
                    s.wait_ge(sta[0].h, sta[0].n)
                    s.wait_ge(sta[1].h, sta[1].n)
                    s.wait_ge(sg.h, g_gd2)
                    for di in range(32):
                        if di >= 1:
                            s.wait_ge(sg.h, sct3 + di * 16)
                        s.dma_start_transpose(Hbuf[:, 0:2048],
                                              pc_op[:, di * P:(di + 1) * P]).then_inc(sd.h, 16)

                @blk.vector
                def _(v):
                    v.wait_ge(sv.h, v_nw + 1)
                    for di in range(32):
                        v.wait_ge(sd.h, trl2 + (di + 1) * 16)
                        v.tensor_scalar(out=Hbuf[:, 2048:4096], in0=Hbuf[:, 0:2048],
                                        scalar1=nwf[:, di:di + 1], scalar2=None,
                                        op0=ALU.mult).then_inc(sv.h, 1)

                @blk.gpsimd
                def _(g):
                    for di in range(32):
                        g.wait_ge(sv.h, v_bo + di + 1)
                        g.dma_start(
                            out=blk_op2[:, :, di * P:(di + 1) * P].rearrange("o p m -> p o m"),
                            in_=Hbuf[:, 2048:4096].rearrange("p (o m) -> p o m", o=16),
                        ).then_inc(sg.h, 16)

            d_cv = sd.reserve(4 * 16)
            p_cv = sp.reserve(1)
            v_cv = sv.reserve(8)
            s_cv = ss.reserve(1)
            CV_TA = [sta[0].n, sta[1].n]
            with nc.Block() as blk:
                @blk.sync
                def _(s):
                    s.wait_ge(sta[0].h, CV_TA[0])
                    s.wait_ge(sta[1].h, CV_TA[1])
                    s.dma_start(out=cwr[:, :],
                                in_=w["cw"].rearrange("(f p) j -> f (p j)", f=34)).then_inc(sd.h, 16)
                    s.dma_start(out=cbr[:, :],
                                in_=w["cb"].rearrange("(f p) -> f p", f=34)).then_inc(sd.h, 16)
                    s.dma_start(out=fstat[0:64, 0:1],
                                in_=w["dtb"].rearrange("(a b) -> a b", a=64)).then_inc(sd.h, 16)
                    s.dma_start(out=fstat[0:64, 1:2],
                                in_=w["alog"].rearrange("(a b) -> a b", a=64)).then_inc(sd.h, 16)


                @blk.tensor
                def _(t):
                    t.wait_ge(sv.h, v_cv + 1)
                    t.wait_ge(sd.h, d_cv + 64)
                    t.wait_ge(sq.h, CONST_Q)
                    for j in range(4):
                        t.transpose(pbank(5)[:, j * 34:(j + 1) * 34],
                                    cwr2[0:34, j * 128:(j + 1) * 128],
                                    identf[0:34, 0:34])
                    t.transpose(pbank(5)[:, 136:170], cbr[:, :],
                                identf[0:34, 0:34]).then_inc(sp.h, 1)

                @blk.vector
                def _(v):
                    v.wait_ge(sd.h, d_cv + 64)
                    for j in range(4):
                        ins = v.tensor_copy(out=cwr2[0:34, j * 128:(j + 1) * 128],
                                            in_=cwr[:, j:512:4])
                    ins.then_inc(sv.h, 1)
                    v.wait_ge(sp.h, p_cv + 1)
                    for j in range(4):
                        v.tensor_copy(out=cwf[:, j * 34:(j + 1) * 34],
                                      in_=pbank(5)[:, j * 34:(j + 1) * 34]).then_inc(sv.h, 1)
                    v.tensor_copy(out=cbf[:, :], in_=pbank(5)[:, 136:170]).then_inc(sv.h, 1)
                    v.wait_ge(ss.h, s_cv + 1)
                    v.tensor_scalar(out=fstat[0:64, 2:3], in0=fstat[0:64, 3:4],
                                    scalar1=-1.0, scalar2=None,
                                    op0=ALU.mult).then_inc(sv.h, 1)
                    v.tensor_scalar(out=fstat[0:64, 4:5], in0=fstat[0:64, 3:4],
                                    scalar1=1.0, scalar2=None,
                                    op0=ALU.mult).then_inc(sv.h, 1)

                @blk.scalar
                def _(sc):
                    sc.wait_ge(sd.h, d_cv + 64)
                    sc.activation(out=fstat[0:64, 3:4], in_=fstat[0:64, 1:2],
                                  func=AF.Exp).then_inc(ss.h, 1)

            for sl in range(4):
                ln_done = emit_ln(sl * 512, 512,
                                  [xafull[f][:, sl * 512:(sl + 1) * 512] for f in range(DF)])
            LN1_V = ln_done

            ZNF = 32
            SKIP_M1 = L0A_PH < 2
            ip_guard_v = sv.n
            ip_guard_q = sq.n
            SCT3_END = sg.n
            hi_ = [sha[0].reserve(33 * 16), sha[1].reserve(33 * 16)]
            p_m1 = sp.reserve(64 * 3 + 2)

            def pfn(fs):
                return p_m1 + (3 * fs if fs <= 64 else 192 + (fs - 64))
            s_m1 = ss.reserve(66 * 5)
            v_m1 = sv.reserve(64 * 2)
            q_m1 = sq.reserve(34 * 5)
            g_m1 = [sgb[0].reserve(32 * 16), sgb[1].reserve(32 * 16)]
            with nc.Block() as blk:
                @blk.sync
                def _(s):
                    for fs in range(66):
                        if fs >= 2:
                            s.wait_ge(sp.h, pfn(fs - 1))
                        else:
                            s.wait_ge(sg.h, SCT3_END)
                        s.dma_start(out=wslab[fs % 2][:, :],
                                    in_=blk_ip[fs, :, :]).then_inc(sha[fs % 2].h, 16)

                @blk.tensor
                def _(t):
                    t.wait_ge(sv.h, LN1_V)
                    for fs in range(66):
                        is_bc = fs >= 64
                        t.wait_ge(sha[fs % 2].h, hi_[fs % 2] + (fs // 2 + 1) * 16)
                        if fs >= 1:
                            t.wait_ge(ss.h, s_m1 + fs * 5)
                        for b in range(4):
                            for k in range(DF):
                                ins = t.matmul(pbank(b)[:, 0:512],
                                               wslab[fs % 2][:, k * P:(k + 1) * P],
                                               xafull[k][:, b * 512:(b + 1) * 512],
                                               start=(k == 0), stop=(k == DF - 1))
                        ins.then_inc(sp.h, 1)
                        if not is_bc:
                            for bt in range(2):
                                t.wait_ge(ss.h, s_m1 + fs * 5 + 5)
                                t.wait_ge(sv.h, v_m1 + fs * 2)
                                for j in range(8):
                                    ins = t.transpose(
                                        ptb[bt][:, j * 128:(j + 1) * 128],
                                        sil_out[:, (bt * 8 + j) * 128:(bt * 8 + j + 1) * 128],
                                        ident[:, :])
                                ins.then_inc(sp.h, 1)

                @blk.scalar
                def _(sc):
                    for fs in range(66):
                        is_z = fs < ZNF
                        is_bc = fs >= 64
                        ci = fs - ZNF
                        sc.wait_ge(sp.h, pfn(fs) + 1)
                        if fs >= 1 and not is_bc:
                            sc.wait_ge(sp.h, pfn(fs - 1) + (3 if fs - 1 < 64 else 1))
                        if is_z:
                            for b in range(4):
                                ins = sc.activation(
                                    out=sil_out[:, b * 512:(b + 1) * 512],
                                    in_=pbank(b)[:, 0:512], func=AF.Silu)
                                ins.then_inc(ss.h, 2 if b == 3 else 1)
                        else:
                            if ci >= 1:
                                sc.wait_ge(sq.h, q_m1 + ci * 5)
                            for b in range(4):
                                sc.copy(out=xraw[:, 3 + b * 512:3 + (b + 1) * 512],
                                        in_=pbank(b)[:, 0:512]).then_inc(ss.h, 1)
                            sc.wait_ge(sq.h, q_m1 + ci * 5 + 5)
                            dst = (bcfm[:, (fs - 64) * 2048:(fs - 63) * 2048]
                                   if is_bc else sil_out)
                            sc.activation(out=dst, in_=acc_f[:, 0:2048], func=AF.Silu,
                                          bias=cbf[:, ci:ci + 1]).then_inc(ss.h, 1)

                @blk.vector
                def _(v):
                    v.wait_ge(sv.h, ip_guard_v)
                    for fs in range(66):
                        if fs >= ZNF:
                            ci = fs - ZNF
                            v.wait_ge(ss.h, s_m1 + fs * 5 + 4)
                            v.memset(xraw[:, 0:3], 0.0).then_inc(sq.h, 1)
                            v.wait_ge(sq.h, q_m1 + ci * 5 + 1)
                            v.tensor_scalar(out=acc_f[:, 0:2048], in0=xraw[:, 0:2048],
                                            scalar1=cwf[:, ci:ci + 1],
                                            scalar2=None, op0=ALU.mult).then_inc(sq.h, 1)
                            for j in range(1, 4):
                                v.wait_ge(sq.h, q_m1 + ci * 5 + 1 + j)
                                v.scalar_tensor_tensor(
                                    out=acc_f[:, 0:2048], in0=xraw[:, j:j + 2048],
                                    scalar=cwf[:, j * 34 + ci:j * 34 + ci + 1],
                                    in1=acc_f[:, 0:2048], op0=ALU.mult,
                                    op1=ALU.add).then_inc(sq.h, 1)
                        if fs >= 64:
                            continue
                        for bt in range(2):
                            v.wait_ge(sp.h, pfn(fs) + 2 + bt)
                            if bt == 0 and fs >= 2:
                                pv = fs % 2
                                v.wait_ge(sgb[pv].h, g_m1[pv] + (fs // 2) * 16)
                            v.tensor_copy(
                                out=zst[fs % 2][:, bt * 1024:(bt + 1) * 1024]
                                    .rearrange("p (a m) -> p a m", a=8),
                                in_=ptb[bt].rearrange("p (a m) -> p a m", a=8),
                            ).then_inc(sv.h, 1)

                @blk.gpsimd
                def _(g):
                    g.wait_ge(sq.h, ip_guard_q)
                    for fs in range(66):
                        if fs < 64:
                            g.wait_ge(sv.h, v_m1 + fs * 2 + 2)
                            dram = ztm_d if fs < ZNF else xtm_d
                            col = fs * 128 if fs < ZNF else (fs - ZNF) * 128
                            g.dma_start(
                                out=dram[:, col:col + 128].rearrange("(tc p) m -> p tc m", p=P),
                                in_=zst[fs % 2].rearrange("p (tc m) -> p tc m", m=P),
                            ).then_inc(sgb[fs % 2].h, 16)

            p_dt = sp.reserve(2)
            s_dt = ss.reserve(3)
            v_dt = sv.reserve(7)
            d_dt = sd.reserve(32)
            M1_SGB = [sgb[0].n, sgb[1].n]
            with nc.Block() as blk:
                @blk.tensor
                def _(t):
                    t.wait_ge(ss.h, s_m1 + 66 * 5)
                    for b in range(4):
                        for k in range(DF):
                            ins = t.matmul(pbank(b)[0:64, 0:512],
                                           blk_dt_sb[:, k * 64:(k + 1) * 64],
                                           xafull[k][:, b * 512:(b + 1) * 512],
                                           start=(k == 0), stop=(k == DF - 1))
                    ins.then_inc(sp.h, 1)
                    t.wait_ge(sd.h, d_dt + 16)
                    t.matmul(pbank(5)[:, 0:64], ones_rowf[0:1, :],
                             m2rows[0:1, 960:1024], start=True,
                             stop=True).then_inc(sp.h, 1)

                @blk.vector
                def _(v):
                    v.wait_ge(sgb[0].h, M1_SGB[0])
                    v.wait_ge(sgb[1].h, M1_SGB[1])
                    v.wait_ge(sv.h, v_m1 + 64 * 2)
                    v.wait_ge(sp.h, p_dt + 1)
                    for b in range(4):
                        v.tensor_scalar(out=dtval[:, b * 512:(b + 1) * 512],
                                        in0=pbank(b)[0:64, 0:512],
                                        scalar1=fstat[0:64, 0:1], scalar2=None,
                                        op0=ALU.add).then_inc(sv.h, 1)
                    v.wait_ge(ss.h, s_dt + 1)
                    v.tensor_scalar(out=dtval[:, :], in0=dtval[:, :], scalar1=0.0,
                                    scalar2=None, op0=ALU.max).then_inc(sv.h, 1)
                    v.wait_ge(ss.h, s_dt + 3)
                    v.wait_ge(sv.h, v_dt + 5)
                    v.tensor_tensor(out=dtval[:, :], in0=dtval[:, :], in1=sp_tmp[:, 0:T],
                                    op=ALU.add).then_inc(sv.h, 1)
                    v.wait_ge(sp.h, p_dt + 2)
                    v.tensor_copy(out=aux[:, 0:64],
                                  in_=pbank(5)[:, 0:64]).then_inc(sv.h, 1)

                @blk.scalar
                def _(sc):
                    sc.wait_ge(sv.h, v_dt + 4)
                    sc.activation(out=sp_tmp[:, 0:T], in_=dtval[:, :],
                                  func=AF.Abs).then_inc(ss.h, 1)
                    sc.wait_ge(ss.h, s_dt + 1)
                    sc.activation(out=sp_tmp[:, 0:T], in_=sp_tmp[:, 0:T],
                                  func=AF.Exp, scale=-1.0).then_inc(ss.h, 1)
                    sc.wait_ge(ss.h, s_dt + 2)
                    sc.activation(out=sp_tmp[:, 0:T], in_=sp_tmp[:, 0:T],
                                  func=AF.Ln, bias=c1f[0:64, 0:1]).then_inc(ss.h, 1)

                @blk.sync
                def _(s):
                    s.dma_start(out=m2rows[0:1, 960:1024],
                                in_=w["dd"].rearrange("(a b) -> a b", a=1)).then_inc(sd.h, 16)
                    s.wait_ge(sd.h, d_dt + 16)
                    s.wait_ge(sv.h, v_dt + 6)
                    s.dma_start(out=dt_d[:, :], in_=dtval[:, :]).then_inc(sd.h, 16)

            x2 = [Hbuf[:, 0:4096], Hbuf[:, 4096:8192]]
            z2 = [Hbuf[:, 8192:12288], Hbuf[:, 12288:16384]]
            Xdt = [Hbuf[:, 16384:20480], Hbuf[:, 20480:24576]]
            y2 = Xdt
            ynfm = [xqt[:, di * 256:(di + 1) * 256] for di in range(32)]
            stage0 = dnslab[1].bitcast(F32)
            stage1 = dnslab[0].bitcast(F32)
            opsl = [dnslab[0][:, 0:4096], dnslab[0][:, 4096:8192]]
            dA_r = m2rows[:, 768:1024]
            cs_r = m2rows[:, 0:256]
            ncs_r = m2rows[:, 256:512]
            ds_r = m2rows[:, 512:768]
            lnBf = lnB.bitcast(F32)
            B_fm = bcfm[:, 0:2048]
            C_fm = bcfm[:, 2048:4096]

            HB_P = sp.n
            q_hz = sq.reserve(1)
            with nc.Block() as blk:
                @blk.gpsimd
                def _(g):
                    g.wait_ge(sp.h, HB_P)
                    g.memset(Hbf[:, :], 0.0).then_inc(sq.h, 1)
            HZERO_Q = q_hz + 1

            def st_row(h):
                g = h // 16
                if g < 3:
                    return stage0[32 * g:32 * g + 1, :], ones_f128[32 * g:32 * g + 1, :]
                return stage1[0:1, :], ones_f128[0:1, :]

            MBLK = nc.Block()
            MBLK.__enter__()
            PREV_GV = None
            PREV_GP = None
            for cc in range(NCH):
                t0c = cc * CH
                PRV = sv.n
                PRS = ss.n
                PRP = sp.n
                PRQ = sq.n
                hx = [sha[0].reserve(16), sha[1].reserve(16)]
                d_c2 = sd.reserve(16)
                p_c2 = sp.reserve(4)
                s_c2 = ss.reserve(3)
                v_c2 = sv.reserve(16)
                q_c2 = sq.reserve(2)
                g_st = sg.reserve(5 * 16)
                with _SameBlk(MBLK) as blk:
                    @blk.sync
                    def _(s):
                        s.wait_ge(sv.h, PRV if PREV_GV is None else PREV_GV)
                        s.wait_ge(sp.h, PRP if PREV_GP is None else PREV_GP)
                        s.wait_ge(ss.h, PRS)
                        s.wait_ge(sq.h, PRQ)
                        s.wait_ge(sha[0].h, hx[0])
                        s.wait_ge(sha[1].h, hx[1])
                        s.dma_start(out=Hbuf[:, 0:8192]
                                    .rearrange("p (ct m) -> p ct m", m=4096),
                                    in_=xtm_d[t0c:t0c + CH, :]
                                    .rearrange("(ct p) m -> p ct m", p=P)
                                    ).then_inc(sha[0].h, 16)
                        s.dma_start(out=Hbuf[:, 8192:16384]
                                    .rearrange("p (ct m) -> p ct m", m=4096),
                                    in_=ztm_d[t0c:t0c + CH, :]
                                    .rearrange("(ct p) m -> p ct m", p=P)
                                    ).then_inc(sha[1].h, 16)
                        s.wait_ge(sd.h, d_c2)
                        s.dma_start(out=dtc[:, :],
                                    in_=dt_d[:, t0c:t0c + CH]).then_inc(sd.h, 16)

                    @blk.vector
                    def _(v):
                        v.wait_ge(sv.h, PRV)
                        v.wait_ge(sp.h, PRP)
                        v.wait_ge(sd.h, d_c2 + 16)
                        v.tensor_scalar(out=dA_r, in0=dtc[:, :],
                                        scalar1=fstat[0:64, 2:3], scalar2=None,
                                        op0=ALU.mult).then_inc(sv.h, 1)
                        v.wait_ge(sv.h, v_c2 + 1)
                        v.tensor_tensor_scan(out=cs_r, data0=dA_r, data1=dA_r,
                                             initial=0.0, op0=ALU.add,
                                             op1=ALU.bypass).then_inc(sv.h, 1)
                        v.wait_ge(sv.h, v_c2 + 2)
                        v.tensor_scalar(out=ncs_r, in0=cs_r, scalar1=-1.0,
                                        scalar2=None, op0=ALU.mult).then_inc(sv.h, 1)
                        v.tensor_scalar(out=ds_r, in0=cs_r,
                                        scalar1=cs_r[:, 255:256], scalar2=None,
                                        op0=ALU.subtract).then_inc(sv.h, 1)
                        v.wait_ge(sp.h, p_c2 + 1)
                        v.wait_ge(ss.h, PRS)
                        v.tensor_copy(out=m2t[:, 0:128],
                                      in_=pbank(5)[:, 0:128]).then_inc(sv.h, 1)
                        v.tensor_copy(out=lnBf[:, 0:128],
                                      in_=pbank(5)[:, 128:256]).then_inc(sv.h, 1)
                        v.tensor_copy(out=lnBf[:, 128:256],
                                      in_=pbank(5)[:, 256:384]).then_inc(sv.h, 1)
                        v.tensor_copy(out=lnA[:, 256:384],
                                      in_=pbank(5)[:, 384:512]).then_inc(sv.h, 1)
                        v.wait_ge(sp.h, p_c2 + 2)
                        v.tensor_copy(out=lnA[:, 0:256],
                                      in_=ptb[0][:, 0:256]).then_inc(sv.h, 1)
                        v.wait_ge(sha[0].h, hx[0] + 16)
                        v.wait_ge(sv.h, v_c2 + 8)
                        for ct in range(2):
                            v.tensor_tensor(
                                out=Xdt[ct].rearrange("p (h e) -> p h e", e=64),
                                in0=x2[ct].rearrange("p (h e) -> p h e", e=64),
                                in1=lnA[:, 256 + ct * 64:256 + (ct + 1) * 64]
                                    .rearrange("p (h o) -> p h o", o=1)
                                    .broadcast_to([P, 64, 64]),
                                op=ALU.mult).then_inc(sv.h, 1)
                        v.wait_ge(sp.h, p_c2 + 3)
                        v.tensor_copy(out=gt_sb[:, 0:256],
                                      in_=pbank(4)[:, 0:256]).then_inc(sv.h, 1)
                        v.tensor_copy(out=gt_sb[:, 256:512],
                                      in_=pbank(4)[:, 256:512]).then_inc(sv.h, 1)
                        v.wait_ge(sg.h, g_st + 80)
                        for g4 in range(4):
                            row, _ = st_row(g4 * 16)
                            ins = v.tensor_copy(out=m2rows[0:1, 960 + g4 * 16:976 + g4 * 16],
                                                in_=row[:, 255:4096:256])
                        ins.then_inc(sv.h, 1)
                        v.wait_ge(sv.h, v_c2 + 11)
                        for ct in range(2):
                            v.tensor_tensor(
                                out=x2[ct].rearrange("p (h e) -> p h e", e=64),
                                in0=x2[ct].rearrange("p (h e) -> p h e", e=64),
                                in1=aux[:, 0:64]
                                    .rearrange("p (h o) -> p h o", o=1)
                                    .broadcast_to([P, 64, 64]),
                                op=ALU.mult).then_inc(sv.h, 1)

                    @blk.scalar
                    def _(sc):
                        sc.wait_ge(sv.h, v_c2 + 4)
                        sc.activation(out=ds_r, in_=ds_r, func=AF.Exp,
                                      scale=-1.0).then_inc(ss.h, 1)
                        sc.wait_ge(ss.h, s_c2 + 1)
                        sc.activation(out=dA_r, in_=cs_r, func=AF.Exp).then_inc(ss.h, 1)
                        sc.wait_ge(sp.h, p_c2 + 4)
                        sc.activation(out=aux[:, 64:128], in_=pbank(0)[:, 0:64],
                                      func=AF.Exp).then_inc(ss.h, 1)

                    @blk.tensor
                    def _(t):
                        if cc == 0:
                            t.wait_ge(sq.h, HZERO_Q)
                        t.wait_ge(ss.h, s_c2 + 2)
                        t.wait_ge(sv.h, v_c2 + 4)
                        for ii, src in enumerate([ncs_r, ds_r, dA_r]):
                            for half in range(2):
                                ins = t.transpose(
                                    pbank(5)[:, ii * 128 + half * 64:
                                             ii * 128 + (half + 1) * 64],
                                    src[:, half * 128:(half + 1) * 128],
                                    identf[0:64, 0:64])
                        for half in range(2):
                            ins = t.transpose(
                                pbank(5)[:, 384 + half * 64:384 + (half + 1) * 64],
                                dtc[:, half * 128:(half + 1) * 128],
                                identf[0:64, 0:64])
                        ins.then_inc(sp.h, 1)
                        for half in range(2):
                            ins = t.transpose(ptb[0][:, half * 128:(half + 1) * 128],
                                              B_fm[:, t0c + half * 128:
                                                   t0c + (half + 1) * 128],
                                              ident[:, :])
                        ins.then_inc(sp.h, 1)
                        for sh in range(2):
                            ins = t.matmul(pbank(4)[:, sh * 256:(sh + 1) * 256],
                                           B_fm[:, t0c + sh * 128:t0c + (sh + 1) * 128],
                                           C_fm[:, t0c:t0c + CH],
                                           start=True, stop=True)
                        ins.then_inc(sp.h, 1)
                        t.wait_ge(sv.h, v_c2 + 14)
                        for g4 in range(4):
                            ins = t.matmul(pbank(0)[:, g4 * 16:(g4 + 1) * 16],
                                           ones_rowf,
                                           m2rows[0:1, 960 + g4 * 16:976 + g4 * 16],
                                           start=True, stop=True)
                        ins.then_inc(sp.h, 1)

                    @blk.gpsimd
                    def _(g):
                        g.wait_ge(sv.h, v_c2 + 2)
                        g.wait_ge(sp.h, PRP)
                        g.wait_ge(ss.h, PRS)
                        g.dma_start(out=cs_d[:, :], in_=cs_r).then_inc(sg.h, 16)
                        g.wait_ge(sg.h, g_st + 16)
                        for g4 in range(4):
                            dst = (stage0[32 * g4:32 * g4 + 1, :] if g4 < 3
                                   else stage1[0:1, :])
                            g.dma_start(out=dst,
                                        in_=cs_d[g4 * 16:(g4 + 1) * 16, :]
                                        .rearrange("h c -> (h c)")
                                        .rearrange("(a n) -> a n", a=1)
                                        ).then_inc(sg.h, 16)
                        g.wait_ge(sv.h, v_c2 + 13)
                        for sh in range(2):
                            g.affine_select(gt_sb[:, sh * 256:(sh + 1) * 256],
                                            gt_sb[:, sh * 256:(sh + 1) * 256],
                                            pattern=[[1, 256]], compare_op=ALU.is_ge,
                                            fill=0.0, base=-128 * sh,
                                            channel_multiplier=-1).then_inc(sq.h, 1)

                d8p = [[xqt[:, 0:1024], xqt[:, 1024:2048]],
                       [xqt[:, 4096:5120], xqt[:, 5120:6144]]]
                M4 = [xqt[:, 2048:3072], xqt[:, 3072:4096]]
                ptb0f = ptb[0][:, :].bitcast(F32)
                def pbc(g_, pr):
                    if g_ % 2 == 0:
                        return pbank(pr)[:, 0:512]
                    return pbank(5)[:, 0:512] if pr == 0 else ptb0f
                bds4 = [sqs[0][:, 0:512], sqs[1][:, 0:512]]
                t1s = tpool[0][:, 0:256]
                t2a = tpool[0][:, 256:512]
                t2s = tpool[1][:, 0:256]
                htmp = tpool[1][:, 256:512]
                NG = 16
                BC = {}
                YD = {}
                SD = {}

                def emit_bcast(t, j):
                    for pr in range(2):
                        h = j * 4 + pr * 2
                        row, ones1 = st_row(h)
                        ins = t.matmul(pbc(j, pr), ones1,
                                       row[:, (h % 16) * 256:(h % 16) * 256 + 512],
                                       start=True, stop=True)
                    ins.then_inc(sp.h, 1)

                for g in range(NG):
                    H0 = g * 4
                    d8 = d8p[g % 2]
                    npg = 4 + (2 if g == 0 else 0) - (1 if g >= NG - 2 else 0)
                    p_g = sp.reserve(npg)
                    s_g = ss.reserve(10)
                    v_g = sv.reserve(12)
                    pv = p_g
                    if g == 0:
                        BC[0] = pv + 1
                        BC[1] = pv + 2
                        BC[2] = pv + 3
                        pv += 3
                    elif g + 2 < NG:
                        BC[g + 2] = pv + 1
                        pv += 1
                    YD[g] = pv + 1
                    SD[g] = pv + 3
                    with _SameBlk(MBLK) as blk:
                        @blk.tensor
                        def _(t):
                            if g == 0:
                                t.wait_ge(ss.h, s_g)
                                t.wait_ge(sv.h, v_g)
                                emit_bcast(t, 0)
                                emit_bcast(t, 1)
                                t.wait_ge(ss.h, s_g + 9)
                                emit_bcast(t, 2)
                            elif g + 2 < NG:
                                t.wait_ge(ss.h, s_g + 9)
                                emit_bcast(t, g + 2)
                            t.wait_ge(sv.h, v_g + 2)
                            for hh in range(4):
                                for ch in range(2):
                                    for sh in range(2):
                                        ins = t.matmul(
                                            pbank(2)[:, (ch * 4 + hh) * 64:(ch * 4 + hh + 1) * 64],
                                            M4[sh][:, hh * 256 + ch * 128:hh * 256 + (ch + 1) * 128],
                                            Xdt[sh][:, (H0 + hh) * 64:(H0 + hh + 1) * 64],
                                            start=(sh == 0), stop=(sh == 1))
                            for ch in range(2):
                                ins = t.matmul(
                                    pbank(3)[:, ch * 256:(ch + 1) * 256],
                                    C_fm[:, t0c + ch * 128:t0c + (ch + 1) * 128],
                                    Hbf[:, H0 * 64:(H0 + 4) * 64],
                                    start=True, stop=True)
                            ins.then_inc(sp.h, 1)
                            t.wait_ge(sv.h, v_g + 4)
                            for hh in range(4):
                                for sh in range(2):
                                    ins = t.matmul(
                                        pbank(4)[:, hh * 64:(hh + 1) * 64],
                                        bds4[sh][:, hh * 128:(hh + 1) * 128],
                                        Xdt[sh][:, (H0 + hh) * 64:(H0 + hh + 1) * 64],
                                        start=(sh == 0), stop=(sh == 1))
                                if hh == 1:
                                    ins.then_inc(sp.h, 1)
                            ins.then_inc(sp.h, 1)

                        @blk.scalar
                        def _(sc):
                            sc.wait_ge(sp.h, BC[g])
                            if g >= 2:
                                sc.wait_ge(sv.h, v_g - 22)
                            for sh in range(2):
                                for hh in range(4):
                                    c0 = sh * 64 + H0 + hh
                                    sc.activation(
                                        out=d8[sh][:, hh * 256:(hh + 1) * 256],
                                        in_=pbc(g, hh // 2)[:, (hh % 2) * 256:
                                                            (hh % 2) * 256 + 256],
                                        func=AF.Abs,
                                        bias=m2t[:, c0:c0 + 1]).then_inc(ss.h, 1)
                                sc.wait_ge(ss.h, s_g + 5 * sh + 4)
                                sc.activation(out=d8[sh][:, :], in_=d8[sh][:, :],
                                              func=AF.Exp,
                                              scale=-1.0).then_inc(ss.h, 1)

                        @blk.vector
                        def _(v):
                            v.wait_ge(sq.h, q_c2 + 2)
                            if g >= 1:
                                v.wait_ge(sp.h, YD[g - 1])
                            for sh in range(2):
                                v.wait_ge(ss.h, s_g + 5 * sh + 5)
                                v.tensor_tensor(
                                    out=M4[sh].rearrange("p (h m) -> p h m", m=256),
                                    in0=d8[sh].rearrange("p (h m) -> p h m", m=256),
                                    in1=gt_sb[:, sh * 256:(sh + 1) * 256]
                                        .rearrange("p (o m) -> p o m", o=1)
                                        .broadcast_to([P, 4, 256]),
                                    op=ALU.mult).then_inc(sv.h, 1)
                            if g >= 1:
                                v.wait_ge(sp.h, SD[g - 1])
                            for sh in range(2):
                                v.tensor_tensor(
                                    out=bds4[sh].rearrange("p (h m) -> p h m", m=128),
                                    in0=lnA[:, sh * 128:(sh + 1) * 128]
                                        .rearrange("p (o m) -> p o m", o=1)
                                        .broadcast_to([P, 4, 128]),
                                    in1=lnBf[:, sh * 64 + H0:sh * 64 + H0 + 4]
                                        .rearrange("p (h o) -> p h o", o=1)
                                        .broadcast_to([P, 4, 128]),
                                    op=ALU.mult).then_inc(sv.h, 1)
                            v.wait_ge(sp.h, YD[g])
                            if g >= 1:
                                v.wait_ge(sv.h, v_g - 2)
                            for ch in range(2):
                                if ch == 1:
                                    v.wait_ge(sv.h, v_g + 7)
                                v.tensor_tensor(
                                    out=t1s.rearrange("p (h m) -> p h m", m=64),
                                    in0=pbank(3)[:, ch * 256:(ch + 1) * 256]
                                        .rearrange("p (h m) -> p h m", m=64),
                                    in1=lnBf[:, 128 + 64 * ch + H0:128 + 64 * ch + H0 + 4]
                                        .rearrange("p (h o) -> p h o", o=1)
                                        .broadcast_to([P, 4, 64]),
                                    op=ALU.mult).then_inc(sv.h, 1)
                                v.tensor_tensor(out=t2s,
                                                in0=pbank(2)[:, ch * 256:(ch + 1) * 256],
                                                in1=x2[ch][:, H0 * 64:(H0 + 4) * 64],
                                                op=ALU.add).then_inc(sv.h, 1)
                                v.wait_ge(sv.h, v_g + 6 + 3 * ch)
                                if ch == 0:
                                    v.wait_ge(sp.h, SD[g])
                                v.tensor_tensor(out=y2[ch][:, H0 * 64:(H0 + 4) * 64],
                                                in0=t1s, in1=t2s,
                                                op=ALU.add).then_inc(sv.h, 1)
                            v.wait_ge(ss.h, s_c2 + 3)
                            v.tensor_tensor(
                                out=htmp.rearrange("p (h m) -> p h m", m=64),
                                in0=Hbf[:, H0 * 64:(H0 + 4) * 64]
                                    .rearrange("p (h m) -> p h m", m=64),
                                in1=aux[:, 64 + H0:64 + H0 + 4]
                                    .rearrange("p (h o) -> p h o", o=1)
                                    .broadcast_to([P, 4, 64]),
                                op=ALU.mult).then_inc(sv.h, 1)
                            v.wait_ge(sv.h, v_g + 11)
                            v.tensor_tensor(out=Hbf[:, H0 * 64:(H0 + 4) * 64],
                                            in0=htmp, in1=pbank(4)[:, 0:256],
                                            op=ALU.add).then_inc(sv.h, 1)

                P6_P = sp.n
                P6_S = ss.n
                P6_V = sv.n
                P6_Q = sq.n
                PREV_GP = P6_P
                PREV_GV = P6_V
                p_6 = sp.reserve(8)
                s_6 = ss.reserve(6)
                v_6 = sv.reserve(10)
                q_6 = sq.reserve(2)
                hm3 = [sha[0].reserve(8 * 16), sha[1].reserve(8 * 16)]
                p_m3 = sp.reserve(16)
                v_m3 = sv.reserve(16)
                with _SameBlk(MBLK) as blk:
                    @blk.gpsimd
                    def _(g):
                        g.wait_ge(sha[1].h, hx[1] + 16)
                        g.wait_ge(sv.h, P6_V)
                        g.wait_ge(sp.h, P6_P)
                        for ct in range(2):
                            g.tensor_tensor(out=y2[ct], in0=y2[ct], in1=z2[ct],
                                            op=ALU.mult).then_inc(sq.h, 1)

                    @blk.scalar
                    def _(sc):
                        sc.wait_ge(sp.h, P6_P)
                        sc.wait_ge(sv.h, P6_V)
                        for ct in range(2):
                            sc.wait_ge(sq.h, q_6 + ct + 1)
                            sc.activation(out=x2[ct], in_=y2[ct], func=AF.Square,
                                          accum_out=fstat[:, 4 + ct:5 + ct]).then_inc(ss.h, 1)
                        for ct in range(2):
                            sc.wait_ge(ss.h, s_6 + 2 + ct)
                            sc.activation(out=fstat[:, 6 + ct:7 + ct],
                                          in_=fstat[:, 4 + ct:5 + ct], func=AF.Ln,
                                          scale=1.0 / 4096.0,
                                          bias=cepsF[:, 0:1]).then_inc(ss.h, 1)
                        for ct in range(2):
                            sc.wait_ge(ss.h, s_6 + 4 + ct)
                            sc.activation(out=fstat[:, 8 + ct:9 + ct],
                                          in_=fstat[:, 6 + ct:7 + ct], func=AF.Exp,
                                          scale=-0.5).then_inc(ss.h, 1)

                    @blk.vector
                    def _(v):
                        for ct in range(2):
                            v.wait_ge(ss.h, s_6 + 5 + ct)
                            v.tensor_scalar(out=y2[ct], in0=y2[ct],
                                            scalar1=fstat[:, 8 + ct:9 + ct],
                                            scalar2=None,
                                            op0=ALU.mult).then_inc(sv.h, 1)
                        for bt in range(8):
                            v.wait_ge(sp.h, p_6 + bt + 1)
                            ct = bt // 4
                            fb0 = (bt % 4) * 8
                            v.tensor_copy(
                                out=xqt.rearrange("p (di m) -> p di m", m=256)
                                    [:, fb0:fb0 + 8, ct * 128:(ct + 1) * 128],
                                in_=ptb[bt % 2].rearrange("p (j m) -> p j m", m=128),
                            ).then_inc(sv.h, 1)

                    @blk.tensor
                    def _(t):
                        for bt in range(8):
                            ct = bt // 4
                            fb0 = (bt % 4) * 8
                            t.wait_ge(sv.h, v_6 + ct + 1)
                            if bt >= 2:
                                t.wait_ge(sv.h, v_6 + 2 + bt - 1)
                            elif bt == 0:
                                t.wait_ge(sv.h, P6_V)
                                t.wait_ge(ss.h, P6_S)
                            for j in range(8):
                                ins = t.transpose(
                                    ptb[bt % 2][:, j * 128:(j + 1) * 128],
                                    y2[ct][:, (fb0 + j) * 128:(fb0 + j + 1) * 128],
                                    ident[:, :])
                            ins.then_inc(sp.h, 1)
                        t.wait_ge(sv.h, v_6 + 10)
                        for dmc in range(16):
                            t.wait_ge(sha[dmc % 2].h, hm3[dmc % 2] + (dmc // 2 + 1) * 16)
                            if dmc >= 2:
                                t.wait_ge(sv.h, v_m3 + dmc - 1)
                            for di in range(32):
                                ins = t.matmul(pbank(2 + dmc % 2)[:, 0:256],
                                               opsl[dmc % 2][:, di * 128:(di + 1) * 128],
                                               ynfm[di][:, 0:256],
                                               start=(di == 0), stop=(di == 31))
                            ins.then_inc(sp.h, 1)

                    @blk.sync
                    def _(s):
                        s.wait_ge(sha[0].h, hm3[0])
                        s.wait_ge(sha[1].h, hm3[1])
                        for dmc in range(16):
                            if dmc >= 2:
                                s.wait_ge(sp.h, p_m3 + dmc - 1)
                            else:
                                s.wait_ge(sp.h, P6_P)
                            s.dma_start(out=opsl[dmc % 2][:, :],
                                        in_=blk_op2[dmc, :, :]).then_inc(sha[dmc % 2].h, 16)

                    @blk.vector
                    def _(v):
                        for dmc in range(16):
                            v.wait_ge(sp.h, p_m3 + dmc + 1)
                            v.tensor_tensor(out=res[dmc][:, t0c:t0c + CH],
                                            in0=pbank(2 + dmc % 2)[:, 0:256],
                                            in1=res[dmc][:, t0c:t0c + CH],
                                            op=ALU.add).then_inc(sv.h, 1)
            MBLK.__exit__(None, None, None)
        MB2 = nc.Block()
        MB2.__enter__()
        CURBLK[0] = MB2
        if "l0f" in blocks:
            wsl = [Hbuf[:, nk * D:(nk + 1) * D] for nk in range(DF)]
            Sx = dnslab[0][:, 2048:4096]
            outtm = dnslab[0][:, 0:2048]
            srow = dnslab[1][0:1, 0:4096].bitcast(F32)

            g_guard = sg.n
            v_guard0 = sv.n
            p_guard0 = sp.n
            d_p = sd.reserve(16)
            ga_p = [sga[0].reserve(8 * 16), sga[1].reserve(8 * 16)]
            p_p = sp.reserve(1)
            v_p = sv.reserve(4)
            q_p = sq.reserve(16)
            with _SameBlk(MB2) as blk:
                @blk.sync
                def _(s):
                    s.dma_start(out=srow,
                                in_=w["fw_s"].rearrange("(a n) -> a n", a=1)).then_inc(sd.h, 16)

                @blk.tensor
                def _(t):
                    t.wait_ge(sd.h, d_p + 16)
                    t.wait_ge(sv.h, v_guard0)
                    for b in range(4):
                        ins = t.matmul(pbank(b)[:, 0:512], ones_rowf[0:1, :],
                                       srow[0:1, b * 512:(b + 1) * 512],
                                       start=True, stop=True)
                    ins.then_inc(sp.h, 1)

                @blk.vector
                def _(v):
                    v.wait_ge(sp.h, p_p + 1)
                    for b in range(4):
                        v.tensor_copy(out=Sx[:, b * 512:(b + 1) * 512],
                                      in_=pbank(b)[:, 0:512]).then_inc(sv.h, 1)

                @blk.gpsimd
                def _(g):
                    g.wait_ge(sg.h, g_guard)
                    g.wait_ge(sp.h, p_guard0)
                    for nk in range(DF):
                        if nk >= 2:
                            g.wait_ge(sga[nk % 2].h, ga_p[nk % 2] + (nk // 2) * 16)
                        g.dma_start(out=wsl[nk],
                                    in_=w["fw"][nk * P:(nk + 1) * P, :]).then_inc(sga[nk % 2].h, 16)
                    g.wait_ge(sv.h, v_p + 4)
                    for nk in range(DF):
                        g.wait_ge(sga[nk % 2].h, ga_p[nk % 2] + (nk // 2 + 1) * 16)
                        g.tensor_tensor(out=wsl[nk], in0=wsl[nk], in1=Sx,
                                        op=ALU.mult).then_inc(sq.h, 1)

            sqdump = [sqs[0], sqs[1], tpool[0], tpool[1]]
            prev_v_end = 0
            for ts4 in range(4):
                tok0 = ts4 * 512
                ln_done = emit_ln(tok0, 512, [xq[f][:, 0:512] for f in range(DF)])
                for tcc in range(4):
                    c0 = tcc * 128
                    r0 = tok0 + c0
                    p_c = sp.reserve(5)
                    s_c = ss.reserve(7)
                    v_c = sv.reserve(20)
                    with _SameBlk(MB2) as blk:
                        @blk.tensor
                        def _(t):
                            t.wait_ge(sv.h, ln_done)
                            t.wait_ge(sq.h, q_p + 16)
                            if prev_v_end:
                                t.wait_ge(sv.h, prev_v_end - 4)
                            for b in range(4):
                                for nk in range(DF):
                                    ins = t.matmul(pbank(b)[:, 0:512],
                                                   xq[nk][:, c0:c0 + 128],
                                                   wsl[nk][:, b * 512:(b + 1) * 512],
                                                   start=(nk == 0), stop=(nk == DF - 1))
                            ins.then_inc(sp.h, 1)
                            for bt in range(4):
                                t.wait_ge(sv.h, v_c + 13 + bt)
                                if bt >= 2:
                                    t.wait_ge(sv.h, v_c + 16 + bt - 1)
                                last = None
                                for j in range(4):
                                    last = t.transpose(
                                        ptb[bt % 2][:, j * 128:(j + 1) * 128],
                                        outtm[:, (bt * 4 + j) * 128:(bt * 4 + j + 1) * 128],
                                        ident[:, :])
                                last.then_inc(sp.h, 1)

                        @blk.scalar
                        def _(sc):
                            sc.wait_ge(sp.h, p_c + 1)
                            for b in range(4):
                                sc.activation(out=sqdump[b][:, 0:512],
                                              in_=pbank(b)[:, 0:512], func=AF.Square,
                                              accum_out=fstat[:, b:b + 1]).then_inc(ss.h, 1)
                            sc.wait_ge(sv.h, v_c + 3)
                            sc.activation(out=fstat[:, 7:8], in_=fstat[:, 6:7],
                                          func=AF.Ln, bias=cepsP[:, 0:1]).then_inc(ss.h, 1)
                            sc.wait_ge(ss.h, s_c + 5)
                            sc.activation(out=fstat[:, 8:9], in_=fstat[:, 7:8],
                                          func=AF.Exp, scale=0.5).then_inc(ss.h, 1)
                            sc.wait_ge(sv.h, v_c + 4)
                            sc.activation(out=fstat[:, 11:12], in_=fstat[:, 9:10],
                                          func=AF.Tanh).then_inc(ss.h, 1)

                        @blk.vector
                        def _(v):
                            v.wait_ge(ss.h, s_c + 4)
                            v.tensor_tensor(out=fstat[:, 4:5], in0=fstat[:, 0:1],
                                            in1=fstat[:, 1:2], op=ALU.add).then_inc(sv.h, 1)
                            v.tensor_tensor(out=fstat[:, 5:6], in0=fstat[:, 2:3],
                                            in1=fstat[:, 3:4], op=ALU.add).then_inc(sv.h, 1)
                            v.wait_ge(sv.h, v_c + 2)
                            v.tensor_tensor(out=fstat[:, 6:7], in0=fstat[:, 4:5],
                                            in1=fstat[:, 5:6], op=ALU.add).then_inc(sv.h, 1)
                            v.wait_ge(ss.h, s_c + 6)
                            v.tensor_scalar(out=fstat[:, 9:10], in0=fstat[:, 8:9],
                                            scalar1=1e-7, scalar2=None,
                                            op0=ALU.max).then_inc(sv.h, 1)
                            v.wait_ge(sv.h, v_c + 4)
                            v.tensor_scalar(out=fstat[:, 10:11], in0=fstat[:, 9:10],
                                            scalar1=1e-7, scalar2=None,
                                            op0=ALU.add).then_inc(sv.h, 1)
                            v.wait_ge(sv.h, v_c + 5)
                            v.reciprocal(out=fstat[:, 12:13],
                                         in_=fstat[:, 10:11]).then_inc(sv.h, 1)
                            v.wait_ge(ss.h, s_c + 7)
                            v.wait_ge(sv.h, v_c + 6)
                            v.tensor_tensor(out=fstat[:, 13:14], in0=fstat[:, 11:12],
                                            in1=fstat[:, 12:13], op=ALU.mult).then_inc(sv.h, 1)
                            v.wait_ge(sv.h, v_c + 7)
                            v.tensor_tensor(out=fstat[:, 14:15], in0=fstat[:, 13:14],
                                            in1=fstat[:, 8:9], op=ALU.mult).then_inc(sv.h, 1)
                            v.wait_ge(sv.h, v_c + 8)
                            v.tensor_scalar(out=fstat[:, 15:16], in0=fstat[:, 14:15],
                                            scalar1=1e-7, scalar2=None,
                                            op0=ALU.max).then_inc(sv.h, 1)
                            v.wait_ge(sv.h, v_c + 9)
                            v.reciprocal(out=fstat[:, 16:17],
                                         in_=fstat[:, 15:16]).then_inc(sv.h, 1)
                            v.wait_ge(sv.h, v_c + 10)
                            v.tensor_scalar(out=fstat[:, 17:18], in0=fstat[:, 16:17],
                                            scalar1=0.99, scalar2=1.0, op0=ALU.mult,
                                            op1=ALU.min).then_inc(sv.h, 1)
                            v.wait_ge(sv.h, v_c + 11)
                            v.tensor_tensor(out=fstat[:, 18:19], in0=fstat[:, 13:14],
                                            in1=fstat[:, 17:18], op=ALU.mult).then_inc(sv.h, 1)
                            v.wait_ge(sv.h, v_c + 12)
                            for b in range(4):
                                v.tensor_scalar(out=outtm[:, b * 512:(b + 1) * 512],
                                                in0=pbank(b)[:, 0:512],
                                                scalar1=fstat[:, 18:19], scalar2=None,
                                                op0=ALU.mult).then_inc(sv.h, 1)
                            for bt in range(4):
                                v.wait_ge(sp.h, p_c + 2 + bt)
                                for j in range(4):
                                    f = bt * 4 + j
                                    ins = v.tensor_tensor(
                                        out=res[f][:, r0:r0 + 128],
                                        in0=ptb[bt % 2][:, j * 128:(j + 1) * 128],
                                        in1=res[f][:, r0:r0 + 128], op=ALU.add)
                                ins.then_inc(sv.h, 1)
                    prev_v_end = sv.n

        if "l1f" in blocks:
            P0b = sp.n
            S0b = ss.n
            V0b = sv.n
            Q0b = sq.n

            def emit_pcS(w_ap, pc_ap, nrow, width, stages, stage_w):
                ncol = width // stage_w
                nt = nrow * ncol
                G_loc = sg.n
                ga = [sga[0].reserve(((nt + 1) // 2) * 16),
                      sga[1].reserve((nt // 2) * 16)]
                ta_ = [sta[0].reserve(((nt + 1) // 2) * 16),
                       sta[1].reserve((nt // 2) * 16)]
                s_b = ss.reserve(nt)
                with _SameBlk(MB2) as blk:
                    @blk.gpsimd
                    def _(g):
                        g.wait_ge(sp.h, P0b)
                        g.wait_ge(ss.h, S0b)
                        g.wait_ge(sv.h, V0b)
                        g.wait_ge(sq.h, Q0b)
                        g.wait_ge(sg.h, G_loc)
                        for i in range(nt):
                            r, c = divmod(i, ncol)
                            if i >= 2:
                                g.wait_ge(sta[i % 2].h, ta_[i % 2] + (i // 2) * 16)
                            else:
                                g.wait_ge(sta[i % 2].h, ta_[i % 2])
                            g.dma_start(out=stages[i % 2][:, 0:stage_w],
                                        in_=w_ap[r * P:(r + 1) * P,
                                                 c * stage_w:(c + 1) * stage_w]
                                        ).then_inc(sga[i % 2].h, 16)

                    @blk.scalar
                    def _(sc):
                        for i in range(nt):
                            sc.wait_ge(sga[i % 2].h, ga[i % 2] + (i // 2 + 1) * 16)
                            sc.sign(out=stages[i % 2][:, 0:stage_w],
                                    in_=stages[i % 2][:, 0:stage_w]).then_inc(ss.h, 1)

                    @blk.sync
                    def _(s):
                        for i in range(nt):
                            r, c = divmod(i, ncol)
                            s.wait_ge(ss.h, s_b + i + 1)
                            s.dma_start(out=pc_ap[r * P:(r + 1) * P,
                                                  c * stage_w:(c + 1) * stage_w],
                                        in_=stages[i % 2][:, 0:stage_w]
                                        ).then_inc(sta[i % 2].h, 16)

            def emit_blkS(pc_ap, blk_ap, nk, rows, stages, srows):
                nh = rows // srows
                oc = srows // P
                trl = sd.reserve(nk * nh * 16)
                g_g = sg.n
                sct = sg.reserve(nk * nh * 16)
                with _SameBlk(MB2) as blk:
                    @blk.sync
                    def _(s):
                        s.wait_ge(sta[0].h, sta[0].n)
                        s.wait_ge(sta[1].h, sta[1].n)
                        s.wait_ge(sd.h, trl)
                        s.wait_ge(sg.h, g_g)
                        for i in range(nk * nh):
                            k, h = divmod(i, nh)
                            if i >= 1:
                                s.wait_ge(sg.h, sct + i * 16)
                            s.dma_start_transpose(stages[i % 2][:, 0:srows],
                                                  pc_ap[h * srows:(h + 1) * srows,
                                                        k * P:(k + 1) * P]
                                                  ).then_inc(sd.h, 16)

                    @blk.gpsimd
                    def _(g):
                        for i in range(nk * nh):
                            k, h = divmod(i, nh)
                            g.wait_ge(sd.h, trl + (i + 1) * 16)
                            g.dma_start(
                                out=blk_ap[h * oc:(h + 1) * oc, :,
                                           k * P:(k + 1) * P].rearrange("o p m -> p o m"),
                                in_=stages[i % 2][:, 0:srows]
                                    .rearrange("p (o m) -> p o m", o=oc),
                            ).then_inc(sg.h, 16)

            st_up2 = [dnslab[0][:, 4096:8192], dnslab[1][:, 4096:8192]]
            emit_pcS(w["up"], pc_up, 64, D, wslab, D)
            emit_blkS(pc_up, blk_up, 16, 8192, st_up2, 4096)
            emit_pcS(w["dn"], pc_dn, 16, 8192, st_up2, 4096)
            emit_blkS(pc_dn, blk_dn, 64, D, wslab, D)
            PREP_G = sg.n
            PREP_V = sv.n

        if "l1a" in blocks:
            NH, NKV, HD = 16, 4, 128
            xa = [Hbuf[:, f * 1152:(f + 1) * 1152] for f in range(DF)]
            kbufs = [Hbuf[:, 18432 + kv * 1152:18432 + (kv + 1) * 1152] for kv in range(NKV)]
            vbufs = [Hbuf[:, 23040 + kv * 1152:23040 + (kv + 1) * 1152] for kv in range(NKV)]
            vt_tile = lambda kv, j: Hbuf[:, 27648 + (kv * 10 + j) * 128:
                                         27648 + (kv * 10 + j + 1) * 128]
            ao = [Hbuf[:, f * 1024:(f + 1) * 1024] for f in range(DF)]
            qs = [(dnslab[0] if f < 8 else dnslab[1])[:, (f % 8) * 1024:(f % 8 + 1) * 1024]
                  for f in range(NH)]
            rt_a = lnA[0:64, 0:512]
            rt_b = tpool[0][0:64, 0:512]
            rt_c = lnB[0:64, 0:512]
            kint = dnslab[1][0:64, 0:4096].bitcast(I32)
            kflt = dnslab[1][0:64, 0:4096].bitcast(F32)

            v_ta = sv.n
            p_ta0 = sp.n
            d_ta = sd.reserve(16)
            q_ta = sq.reserve(10)
            s_ta = ss.reserve(3)
            v_tb = sv.reserve(12)
            p_ta = sp.reserve(1)
            TPI = float(2 * np.pi)
            with _SameBlk(MB2) as blk:
                @blk.gpsimd
                def _(g):
                    g.wait_ge(sq.h, CONST_Q)
                    g.wait_ge(sv.h, v_ta)
                    g.iota(fstat[0:64, 20:21], pattern=[[1, 1]], base=0,
                           channel_multiplier=1,
                           allow_small_or_imprecise_dtypes=True).then_inc(sq.h, 1)
                    g.iota(sinf[:, :], pattern=[[1, T]], base=0, channel_multiplier=0,
                           allow_small_or_imprecise_dtypes=True).then_inc(sq.h, 1)
                    g.memset(mask3[:, :], 0.0).then_inc(sq.h, 1)
                    g.wait_ge(sq.h, q_ta + 3)
                    g.affine_select(mask3[:, 0:256], mask3[:, 0:256], pattern=[[1, 256]],
                                    compare_op=ALU.is_ge, fill=-1e30, base=64,
                                    channel_multiplier=-1).then_inc(sq.h, 1)
                    g.wait_ge(sq.h, q_ta + 4)
                    g.affine_select(mask3[:, 0:256], mask3[:, 0:256], pattern=[[-1, 256]],
                                    compare_op=ALU.is_ge, fill=-1e30, base=64,
                                    channel_multiplier=1).then_inc(sq.h, 1)
                    g.wait_ge(sq.h, q_ta + 5)
                    g.affine_select(mask3[:, 256:512], mask3[:, 256:512], pattern=[[1, 256]],
                                    compare_op=ALU.is_ge, fill=-1e30, base=0,
                                    channel_multiplier=-1).then_inc(sq.h, 1)
                    g.wait_ge(sq.h, q_ta + 6)
                    g.affine_select(mask3[:, 256:512], mask3[:, 256:512], pattern=[[-1, 256]],
                                    compare_op=ALU.is_ge, fill=-1e30, base=128,
                                    channel_multiplier=1).then_inc(sq.h, 1)
                    g.affine_select(mask3[:, 512:768], mask3[:, 512:768], pattern=[[1, 256]],
                                    compare_op=ALU.is_ge, fill=-1e30, base=0,
                                    channel_multiplier=-1).then_inc(sq.h, 1)
                    g.wait_ge(sq.h, q_ta + 8)
                    g.affine_select(mask3[:, 512:768], mask3[:, 512:768], pattern=[[-1, 256]],
                                    compare_op=ALU.is_ge, fill=-1e30, base=128,
                                    channel_multiplier=1).then_inc(sq.h, 1)
                    g.wait_ge(sq.h, q_ta + 9)
                    g.affine_select(mask3[:, 512:768], mask3[:, 512:768], pattern=[[-1, 256]],
                                    compare_op=ALU.is_ge, fill=-1e30, base=191,
                                    channel_multiplier=0).then_inc(sq.h, 1)

                @blk.scalar
                def _(sc):
                    sc.wait_ge(sq.h, q_ta + 1)
                    sc.activation(out=fstat[0:64, 21:22], in_=fstat[0:64, 20:21],
                                  func=AF.Exp,
                                  scale=float(-np.log(10000.0) / 64.0)).then_inc(ss.h, 1)
                    sc.wait_ge(sv.h, v_tb + 5)
                    sc.activation(out=sinf[:, :], in_=sinf[:, :],
                                  func=AF.Sin).then_inc(ss.h, 1)
                    sc.wait_ge(sv.h, v_tb + 10)
                    sc.activation(out=cosf[:, :], in_=cosf[:, :],
                                  func=AF.Sin).then_inc(ss.h, 1)

                @blk.vector
                def _(v):
                    v.wait_ge(ss.h, s_ta + 1)
                    v.wait_ge(sq.h, q_ta + 2)
                    v.wait_ge(sp.h, p_ta0)
                    v.tensor_scalar(out=cosf[:, :], in0=sinf[:, :],
                                    scalar1=fstat[0:64, 21:22], scalar2=None,
                                    op0=ALU.mult).then_inc(sv.h, 1)
                    v.wait_ge(sv.h, v_tb + 1)
                    v.tensor_scalar(out=kint[:, :], in0=cosf[:, :], scalar1=1.0 / TPI,
                                    scalar2=None, op0=ALU.mult).then_inc(sv.h, 1)
                    v.wait_ge(sv.h, v_tb + 2)
                    v.scalar_tensor_tensor(out=sinf[:, :], in0=kint[:, :], scalar=-TPI,
                                           in1=cosf[:, :], op0=ALU.mult,
                                           op1=ALU.add).then_inc(sv.h, 1)
                    v.wait_ge(sv.h, v_tb + 3)
                    v.tensor_scalar(out=kflt[:, :], in0=sinf[:, :], scalar1=float(np.pi),
                                    scalar2=None, op0=ALU.is_ge).then_inc(sv.h, 1)
                    v.wait_ge(sv.h, v_tb + 4)
                    v.scalar_tensor_tensor(out=sinf[:, :], in0=kflt[:, :], scalar=-TPI,
                                           in1=sinf[:, :], op0=ALU.mult,
                                           op1=ALU.add).then_inc(sv.h, 1)
                    v.wait_ge(ss.h, s_ta + 2)
                    v.tensor_scalar(out=cosf[:, :], in0=cosf[:, :], scalar1=float(np.pi / 2),
                                    scalar2=None, op0=ALU.add).then_inc(sv.h, 1)
                    v.wait_ge(sv.h, v_tb + 6)
                    v.tensor_scalar(out=kint[:, :], in0=cosf[:, :], scalar1=1.0 / TPI,
                                    scalar2=None, op0=ALU.mult).then_inc(sv.h, 1)
                    v.wait_ge(sv.h, v_tb + 7)
                    v.scalar_tensor_tensor(out=cosf[:, :], in0=kint[:, :], scalar=-TPI,
                                           in1=cosf[:, :], op0=ALU.mult,
                                           op1=ALU.add).then_inc(sv.h, 1)
                    v.wait_ge(sv.h, v_tb + 8)
                    v.tensor_scalar(out=kflt[:, :], in0=cosf[:, :], scalar1=float(np.pi),
                                    scalar2=None, op0=ALU.is_ge).then_inc(sv.h, 1)
                    v.wait_ge(sv.h, v_tb + 9)
                    v.scalar_tensor_tensor(out=cosf[:, :], in0=kflt[:, :], scalar=-TPI,
                                           in1=cosf[:, :], op0=ALU.mult,
                                           op1=ALU.add).then_inc(sv.h, 1)
                    v.wait_ge(ss.h, s_ta + 3)
                    v.tensor_copy(out=cosf2[:, :], in_=cosf[:, :]).then_inc(sv.h, 1)
                    v.tensor_copy(out=sinf2[:, :], in_=sinf[:, :]).then_inc(sv.h, 1)

                @blk.sync
                def _(s):
                    s.wait_ge(sd.h, d_ta)
                    s.dma_start(out=sk_row[:, :],
                                in_=w["sink"].rearrange("a h b c -> a (h b c)")).then_inc(sd.h, 16)

                @blk.tensor
                def _(t):
                    t.wait_ge(sd.h, d_ta + 16)
                    t.wait_ge(sq.h, CONST_Q)
                    t.wait_ge(sv.h, v_ta)
                    t.matmul(pbank(5)[:, 0:16], ones_rowf[0:1, :], sk_row[0:1, :],
                             start=True, stop=True).then_inc(sp.h, 1)
            v_tc = sv.reserve(1)
            with _SameBlk(MB2) as blk:
                @blk.vector
                def _(v):
                    v.wait_ge(sp.h, p_ta + 1)
                    v.tensor_copy(out=sinkb[:, :], in_=pbank(5)[:, 0:16]).then_inc(sv.h, 1)
            ATT_TABLES_V = sv.n
            ATT_TABLES_S = ss.n

            def emit_blk_direct(w_ap, blk_ap, ncols, noc, permute):
                p_gd = sp.n
                v_gd = sv.n
                q_gd = sq.n
                g_gd = sg.n
                ga = [sga[0].reserve(8 * 16), sga[1].reserve(8 * 16)]
                v_bd = sv.reserve(16)
                sct = sg.reserve(16 * 16)
                pslab = dnslab[0][:, 0:D]
                with _SameBlk(MB2) as blk:
                    @blk.gpsimd
                    def _(g):
                        g.wait_ge(sp.h, p_gd)
                        g.wait_ge(sq.h, q_gd)
                        g.wait_ge(sv.h, v_gd)
                        g.wait_ge(sg.h, g_gd)
                        for kc in range(16):
                            if kc >= 2:
                                if permute:
                                    g.wait_ge(sv.h, v_bd + kc - 1)
                                else:
                                    g.wait_ge(sg.h, sct + (kc - 1) * 16)
                            g.dma_start(out=wslab[kc % 2][:, 0:ncols],
                                        in_=w_ap[kc * P:(kc + 1) * P, :]).then_inc(sga[kc % 2].h, 16)
                            g.wait_ge(sv.h, v_bd + kc + 1)
                            if kc >= 1:
                                g.wait_ge(sg.h, sct + kc * 16)
                            src = pslab[:, 0:ncols] if permute else wslab[kc % 2][:, 0:ncols]
                            g.dma_start(
                                out=blk_ap[:, :, kc * P:(kc + 1) * P].rearrange("o p m -> p o m"),
                                in_=src.rearrange("p (o m) -> p o m", o=noc),
                            ).then_inc(sg.h, 16)

                    @blk.vector
                    def _(v):
                        for kc in range(16):
                            v.wait_ge(sga[kc % 2].h, ga[kc % 2] + (kc // 2 + 1) * 16)
                            if permute:
                                if kc >= 1:
                                    v.wait_ge(sg.h, sct + kc * 16)
                                for oc in range(noc):
                                    v.tensor_copy(
                                        out=pslab[:, oc * 128:oc * 128 + 64],
                                        in_=wslab[kc % 2][:, oc * 128:(oc + 1) * 128:2])
                                    ins = v.tensor_copy(
                                        out=pslab[:, oc * 128 + 64:(oc + 1) * 128],
                                        in_=wslab[kc % 2][:, oc * 128 + 1:(oc + 1) * 128:2])
                                ins.then_inc(sv.h, 1)
                            else:
                                v.nop().then_inc(sv.h, 1)

            emit_blk_direct(w["wq"], blk_q, D, 16, True)
            emit_blk_direct(w["wk"], blk_k, 512, 4, True)
            emit_blk_direct(w["wv"], blk_v, 512, 4, False)
            emit_blk_direct(w["wo"], blk_o, D, 16, False)

            for hh2 in range(2):
                t0 = hh2 * 1024
                lo = max(0, t0 - 64)
                hi = min(T, t0 + 1024 + 64)
                kw = hi - lo
                ln_parts = []
                off = 0
                while off < kw:
                    Wp = min(512, kw - off)
                    ln_done = emit_ln(lo + off, Wp,
                                      [xa[f][:, off:off + Wp] for f in range(DF)])
                    off += Wp
                XA_V = ln_done

                p_gd = sp.n
                s_gd = ss.n
                g_gd2 = sg.n
                hq = [sha[0].reserve(8 * 16), sha[1].reserve(8 * 16)]
                p_q = sp.reserve(32)
                v_q = sv.reserve(32)
                hkv = [sha[0].reserve(2 * 2 * 16), sha[1].reserve(2 * 2 * 16)]
                p_kv = sp.reserve(NKV * 2 * 3)
                v_kv = sv.reserve(NKV * 2 * 3)
                with _SameBlk(MB2) as blk:
                    @blk.sync
                    def _(s):
                        s.wait_ge(sha[0].h, hq[0])
                        s.wait_ge(sha[1].h, hq[1])
                        for qf in range(NH):
                            if qf >= 2:
                                s.wait_ge(sp.h, p_q + (qf - 1) * 2)
                            else:
                                s.wait_ge(sp.h, p_gd)
                                s.wait_ge(sg.h, g_gd2)
                            s.dma_start(out=wslab[qf % 2][:, :],
                                        in_=blk_q[qf, :, :]).then_inc(sha[qf % 2].h, 16)
                        for kv in range(NKV):
                            for which in range(2):
                                i = kv * 2 + which
                                if i >= 2:
                                    s.wait_ge(sp.h, p_kv + (i - 1) * 3)
                                else:
                                    s.wait_ge(sp.h, p_q + 32)
                                src = blk_k if which == 0 else blk_v
                                s.dma_start(out=wslab[i % 2][:, :],
                                            in_=src[kv, :, :]).then_inc(sha[i % 2].h, 16)

                    @blk.tensor
                    def _(t):
                        t.wait_ge(sv.h, XA_V)
                        for qf in range(NH):
                            t.wait_ge(sha[qf % 2].h, hq[qf % 2] + (qf // 2 + 1) * 16)
                            for tb in range(2):
                                i2 = qf * 2 + tb
                                if i2 >= 2:
                                    t.wait_ge(sv.h, v_q + i2 - 1)
                                for k in range(DF):
                                    ins = t.matmul(
                                        pbank(i2 % 2)[:, 0:512],
                                        wslab[qf % 2][:, k * P:(k + 1) * P],
                                        xa[k][:, (t0 - lo) + tb * 512:(t0 - lo) + (tb + 1) * 512],
                                        start=(k == 0), stop=(k == DF - 1))
                                ins.then_inc(sp.h, 1)
                        tbs = [(0, 512), (512, 512), (1024, kw - 1024)]
                        for kv in range(NKV):
                            for which in range(2):
                                i = kv * 2 + which
                                t.wait_ge(sha[i % 2].h, hkv[i % 2] + (i // 2 + 1) * 16)
                                for ti, (toff, tw) in enumerate(tbs):
                                    i3 = i * 3 + ti
                                    if i3 >= 2:
                                        t.wait_ge(sv.h, v_kv + i3 - 1)
                                    else:
                                        t.wait_ge(sv.h, v_q + 31 + i3)
                                    for k in range(DF):
                                        ins = t.matmul(
                                            pbank(i3 % 2)[:, 0:tw],
                                            wslab[i % 2][:, k * P:(k + 1) * P],
                                            xa[k][:, toff:toff + tw],
                                            start=(k == 0), stop=(k == DF - 1))
                                    ins.then_inc(sp.h, 1)

                    @blk.vector
                    def _(v):
                        for qf in range(NH):
                            for tb in range(2):
                                i2 = qf * 2 + tb
                                v.wait_ge(sp.h, p_q + i2 + 1)
                                v.tensor_copy(out=qs[qf][:, tb * 512:(tb + 1) * 512],
                                              in_=pbank(i2 % 2)[:, 0:512]).then_inc(sv.h, 1)
                        tbs = [(0, 512), (512, 512), (1024, kw - 1024)]
                        for kv in range(NKV):
                            for which in range(2):
                                i = kv * 2 + which
                                dst = kbufs[kv] if which == 0 else vbufs[kv]
                                for ti, (toff, tw) in enumerate(tbs):
                                    i3 = i * 3 + ti
                                    v.wait_ge(sp.h, p_kv + i3 + 1)
                                    v.tensor_copy(out=dst[:, toff:toff + tw],
                                                  in_=pbank(i3 % 2)[:, 0:tw]).then_inc(sv.h, 1)

                QKV_V = sv.n
                n_rp = NH * 2 + NKV * 3
                v_r = sv.reserve(n_rp * 6)
                with _SameBlk(MB2) as blk:
                    @blk.vector
                    def _(v):
                        v.wait_ge(sv.h, QKV_V)
                        v.wait_ge(ss.h, ATT_TABLES_S)
                        idx = 0
                        pieces = []
                        for f in range(NH):
                            for pc in range(2):
                                pieces.append((qs[f], pc * 512, 512, t0 + pc * 512))
                        for kv in range(NKV):
                            for pc, (toff, tw) in enumerate([(0, 512), (512, 512), (1024, kw - 1024)]):
                                pieces.append((kbufs[kv], toff, tw, lo + toff))
                        for (buf, coff, cw, gt) in pieces:
                            b0 = v_r + idx * 6
                            v.wait_ge(sv.h, b0)
                            qr = buf[0:64, coff:coff + cw]
                            qi = buf[64:128, coff:coff + cw]
                            cs = cosf[0:64, gt:gt + cw]
                            sn = sinf[0:64, gt:gt + cw]
                            cs2 = cosf2[:, gt:gt + cw]
                            sn2 = sinf2[:, gt:gt + cw]
                            ta = rt_a[:, 0:cw]
                            tb_ = rt_b[:, 0:cw]
                            tc_ = rt_c[:, 0:cw]
                            v.tensor_tensor(out=ta, in0=qr, in1=cs, op=ALU.mult).then_inc(sv.h, 1)
                            v.tensor_tensor(out=tb_, in0=qr, in1=sn, op=ALU.mult).then_inc(sv.h, 1)
                            v.wait_ge(sv.h, b0 + 2)
                            v.tensor_tensor(out=tc_, in0=qi, in1=sn2, op=ALU.mult).then_inc(sv.h, 1)
                            v.wait_ge(sv.h, b0 + 3)
                            v.tensor_tensor(out=qr, in0=ta, in1=tc_, op=ALU.subtract).then_inc(sv.h, 1)
                            v.wait_ge(sv.h, b0 + 4)
                            v.tensor_tensor(out=ta, in0=qi, in1=cs2, op=ALU.mult).then_inc(sv.h, 1)
                            v.wait_ge(sv.h, b0 + 5)
                            v.tensor_tensor(out=qi, in0=tb_, in1=ta, op=ALU.add).then_inc(sv.h, 1)
                            idx += 1
                ROPE_V = sv.n

                if hh2 == 0:
                    grids = [(0, [0, 128]), (1, [64 + 128 * j for j in range(8)])]
                else:
                    grids = [(0, [128 * j for j in range(9)])]
                vt_index = {}
                n_vt = sum(len(offs) for _, offs in grids) * NKV
                p_vt = sp.reserve((n_vt + 3) // 4)
                v_vt = sv.reserve((n_vt + 3) // 4)
                q_vt = sq.reserve(1)
                with _SameBlk(MB2) as blk:
                    @blk.gpsimd
                    def _(g):
                        g.wait_ge(sv.h, QKV_V)
                        for kv in range(NKV):
                            g.memset(vbufs[kv][:, kw:1152], 0.0)
                            ins = g.memset(kbufs[kv][:, kw:1152], 0.0)
                        ins.then_inc(sq.h, 1)

                    @blk.tensor
                    def _(t):
                        t.wait_ge(sv.h, ROPE_V)
                        t.wait_ge(sq.h, q_vt + 1)
                        tiles = []
                        for kv in range(NKV):
                            slot = 0
                            for gi, offs in grids:
                                for off in offs:
                                    vt_index[(kv, gi, off)] = slot
                                    tiles.append((kv, slot, off))
                                    slot += 1
                        for bi in range(0, len(tiles), 4):
                            batch = tiles[bi:bi + 4]
                            bb = bi // 4
                            if bb >= 2:
                                t.wait_ge(sv.h, v_vt + bb - 1)
                            for j, (kv, slot, off) in enumerate(batch):
                                ins = t.transpose(ptb[bb % 2][:, j * 128:(j + 1) * 128],
                                                  vbufs[kv][:, off:off + 128], ident[:, :])
                            ins.then_inc(sp.h, 1)

                    @blk.vector
                    def _(v):
                        tiles = []
                        for kv in range(NKV):
                            slot = 0
                            for gi, offs in grids:
                                for off in offs:
                                    tiles.append((kv, slot, off))
                                    slot += 1
                        for bi in range(0, len(tiles), 4):
                            bb = bi // 4
                            v.wait_ge(sp.h, p_vt + bb + 1)
                            for j, (kv, slot, off) in enumerate(batch := tiles[bi:bi + 4]):
                                ins = v.tensor_copy(out=vt_tile(kv, slot),
                                                    in_=ptb[bb % 2][:, j * 128:(j + 1) * 128])
                            ins.then_inc(sv.h, 1)
                VT_V = sv.n

                scale = float(HD ** -0.5)
                for qc in range(8):
                    gqc = hh2 * 8 + qc
                    if hh2 == 0:
                        if qc == 0:
                            koff, mwid, moff, gi = 0, 256, 0, 0
                        else:
                            koff, mwid, moff, gi = qc * 128 - 64, 256, 256, 1
                    else:
                        koff, gi = qc * 128, 0
                        if qc == 7:
                            mwid, moff = 256, 512
                        else:
                            mwid, moff = 256, 256
                    for g4 in range(4):
                        p_s = sp.reserve(1)
                        s_s = ss.reserve(4)
                        v_s = sv.reserve(6)
                        p_av = sp.reserve(NH // 4 * 2)
                        v_av = sv.reserve(NH // 4 * 2)
                        pslc = pf[:, (g4 % 2) * 1024:(g4 % 2) * 1024 + 1024]
                        with _SameBlk(MB2) as blk:
                            @blk.tensor
                            def _(t):
                                t.wait_ge(sv.h, VT_V)
                                t.wait_ge(ss.h, s_s)
                                for hh in range(4):
                                    ins = t.matmul(pslc[:, hh * 256:hh * 256 + 256],
                                                   qs[g4 * 4 + hh][:, qc * 128:(qc + 1) * 128],
                                                   kbufs[g4][:, koff:koff + 256],
                                                   start=True, stop=True)
                                ins.then_inc(sp.h, 1)
                                for hh in range(4):
                                    hgl = g4 * 4 + hh
                                    t.wait_ge(sv.h, v_s + 5 + (hh // 2))
                                    if hh >= 2:
                                        t.wait_ge(sv.h, v_av + 2 * hh - 3)
                                    last = None
                                    for jj in range(2):
                                        last = t.transpose(
                                            ptb[hh % 2][:, jj * 128:(jj + 1) * 128],
                                            tpool[hh // 2][:, (hh % 2) * 256 + jj * 128:
                                                           (hh % 2) * 256 + (jj + 1) * 128],
                                            ident[:, :])
                                    last.then_inc(sp.h, 1)
                                    t.wait_ge(sv.h, v_av + 2 * hh + 1)
                                    for jj in range(2):
                                        off_j = koff + jj * 128
                                        slot = vt_index[(g4, gi, off_j)]
                                        ins = t.matmul(pbank(4)[:, 0:128],
                                                       vt_tile(g4, slot),
                                                       (lnA if hh % 2 == 0 else lnB)[:, jj * 128:(jj + 1) * 128],
                                                       start=(jj == 0), stop=(jj == 1))
                                    ins.then_inc(sp.h, 1)

                            @blk.vector
                            def _(v):
                                v.wait_ge(sp.h, p_s + 1)
                                v.tensor_tensor(
                                    out=pslc.rearrange("p (h n) -> p h n", h=4),
                                    in0=pslc.rearrange("p (h n) -> p h n", h=4),
                                    in1=mask3[:, moff:moff + mwid]
                                        .rearrange("p (a n) -> p a n", a=1)
                                        .broadcast_to([P, 4, 256]),
                                    op=ALU.add).then_inc(sv.h, 1)
                                v.wait_ge(sv.h, v_s + 1)
                                v.tensor_reduce(out=fstat[:, 16:20].rearrange("p (h a) -> p h a", a=1),
                                                in_=pslc.rearrange("p (h n) -> p h n", h=4),
                                                axis=mybir.AxisListType.X, op=ALU.max,
                                                negate=True).then_inc(sv.h, 1)
                                v.wait_ge(sv.h, v_s + 2)
                                v.tensor_scalar(out=fstat[:, 16:20], in0=fstat[:, 16:20],
                                                scalar1=scale, scalar2=None,
                                                op0=ALU.mult).then_inc(sv.h, 1)
                                v.wait_ge(ss.h, s_s + 4)
                                v.reciprocal(out=fstat[:, 20:24],
                                             in_=fstat[:, 12:16]).then_inc(sv.h, 1)
                                v.wait_ge(sv.h, v_s + 4)
                                for hh in range(4):
                                    ins = v.tensor_scalar(
                                        out=tpool[hh // 2][:, (hh % 2) * 256:(hh % 2 + 1) * 256],
                                        in0=sqs[hh // 2][:, (hh % 2) * 256:(hh % 2 + 1) * 256],
                                        scalar1=fstat[:, 20 + hh:21 + hh], scalar2=None,
                                        op0=ALU.mult)
                                    if hh % 2 == 1:
                                        ins.then_inc(sv.h, 1)
                                for hh in range(4):
                                    v.wait_ge(sp.h, p_av + 2 * hh + 1)
                                    v.tensor_copy(out=(lnA if hh % 2 == 0 else lnB)[:, 0:256],
                                                  in_=ptb[hh % 2][:, 0:256]).then_inc(sv.h, 1)
                                    v.wait_ge(sp.h, p_av + 2 * hh + 2)
                                    v.tensor_copy(out=ao[g4 * 4 + hh][:, qc * 128:(qc + 1) * 128],
                                                  in_=pbank(4)[:, 0:128]).then_inc(sv.h, 1)

                            @blk.scalar
                            def _(sc):
                                sc.wait_ge(sv.h, v_s + 3)
                                for hh in range(4):
                                    sc.activation(out=sqs[hh // 2][:, (hh % 2) * 256:(hh % 2 + 1) * 256],
                                                  in_=pslc[:, hh * 256:hh * 256 + 256],
                                                  func=AF.Exp, scale=scale,
                                                  bias=fstat[:, 16 + hh:17 + hh],
                                                  accum_out=fstat[:, 12 + hh:13 + hh]).then_inc(ss.h, 1)

                AO_V = sv.n
                ho = [sha[0].reserve(8 * 16), sha[1].reserve(8 * 16)]
                p_o = sp.reserve(32)
                v_o = sv.reserve(32)
                with _SameBlk(MB2) as blk:
                    @blk.sync
                    def _(s):
                        for dmc in range(16):
                            if dmc >= 2:
                                s.wait_ge(sp.h, p_o + (dmc - 1) * 2)
                            else:
                                s.wait_ge(sp.h, p_o)
                            s.dma_start(out=wslab[dmc % 2][:, :],
                                        in_=blk_o[dmc, :, :]).then_inc(sha[dmc % 2].h, 16)

                    @blk.tensor
                    def _(t):
                        t.wait_ge(sv.h, AO_V)
                        for dmc in range(16):
                            t.wait_ge(sha[dmc % 2].h, ho[dmc % 2] + (dmc // 2 + 1) * 16)
                            for tb in range(2):
                                i2 = dmc * 2 + tb
                                if i2 >= 2:
                                    t.wait_ge(sv.h, v_o + i2 - 1)
                                for k in range(DF):
                                    ins = t.matmul(pbank(4 + i2 % 2)[:, 0:512],
                                                   wslab[dmc % 2][:, k * P:(k + 1) * P],
                                                   ao[k][:, tb * 512:(tb + 1) * 512],
                                                   start=(k == 0), stop=(k == DF - 1))
                                ins.then_inc(sp.h, 1)

                    @blk.vector
                    def _(v):
                        for dmc in range(16):
                            for tb in range(2):
                                i2 = dmc * 2 + tb
                                v.wait_ge(sp.h, p_o + i2 + 1)
                                v.tensor_tensor(
                                    out=res[dmc][:, t0 + tb * 512:t0 + (tb + 1) * 512],
                                    in0=pbank(4 + i2 % 2)[:, 0:512],
                                    in1=res[dmc][:, t0 + tb * 512:t0 + (tb + 1) * 512],
                                    op=ALU.add).then_inc(sv.h, 1)

        if "l1f" in blocks:
            d_w5 = sd.reserve(3 * 16)
            p_w5 = sp.reserve(1)
            v_w5 = sv.reserve(3)
            with _SameBlk(MB2) as blk:
                @blk.sync
                def _(s):
                    s.wait_ge(sd.h, d_w5)
                    s.dma_start(out=usr[:, :],
                                in_=w["us"].rearrange("(r c) -> r c", r=64)).then_inc(sd.h, 16)
                    s.dma_start(out=dsr[:, :],
                                in_=w["ds"].rearrange("(r c) -> r c", r=16)).then_inc(sd.h, 16)
                    s.dma_start(out=osr[:, :],
                                in_=w["os"].rearrange("(a b) -> a b", a=1)).then_inc(sd.h, 16)

                @blk.tensor
                def _(t):
                    t.wait_ge(sd.h, d_w5 + 48)
                    t.wait_ge(sq.h, CONST_Q)
                    t.transpose(pbank(2)[:, 0:64], usr[:, :], identf[0:64, 0:64])
                    t.transpose(pbank(3)[:, 0:16], dsr[:, :], identf[0:16, 0:16])
                    t.matmul(pbank(3)[:, 16:17], ones_rowf[0:1, :], osr[0:1, 0:1],
                             start=True, stop=True).then_inc(sp.h, 1)

                @blk.vector
                def _(v):
                    v.wait_ge(sp.h, p_w5 + 1)
                    v.tensor_copy(out=usf[:, :], in_=pbank(2)[:, 0:64]).then_inc(sv.h, 1)
                    v.tensor_copy(out=osb[:, :], in_=pbank(3)[:, 16:17]).then_inc(sv.h, 1)
                    v.wait_ge(sv.h, v_w5 + 2)
                    v.tensor_scalar(out=dsosf[:, :], in0=pbank(3)[:, 0:16],
                                    scalar1=osb[:, 0:1], scalar2=None,
                                    op0=ALU.mult).then_inc(sv.h, 1)


            for qq in range(4):
                tok0 = qq * 512
                ln_done = emit_ln(tok0, 512, [xq[f][:, 0:512] for f in range(DF)])

                p_guard = sp.n
                s_guard = ss.n
                du = [sha[0].reserve(32 * 16), sha[1].reserve(32 * 16)]
                p_up = sp.reserve(64)
                s_up = ss.reserve(64)
                with _SameBlk(MB2) as blk:
                    @blk.sync
                    def _(s):
                        s.wait_ge(sha[0].h, du[0])
                        s.wait_ge(sha[1].h, du[1])
                        for hc in range(64):
                            if hc >= 2:
                                s.wait_ge(sp.h, p_up + hc - 1)
                            else:
                                s.wait_ge(sp.h, p_guard)
                            s.dma_start(out=wslab[hc % 2][:, :],
                                        in_=blk_up[hc, :, :]).then_inc(sha[hc % 2].h, 16)

                    @blk.tensor
                    def _(t):
                        t.wait_ge(sv.h, ln_done)
                        for hc in range(64):
                            t.wait_ge(sha[hc % 2].h, du[hc % 2] + (hc // 2 + 1) * 16)
                            if hc >= 2:
                                t.wait_ge(ss.h, s_up + hc - 1)
                            else:
                                t.wait_ge(ss.h, s_guard)
                            for k in range(DF):
                                ins = t.matmul(pbank(hc % 2)[:, 0:512],
                                               wslab[hc % 2][:, k * P:(k + 1) * P],
                                               xq[k][:, 0:512],
                                               start=(k == 0), stop=(k == DF - 1))
                            ins.then_inc(sp.h, 1)

                    @blk.scalar
                    def _(sc):
                        if qq == 0:
                            sc.wait_ge(sg.h, PREP_G)
                        for hc in range(64):
                            sc.wait_ge(sp.h, p_up + hc + 1)
                            sc.activation(out=Hbuf[:, hc * 512:(hc + 1) * 512],
                                          in_=pbank(hc % 2)[:, 0:512], func=AF.Relu,
                                          scale=usf[:, hc:hc + 1]).then_inc(ss.h, 1)

                dd = [sha[0].reserve(8 * 16), sha[1].reserve(8 * 16)]
                p_dn = sp.reserve(16)
                v_dn = sv.reserve(16)
                with _SameBlk(MB2) as blk:
                    @blk.sync
                    def _(s):
                        s.wait_ge(sha[0].h, dd[0])
                        s.wait_ge(sha[1].h, dd[1])
                        for dmc in range(16):
                            if dmc >= 2:
                                s.wait_ge(sp.h, p_dn + dmc - 1)
                            s.dma_start(out=dnslab[dmc % 2][:, :],
                                        in_=blk_dn[dmc, :, :]).then_inc(sha[dmc % 2].h, 16)

                    @blk.tensor
                    def _(t):
                        t.wait_ge(ss.h, s_up + 64)
                        for dmc in range(16):
                            t.wait_ge(sha[dmc % 2].h, dd[dmc % 2] + (dmc // 2 + 1) * 16)
                            if dmc >= 2:
                                t.wait_ge(sv.h, v_dn + dmc - 1)
                            for hc in range(64):
                                ins = t.matmul(pbank(2 + dmc % 2)[:, 0:512],
                                               dnslab[dmc % 2][:, hc * P:(hc + 1) * P],
                                               Hbuf[:, hc * 512:(hc + 1) * 512],
                                               start=(hc == 0), stop=(hc == 63))
                            ins.then_inc(sp.h, 1)

                    @blk.vector
                    def _(v):
                        v.wait_ge(sv.h, v_w5 + 3)
                        for dmc in range(16):
                            v.wait_ge(sp.h, p_dn + dmc + 1)
                            v.scalar_tensor_tensor(
                                out=res[dmc][:, tok0:tok0 + 512],
                                in0=pbank(2 + dmc % 2)[:, 0:512],
                                scalar=dsosf[:, dmc:dmc + 1],
                                in1=res[dmc][:, tok0:tok0 + 512],
                                op0=ALU.mult, op1=ALU.add).then_inc(sv.h, 1)

        MB2.__exit__(None, None, None)
        CURBLK[0] = None
        pe0 = sp.reserve(16 * 4)
        ve0 = sv.reserve(16 * 4)
        st0 = [sgb[0].reserve(8 * 16), sgb[1].reserve(8 * 16)]
        with nc.Block() as blk:
            @blk.tensor
            def _(t):
                for tc in range(16):
                    for b in range(4):
                        i = tc * 4 + b
                        if i >= 2:
                            t.wait_ge(sv.h, ve0 + i - 1)
                        elif i == 0:
                            t.wait_ge(sv.h, ve0)
                        half = i % 2
                        last = None
                        for j in range(4):
                            f = b * 4 + j
                            last = t.transpose(
                                ptb[half][:, j * 128:(j + 1) * 128],
                                res[f][:, tc * P:(tc + 1) * P], ident[:, :])
                        last.then_inc(sp.h, 1)

            @blk.vector
            def _(v):
                for tc in range(16):
                    if tc >= 1:
                        pv = (tc - 1) % 2
                        v.wait_ge(sgb[pv].h, st0[pv] + ((tc - 1) // 2 + 1) * 16)
                    for b in range(4):
                        i = tc * 4 + b
                        v.wait_ge(sp.h, pe0 + i + 1)
                        half = i % 2
                        for j in range(4):
                            f = b * 4 + j
                            ins = v.tensor_copy(
                                out=ytm[tc % 2][:, f * P:(f + 1) * P],
                                in_=ptb[half][:, j * 128:(j + 1) * 128])
                        ins.then_inc(sv.h, 1)

            @blk.gpsimd
            def _(g):
                for tc in range(16):
                    g.wait_ge(sv.h, ve0 + tc * 4 + 4)
                    g.dma_start(out=y_h[tc * P:(tc + 1) * P, :],
                                in_=ytm[tc % 2][:, :]).then_inc(sgb[tc % 2].h, 16)
                g.wait_ge(sgb[0].h, st0[0] + 128)
                g.wait_ge(sgb[1].h, st0[1] + 128)

    return nc




_NC_CACHE = {}


def _get_nc(blocks):
    key = tuple(blocks)
    if key not in _NC_CACHE:
        _NC_CACHE[key] = build(blocks)
    return _NC_CACHE[key]


WKEYS = {
    "l1f": ["L1_ffn_up_weight", "L1_ffn_down_weight", "L1_ffn_up_scale",
            "L1_ffn_down_scale", "L1_ffn_output_scale"],
    "l0f": ["L0_ffn_weight", "L0_ffn_scale"],
    "l1a": ["L1_attn_wq", "L1_attn_wk", "L1_attn_wv", "L1_attn_wo", "L1_attn_sink"],
    "l0a": ["L0_attn_in_proj_w", "L0_attn_conv_w", "L0_attn_conv_b",
            "L0_attn_dt_bias", "L0_attn_A_log", "L0_attn_D",
            "L0_attn_norm_weight", "L0_attn_out_proj_w"],
}


def make_in_map(inputs, core, blocks=ALL_BLOCKS):
    m = {"x": np.ascontiguousarray(np.asarray(inputs["L0_attn_u"][core], dtype=np.float32))}
    for b in blocks:
        for k in WKEYS.get(b, []):
            m[k] = np.ascontiguousarray(np.asarray(inputs[k], dtype=np.float32))
    return m


def kernel(**inputs):
    nc = _get_nc(ALL_BLOCKS)
    in_maps = [make_in_map(inputs, c, ALL_BLOCKS) for c in range(8)]
    res = run_bass_kernel_spmd(nc, in_maps, core_ids=list(range(8)))
    out = np.stack([r["y"] for r in res.results], axis=0)
    return out.astype(np.float32)
